# version 27
# baseline (speedup 1.0000x reference)
import numpy as np
from contextlib import ExitStack

import concourse.bass as bass
import concourse.mybir as mybir
from concourse.bass_utils import run_bass_kernel_spmd

F32 = mybir.dt.float32
BF16 = mybir.dt.bfloat16
F16 = mybir.dt.float16
AF = mybir.ActivationFunctionType
ALU = mybir.AluOpType
AX = mybir.AxisListType

N_CORES = 8
B_TOT, T, D = 16, 2048, 1024
NB = B_TOT // N_CORES
C = 128
NCH = T // C
G = D // 128
HID = 2048
HG = HID // 128
II = 2816
IG = II // 128
EPS_LN = 1e-5
EPS_RMS = 1e-6
RBIAS = 1536.0
NCHUNKS = NB * NCH

DMA_SEM = {"sp": "dS", "se": "dA", "pl": "dP"}


class Prog:

    def __init__(self, ctx):
        self.ctx = ctx
        self.steps = []
        self.labels = {}
        self.counts = dict(ctx.semvals)
        self.counts0 = dict(ctx.semvals)
        self.dma_sems = {}
        self.marks = {}
        self.cur_group = None

    def _snap(self, label):
        assert label not in self.labels, f"dup label {label}"
        self.labels[label] = dict(self.counts)

    def _chk(self, waits):
        for (s, lab) in waits:
            assert lab in self.labels, f"wait on unknown/future label {lab}"

    def step(self, eng, fn, waits=(), inc=(), label=None, dur=250, dma=False):
        self._chk(waits)
        self.steps.append(dict(eng=eng, fn=fn, waits=tuple(waits),
                               incs=tuple(inc), label=label, dur=dur, dma=dma,
                               group=self.cur_group))
        for (s, k) in inc:
            self.counts[s] = self.counts.get(s, self.ctx.semvals.get(s, 0)) + k
        if label:
            self._snap(label)

    def dma(self, queue, out, in_, sem=None, waits=(), label=None, transpose=False,
            slow=False, xfer=2500):
        assert sem is not None, "dma needs an explicit stream sem"
        self.ctx.get_sem(sem)
        self.dma_sems[sem] = queue

        def fn(e, out=out, in_=in_, tr=transpose, slow=slow):
            if tr:
                return e.dma_start(out=out, in_=in_, transpose=True)
            if slow:
                return e.dma_start(out=out, in_=in_, allow_slow_non_contiguous=True)
            return e.dma_start(out=out, in_=in_)
        self.step(queue, fn, waits=waits, inc=[(sem, 16)], label=label,
                  dur=150, dma=xfer)

    def quiesce(self, tag):
        self.mark(f"_qz_{tag}")
        per_q = {}
        for sem, q in self.dma_sems.items():
            per_q.setdefault(q, []).append(sem)
        for q, sems in per_q.items():
            self.step(q, lambda e: None,
                      waits=[(s, f"_qz_{tag}") for s in sems])
        self.dma_sems = {}

    _chain_ctr = [0]

    def chain(self, eng, fns, waits=(), inc=(), label=None):
        n = len(fns)
        prev = None
        for j, fn in enumerate(fns):
            w = list(waits) if j == 0 else [("sCH", prev)]
            if j == n - 1:
                self.step(eng, fn, waits=w, inc=inc, label=label)
            else:
                Prog._chain_ctr[0] += 1
                lb = f"_ch{Prog._chain_ctr[0]}"
                self.step(eng, fn, waits=w, inc=[("sCH", 1)], label=lb)
                prev = lb

    def mark(self, label):
        self._snap(label)
        self.marks[label] = len(self.steps)

    def at(self, sem, label):
        return self.labels[label][sem]

    def _schedule(self):
        n = len(self.steps)
        prod = {}
        for j, st in enumerate(self.steps):
            if st["label"] is not None:
                prod[st["label"]] = j
        preds = [[] for _ in range(n)]
        succs = [[] for _ in range(n)]

        def edge(a, b, lat):
            preds[b].append((a, lat))
            succs[a].append(b)

        inc_idx = {}
        for j, st in enumerate(self.steps):
            for (s, _) in st["incs"]:
                inc_idx.setdefault(s, []).append(j)
        for j, st in enumerate(self.steps):
            for (s, lab) in st["waits"]:
                if lab in prod:
                    a = prod[lab]
                    pa = self.steps[a]
                    assert any(ss == s for (ss, _) in pa["incs"]), \
                        f"reorder: wait ({s},{lab}) not produced at label"
                    lat = pa["dma"] if pa["dma"] else \
                        (50 if pa["eng"] == st["eng"] else 400)
                    edge(a, j, lat)
                else:
                    mp = self.marks[lab]
                    for a in inc_idx.get(s, []):
                        if a < mp:
                            pa = self.steps[a]
                            edge(a, j, pa["dma"] if pa["dma"] else 400)
        last = {}
        for j, st in enumerate(self.steps):
            key = (st["group"], st["eng"])
            if key in last:
                edge(last[key], j, 0)
            last[key] = j
        lastd = {}
        for j, st in enumerate(self.steps):
            if st["dma"]:
                sm = st["incs"][0][0]
                if sm in lastd:
                    edge(lastd[sm], j, 0)
                lastd[sm] = j
        prio = [0.0] * n
        for j in range(n - 1, -1, -1):
            m = 0.0
            for b in succs[j]:
                if prio[b] > m:
                    m = prio[b]
            prio[j] = m + self.steps[j]["dur"]
        indeg = [len(preds[j]) for j in range(n)]
        est = [0.0] * n
        avail = [j for j in range(n) if indeg[j] == 0]
        fin = [0.0] * n
        R = {}
        order = []
        for _ in range(n):
            best = None
            bstart = None
            for j in avail:
                stt = max(R.get(self.steps[j]["eng"], 0.0), est[j])
                if bstart is None or stt < bstart:
                    bstart = stt
            for j in avail:
                stt = max(R.get(self.steps[j]["eng"], 0.0), est[j])
                if stt <= bstart + 200.0:
                    if best is None or prio[j] > prio[best]:
                        best = j
            j = best
            avail.remove(j)
            stt = max(R.get(self.steps[j]["eng"], 0.0), est[j])
            fin[j] = stt + self.steps[j]["dur"]
            R[self.steps[j]["eng"]] = fin[j]
            order.append(j)
            done = set()
            for b in succs[j]:
                if b in done:
                    continue
                done.add(b)
                ndec = 0
                for (a, lat) in preds[b]:
                    if a == j:
                        if fin[j] + lat > est[b]:
                            est[b] = fin[j] + lat
                        ndec += 1
                indeg[b] -= ndec
                if indeg[b] == 0:
                    avail.append(b)
        assert len(order) == n, "scheduler failed to order all steps"
        return order

    def emit(self, reorder=False):
        ctx = self.ctx
        steps = self.steps
        labels = self.labels
        if reorder:
            order = self._schedule()
            steps = [self.steps[j] for j in order]
            counts = dict(self.counts0)
            labels = {}
            for st in steps:
                for (s, k) in st["incs"]:
                    counts[s] = counts.get(s, 0) + k
                if st["label"]:
                    labels[st["label"]] = dict(counts)
            for lab, mp in self.marks.items():
                labels[lab] = dict(self.counts0) if mp == 0 else dict(counts)
        with ctx.nc.Block() as blk:
            def make(eng_name):
                def body(e):
                    for st in steps:
                        if st["eng"] != eng_name:
                            continue
                        for (s, lab) in st["waits"]:
                            e.wait_ge(ctx.sems[s], labels[lab][s])
                        r = st["fn"](e)
                        if st["incs"]:
                            last = r[-1] if isinstance(r, tuple) else r
                            assert last is not None, "step with incs emitted no op"
                            for (s, k) in st["incs"]:
                                last.then_inc(ctx.sems[s], k)
                return body
            blk.sync(make("sp"))
            blk.tensor(make("te"))
            blk.vector(make("ve"))
            blk.scalar(make("se"))
            blk.gpsimd(make("pl"))
        for s in self.counts:
            ctx.semvals[s] = self.counts[s]



def evac(p, par, out, in_, waits=(), label=None, scale=None):
    if par % 2 == 0:
        if scale is None:
            fn = lambda e: e.activation(out, in_, AF.Copy)
        else:
            fn = lambda e: e.activation(out, in_, AF.Copy, scale=scale)
        p.step("se", fn, waits=waits, inc=[("sEV", 1)], label=label)
        return "sEV"
    if scale is None:
        fn = lambda e: e.tensor_copy(out, in_)
    else:
        fn = lambda e: e.tensor_scalar_mul(out, in_, scale)
    p.step("ve", fn, waits=waits, inc=[("sVX", 1)], label=label)
    return "sVX"


class Ctx:
    pass


def declare_io(nc, debug):
    i = {}
    i["x"] = nc.dram_tensor("L0_attn_x", [NB, T, D], F32, kind="ExternalInput").ap()
    i["n1w"] = nc.dram_tensor("L0_attn_norm1_w", [D], F32, kind="ExternalInput").ap()
    i["wq"] = nc.dram_tensor("L0_attn_Wq", [D, D], F32, kind="ExternalInput").ap()
    i["wk"] = nc.dram_tensor("L0_attn_Wk", [D, D], F32, kind="ExternalInput").ap()
    i["wv"] = nc.dram_tensor("L0_attn_Wv", [D, D], F32, kind="ExternalInput").ap()
    i["bw"] = nc.dram_tensor("L0_attn_beta_w", [D, 1], F32, kind="ExternalInput").ap()
    i["bb"] = nc.dram_tensor("L0_attn_beta_b", [1], F32, kind="ExternalInput").ap()
    i["wo"] = nc.dram_tensor("L0_attn_Wo", [D, D], F32, kind="ExternalInput").ap()
    i["al"] = nc.dram_tensor("L0_attn_alpha_logit", [D], F32, kind="ExternalInput").ap()
    i["n2w"] = nc.dram_tensor("L0_attn_norm2_w", [D], F32, kind="ExternalInput").ap()
    i["awg"] = nc.dram_tensor("L0_attn_w_gate", [D, HID], F32, kind="ExternalInput").ap()
    i["awu"] = nc.dram_tensor("L0_attn_w_up", [D, HID], F32, kind="ExternalInput").ap()
    i["awd"] = nc.dram_tensor("L0_attn_w_down", [HID, D], F32, kind="ExternalInput").ap()
    i["cx"] = nc.dram_tensor("L0_ffn_clip_x", [1], F32, kind="ExternalInput").ap()
    i["cd"] = nc.dram_tensor("L0_ffn_clip_down", [1], F32, kind="ExternalInput").ap()
    i["fwg"] = nc.dram_tensor("L0_ffn_w_gate", [II, D], F32, kind="ExternalInput").ap()
    i["fwu"] = nc.dram_tensor("L0_ffn_w_up", [II, D], F32, kind="ExternalInput").ap()
    i["fwd"] = nc.dram_tensor("L0_ffn_w_down", [D, II], F32, kind="ExternalInput").ap()
    i["y"] = nc.dram_tensor("y", [NB, T, D], F32, kind="ExternalOutput").ap()
    dk = "ExternalOutput" if debug else "Internal"
    i["q_s"] = nc.dram_tensor("q_s", [NB, NCH, C, D], BF16, kind=dk).ap()
    i["k_s"] = nc.dram_tensor("k_s", [NB, NCH, C, D], BF16, kind=dk).ap()
    i["v_s"] = nc.dram_tensor("v_s", [NB, NCH, C, D], BF16, kind=dk).ap()
    i["o_s"] = nc.dram_tensor("o_s", [NB, NCH, C, D], BF16, kind=dk).ap()
    i["ax_s"] = nc.dram_tensor("ax_s", [NB, T, D], F32, kind=dk).ap()
    i["x1_s"] = nc.dram_tensor("x1_s", [NB, T, D], F32, kind=dk).ap()
    i["t_s"] = nc.dram_tensor("t_s", [NCHUNKS // 4, 128, IG, 4 * C], BF16,
                              kind=dk).ap()
    i["wgq_s"] = nc.dram_tensor("wgq_s", [II, D], BF16, kind=dk).ap()
    i["wuq_s"] = nc.dram_tensor("wuq_s", [II, D], BF16, kind=dk).ap()
    i["wdq_s"] = nc.dram_tensor("wdq_s", [D, II], BF16, kind=dk).ap()
    if debug:
        i["st_dbg"] = nc.dram_tensor("st_dbg", [128, 10, NCHUNKS], F32,
                                     kind="ExternalOutput").ap()
        i["hq_dbg"] = nc.dram_tensor("hq_dbg", [NB, NCH, C, II], BF16,
                                     kind="ExternalOutput").ap()
        i["xq_dbg"] = nc.dram_tensor("xq_dbg", [NB, NCH, C, D], BF16,
                                     kind="ExternalOutput").ap()
        i["col_dbg"] = nc.dram_tensor("col_dbg", [128, 4], F32,
                                      kind="ExternalOutput").ap()
    i["cc1_in"] = nc.dram_tensor("cc1_in", [1, 128], F32)
    i["cc1_out"] = nc.dram_tensor("cc1_out", [1, 128], F32)
    i["cc2_in"] = nc.dram_tensor("cc2_in", [1, 128], F32)
    i["cc2_out"] = nc.dram_tensor("cc2_out", [1, 128], F32)
    return i


def build(debug=False, phases=("p1", "w", "p2a", "p2bi", "p2bii", "p3")):
    nc = bass.Bass("TRN2", target_bir_lowering=False)
    ctx = Ctx()
    ctx.nc = nc
    ctx.debug = debug
    ctx.io = declare_io(nc, debug)

    with ExitStack() as stack:
        ctx.stack = stack
        ctx.sb = lambda n, s, d: stack.enter_context(nc.sbuf_tensor(n, s, d))
        ctx.sems = {}
        ctx.semvals = {}

        def get_sem(name):
            if name not in ctx.sems:
                ctx.sems[name] = stack.enter_context(nc.semaphore(name))
                ctx.semvals.setdefault(name, 0)
            return ctx.sems[name]
        ctx.get_sem = get_sem
        for s in ("sMM", "sEV", "sVX", "sCC", "sCH"):
            get_sem(s)

        sb = ctx.sb
        ctx.ident = sb("ident", [128, 128], F32)
        ctx.ident_bf = sb("ident_bf", [128, 128], BF16)
        ctx.tril_s = sb("tril_s", [128, 128], F32)
        ctx.triu_i = sb("triu_i", [128, 128], F32)
        ctx.ones_row = sb("ones_row", [1, 128], F32)
        ctx.r1_st = sb("r1_st", [128, NCHUNKS], F32)
        ctx.b1_st = sb("b1_st", [128, NCHUNKS], F32)
        ctx.beta_st = sb("beta_st", [128, NCHUNKS], F32)
        ctx.betan_st = sb("betan_st", [128, NCHUNKS], F32)
        ctx.r2_st = sb("r2_st", [128, NCHUNKS], F32)
        ctx.mu2_st = sb("mu2_st", [128, NCHUNKS], F32)
        ctx.var2_st = sb("var2_st", [128, NCHUNKS], F32)
        ctx.mx_st = sb("mx_st", [128, NCHUNKS], F32)
        ctx.mn_st = sb("mn_st", [128, NCHUNKS], F32)
        ctx.alpha_sb = sb("alpha_sb", [128, G], F32)
        ctx.a_sb = sb("a_sb", [128, G], F32)
        ctx.n1w_sb = sb("n1w_sb", [128, G], F32)
        ctx.n2w_sb = sb("n2w_sb", [128, G], F32)
        ctx.bbn_col = sb("bbn_col", [128, 1], F32)
        ctx.sig_cell = sb("sig_cell", [1, 8], F32)
        ctx.qs_st = sb("qs_st", [128, NCHUNKS], F32)
        ctx.qb_st = sb("qb_st", [128, NCHUNKS], F32)
        ctx.s1_col = sb("s1_col", [128, 1], F32)
        ctx.is1_col = sb("is1_col", [128, 1], F32)
        ctx.ccb = sb("ccb", [1, 128], F32)
        ctx.s2_col = sb("s2_col", [128, 1], F32)
        ctx.is2_col = sb("is2_col", [128, 1], F32)

        emit_init(ctx)
        if "p1" in phases:
            emit_p1(ctx)
        if "w" in phases:
            emit_wquant(ctx)
        if "p2a" in phases:
            emit_p2a(ctx)
        if "p2bii" in phases:
            emit_p2b(ctx)
        elif "p2bi" in phases:
            emit_p2bi(ctx)
        if "p3" in phases:
            emit_p3(ctx)
        if debug:
            emit_debug_dump(ctx)
        emit_final(ctx)

    return nc


CONST_VALS = (EPS_LN, EPS_LN * EPS_RMS, EPS_RMS, RBIAS)


def emit_init(ctx):
    nc, io = ctx.nc, ctx.io
    ctx.const_tiles = ctx.sb("const_tiles", [128, len(CONST_VALS)], F32)
    for j, v in enumerate(CONST_VALS):
        nc.const_aps.register(F32, v, ctx.const_tiles[:, j:j + 1])
    with ExitStack() as st:
        ri = st.enter_context(nc.sbuf_tensor("ri", [128, 128], F32))
        ci = st.enter_context(nc.sbuf_tensor("ci", [128, 128], F32))
        pcol = st.enter_context(nc.psum_tensor("pcol", [128, 8], F32))
        p = Prog(ctx)
        p.mark("start")
        def setconsts(e):
            r = None
            for j, v in enumerate(CONST_VALS):
                r = e.memset(ctx.const_tiles[:, j:j + 1], v)
            return r
        p.step("pl", setconsts, inc=[("sVX", 1)], label="i_const")
        p.dma("sp", ctx.n1w_sb[:, :], io["n1w"].rearrange("(g p) -> p g", p=128), sem="ild", slow=True)
        p.dma("sp", ctx.n2w_sb[:, :], io["n2w"].rearrange("(g p) -> p g", p=128), sem="ild", slow=True)
        p.dma("sp", ctx.alpha_sb[:, :], io["al"].rearrange("(g p) -> p g", p=128), sem="ild", slow=True)
        p.dma("sp", ctx.sig_cell[0:1, 0:1], io["cx"].unsqueeze(0), sem="ild")
        p.dma("sp", ctx.sig_cell[0:1, 1:2], io["cd"].unsqueeze(0), sem="ild")
        p.dma("sp", ctx.sig_cell[0:1, 4:5], io["bb"].unsqueeze(0), sem="ild", label="i_ld")
        p.step("pl", lambda e: (
            e.iota(ri[:, :], [[0, 128]], channel_multiplier=1,
                   allow_small_or_imprecise_dtypes=True),
            e.iota(ci[:, :], [[1, 128]], channel_multiplier=0,
                   allow_small_or_imprecise_dtypes=True)),
            inc=[("sVX", 1)], label="i_iota")
        p.chain("ve", [
            lambda e: (
                e.tensor_tensor(ctx.tril_s[:, :], ri[:, :], ci[:, :], ALU.is_gt),
                e.tensor_tensor(ctx.triu_i[:, :], ri[:, :], ci[:, :], ALU.is_le),
                e.memset(ctx.ones_row[:, :], 1.0),
                e.tensor_tensor(ctx.ident[:, :], ri[:, :], ci[:, :], ALU.is_equal)),
            lambda e: e.tensor_copy(ctx.ident_bf[:, :], ctx.ident[:, :])],
            waits=[("sVX", "i_iota")], inc=[("sVX", 1)], label="i_masks")
        p.step("se", lambda e: (
            e.activation(ctx.a_sb[:, :], ctx.alpha_sb[:, :], AF.Exp, scale=-1.0),
            e.activation(ctx.sig_cell[0:1, 0:2], ctx.sig_cell[0:1, 0:2],
                         AF.Exp, scale=-1.0)),
            waits=[("ild", "i_ld")], inc=[("sEV", 1)], label="i_exp")
        p.step("ve", lambda e: (
            e.tensor_scalar_add(ctx.a_sb[:, :], ctx.a_sb[:, :], 1.0),
            e.tensor_scalar_add(ctx.sig_cell[0:1, 0:2], ctx.sig_cell[0:1, 0:2], 1.0)),
            waits=[("sEV", "i_exp")], inc=[("sVX", 1)], label="i_add1")
        p.step("ve", lambda e: (
            e.reciprocal(ctx.alpha_sb[:, :], ctx.a_sb[:, :]),
            e.reciprocal(ctx.sig_cell[0:1, 2:4], ctx.sig_cell[0:1, 0:2])),
            waits=[("sVX", "i_add1")], inc=[("sVX", 1)], label="i_recip")
        p.step("ve", lambda e: (
            e.tensor_copy(ctx.sig_cell[0:1, 0:2], ctx.sig_cell[0:1, 2:4]),
            e.tensor_scalar(ctx.a_sb[:, :], ctx.alpha_sb[:, :], -1.0, 1.0,
                            ALU.mult, ALU.add)),
            waits=[("sVX", "i_recip")], inc=[("sVX", 1)], label="i_sig")
        p.step("te", lambda e: e.matmul(pcol[:, 0:1], ctx.ones_row[:, :],
                                        ctx.sig_cell[0:1, 4:5], start=True, stop=True),
               waits=[("sVX", "i_masks"), ("ild", "i_ld")], inc=[("sMM", 1)],
               label="i_bbmm")
        p.step("ve", lambda e: e.tensor_scalar_mul(ctx.bbn_col[:, :], pcol[:, 0:1], -1.0),
               waits=[("sMM", "i_bbmm")], inc=[("sVX", 1)], label="i_bbn")
        p.quiesce("init")
        p.emit()


def emit_p1(ctx):
    nc, io = ctx.nc, ctx.io
    with ExitStack() as st:
        lsb = lambda n, s, d: st.enter_context(nc.sbuf_tensor(n, s, d))
        lps = lambda n, s: st.enter_context(nc.psum_tensor(n, s, F32))
        wq_sb = lsb("wq_sb", [128, G, D], BF16)
        wk_sb = lsb("wk_sb", [128, G, D], BF16)
        wv_sb = lsb("wv_sb", [128, G, D], BF16)
        bw_sb = lsb("bw_sb", [128, G, 1], BF16)

        wstage = lsb("wstage", [128, G, D], BF16)
        bstage = lsb("bstage", [128, G], F32)

        x0t = lsb("x0t", [128, 4, D], F32)
        h_bf2 = lsb("h_bf", [128, 2, D], BF16)
        kk_scr = lsb("kk_scr", [128, D], BF16)
        h_dm2 = lsb("h_dm", [128, 2, G, C], BF16)
        q_tm2 = lsb("q_tm", [128, 2, D], BF16)
        k_tm2 = lsb("k_tm", [128, 2, D], BF16)
        v_tm2 = lsb("v_tm", [128, 2, D], BF16)
        vb_c2 = lsb("vb_c", [128, 2, G, C], F32)
        ema2 = lsb("ema2", [128, 2, G, C], F32)
        vst2 = lsb("vst2", [128, 2, D], BF16)
        stat2 = lsb("stat", [128, 2, 10], F32)
        bcell = lsb("bcell", [128, 1], F32)
        ptp = [lps(f"p1tp{j}", [128, 128]) for j in range(4)]
        ptpb = [t[:, :].bitcast(BF16)[:, 0:128] for t in ptp]
        pA = lps("p1A", [128, D])
        pk = lps("p1k", [128, D])

        p = Prog(ctx)
        p.mark("p1s")
        p.cur_group = ("WL", 0)
        for wi, (srcw, dst) in enumerate([(io["wq"], wq_sb), (io["wk"], wk_sb),
                                          (io["wv"], wv_sb)]):
            w = [("sEV", f"wf{wi-1}")] if wi else []
            p.dma("pl", wstage[:, :, :], srcw.rearrange("(g p) d -> p g d", p=128),
                  sem="wl", waits=w, label=f"wl{wi}", xfer=9000)

            def fold(e, dst=dst):
                r = None
                for g in range(G):
                    r = e.activation(dst[:, g, :], wstage[:, g, :], AF.Copy,
                                     scale=ctx.n1w_sb[:, g:g + 1])
                return r
            p.step("se", fold, waits=[("wl", f"wl{wi}")], inc=[("sEV", 1)],
                   label=f"wf{wi}", dur=2200)
        p.dma("sp", bstage[:, :], io["bw"].rearrange("(g p) o -> p (g o)", p=128),
              sem="bwl", label="bwl", slow=True)
        p.chain("ve", [
            lambda e: e.tensor_tensor(bstage[:, :], bstage[:, :],
                                      ctx.n1w_sb[:, :], ALU.mult),
            lambda e: e.tensor_copy(bw_sb[:, :, 0], bstage[:, :])],
            waits=[("bwl", "bwl"), ("sEV", "wf2")], inc=[("sVX", 1)], label="bwf")
        tp_user = [None] * 4

        def tpw(bank):
            return [tp_user[bank]] if tp_user[bank] else []

        for b in range(NB):
            for c in range(NCH):
                i = b * NCH + c
                s2 = i % 2
                p.cur_group = ("C", i)
                h_bf = h_bf2[:, s2, :]
                h_dm = h_dm2[:, s2, :, :]
                q_tm = q_tm2[:, s2, :]
                k_tm = k_tm2[:, s2, :]
                v_tm = v_tm2[:, s2, :]
                stat = stat2[:, s2, :]
                w = [("sVX", f"h{i-4}")] if i >= 4 else []
                p.dma("sp", x0t[:, i % 4, :], io["x"][b, c * C:(c + 1) * C, :],
                      sem=f"x0{i % 4}", waits=w, label=f"x0{i}", xfer=4000)
                suw = [(f"x0{i % 4}", f"x0{i}")]
                if i >= 2:
                    suw.append(("sVX", f"mv{i-2}"))
                p.step("ve", lambda e, i=i, stat=stat: e.reduce_sum(
                    stat[:, 0:1], x0t[:, i % 4, :], AX.X),
                    waits=suw, inc=[("sVX", 1)], label=f"sum{i}", dur=450)
                sqw = [(f"x0{i % 4}", f"x0{i}")]
                if i >= 2:
                    sqw.append(("sVX", f"mv{i-2}"))
                p.step("se", lambda e, i=i, stat=stat: e.activation(
                    kk_scr[:, :], x0t[:, i % 4, :], AF.Square,
                    accum_out=stat[:, 1:2]),
                    waits=sqw, inc=[("sEV", 1)], label=f"sq{i}", dur=1150)
                mvw = [("sVX", f"sum{i}"), ("sEV", f"sq{i}")]
                if i >= 2:
                    mvw.append(("sVX", f"bc{i-2}"))
                p.chain("ve", [
                    lambda e, stat=stat: e.tensor_scalar_mul(stat[:, 2:3],
                                                             stat[:, 0:1], 1.0 / D),
                    lambda e, stat=stat: e.scalar_tensor_tensor(
                        stat[:, 3:4], stat[:, 2:3], -1.0, stat[:, 2:3],
                        ALU.mult, ALU.mult),
                    lambda e, stat=stat: e.scalar_tensor_tensor(
                        stat[:, 3:4], stat[:, 1:2], 1.0 / D, stat[:, 3:4],
                        ALU.mult, ALU.add)],
                    waits=mvw, inc=[("sVX", 1)], label=f"mv{i}")
                rsw = [("sVX", f"mv{i}")]
                if i >= 2:
                    rsw.append(("sVX", f"h{i-2}"))
                p.chain("se", [
                    lambda e, stat=stat: (
                        e.activation(stat[:, 4:5], stat[:, 3:4], AF.Ln, bias=EPS_LN),
                        e.activation(stat[:, 5:6], stat[:, 3:4], AF.Ln,
                                     scale=1.0 + EPS_RMS, bias=EPS_LN * EPS_RMS)),
                    lambda e, stat=stat: (
                        e.activation(stat[:, 4:5], stat[:, 4:5], AF.Exp, scale=-0.5),
                        e.activation(stat[:, 5:6], stat[:, 5:6], AF.Exp, scale=-0.5))],
                    waits=rsw, inc=[("sEV", 1)], label=f"rs{i}")
                p.step("ve", lambda e, i=i, stat=stat: (
                    e.tensor_copy(ctx.r1_st[:, i:i + 1], stat[:, 4:5]),
                    e.scalar_tensor_tensor(ctx.b1_st[:, i:i + 1], stat[:, 2:3],
                                           -1.0, stat[:, 4:5], ALU.mult, ALU.mult),
                    e.scalar_tensor_tensor(stat[:, 6:7], stat[:, 2:3],
                                           -1.0, stat[:, 5:6], ALU.mult, ALU.mult)),
                    waits=[("sEV", f"rs{i}")], inc=[("sVX", 1)], label=f"bc{i}")
                hw = [("sVX", f"bc{i}"), (f"x0{i % 4}", f"x0{i}")]
                if i >= 2:
                    hw.append(("sMM", f"htp{i-2}_{G-1}"))
                p.step("ve", lambda e, i=i, h_bf=h_bf, stat=stat: e.tensor_scalar(
                    h_bf, x0t[:, i % 4, :], stat[:, 5:6], stat[:, 6:7],
                    ALU.mult, ALU.add),
                    waits=hw, inc=[("sVX", 1)], label=f"h{i}", dur=600)
                for g in range(G):
                    bk = g % 4
                    p.step("te", lambda e, g=g, bk=bk, h_bf=h_bf: e.transpose(
                        ptpb[bk], h_bf[:, g * 128:(g + 1) * 128],
                        ctx.ident_bf[:, :]),
                        waits=[("sVX", f"h{i}")] + tpw(bk),
                        inc=[("sMM", 1)], label=f"htp{i}_{g}", dur=120)
                    ew = [("sMM", f"htp{i}_{g}")]
                    if g < 2 and i >= 2:
                        ew.append(("sMM", f"bmm{i-2}"))
                    sm = evac(p, g, h_dm[:, g, :], ptpb[bk],
                              waits=ew, label=f"hde{i}_{g}")
                    tp_user[bk] = (sm, f"hde{i}_{g}")
                def mmh(e, wsb, po, hh, h_dm=h_dm):
                    r = None
                    for g in range(G):
                        r = e.matmul(po[:, hh * 512:(hh + 1) * 512], h_dm[:, g, :],
                                     wsb[:, g, hh * 512:(hh + 1) * 512],
                                     start=(g == 0), stop=(g == G - 1))
                    return r
                for hh in range(2):
                    qw = [("sVX", f"hde{i}_{G-1}"), ("sEV", f"hde{i}_{G-2}")]
                    if i == 0:
                        qw.append(("sEV", "wf0"))
                    if i >= 1:
                        qw.append((("sEV", f"vev{i-1}_0") if hh == 0
                                   else ("sVX", f"vev{i-1}_1")))
                    p.step("te", lambda e, hh=hh, mmh=mmh: mmh(e, wq_sb, pA, hh),
                           waits=qw, inc=[("sMM", 1)], label=f"qmm{i}_{hh}",
                           dur=1700)
                qv0 = [("sMM", f"qmm{i}_0")]
                qv1 = [("sMM", f"qmm{i}_1")]
                if i >= 2:
                    qv0.append((f"qst{i % 2}", f"qst{i-2}"))
                    qv1.append((f"qst{i % 2}", f"qst{i-2}"))
                p.step("se", lambda e, q_tm=q_tm: e.activation(
                    q_tm[:, 0:512], pA[:, 0:512], AF.Copy),
                    waits=qv0, inc=[("sEV", 1)], label=f"qev{i}_0", dur=580)
                p.step("ve", lambda e, q_tm=q_tm: e.tensor_copy(
                    q_tm[:, 512:1024], pA[:, 512:1024]),
                    waits=qv1, inc=[("sVX", 1)], label=f"qev{i}_1", dur=580)
                p.dma("se", io["q_s"][b, c, :, :], q_tm, sem=f"qst{i % 2}",
                      waits=[("sEV", f"qev{i}_0"), ("sVX", f"qev{i}_1")],
                      label=f"qst{i}")
                for hh in range(2):
                    vw = [("sVX", f"hde{i}_{G-1}"), ("sEV", f"hde{i}_{G-2}")]
                    if i == 0:
                        vw.append(("sEV", "wf2"))
                    vw.append((("sEV", f"qev{i}_0") if hh == 0
                               else ("sVX", f"qev{i}_1")))
                    p.step("te", lambda e, hh=hh, mmh=mmh: mmh(e, wv_sb, pA, hh),
                           waits=vw, inc=[("sMM", 1)], label=f"vmm{i}_{hh}",
                           dur=1700)
                vv0 = [("sMM", f"vmm{i}_0")]
                vv1 = [("sMM", f"vmm{i}_1")]
                if i >= 2:
                    vv0.append(("sMM", f"vtp{i-2}_{G-1}"))
                    vv1.append(("sMM", f"vtp{i-2}_{G-1}"))
                p.step("se", lambda e, v_tm=v_tm: e.activation(
                    v_tm[:, 0:512], pA[:, 0:512], AF.Copy),
                    waits=vv0, inc=[("sEV", 1)], label=f"vev{i}_0", dur=580)
                p.step("ve", lambda e, v_tm=v_tm: e.tensor_copy(
                    v_tm[:, 512:1024], pA[:, 512:1024]),
                    waits=vv1, inc=[("sVX", 1)], label=f"vev{i}_1", dur=580)
                for g in range(G):
                    bk = g % 4
                    p.step("te", lambda e, g=g, bk=bk, v_tm=v_tm: e.transpose(
                        ptpb[bk], v_tm[:, g * 128:(g + 1) * 128],
                        ctx.ident_bf[:, :]),
                        waits=[("sEV", f"vev{i}_0"), ("sVX", f"vev{i}_1")]
                        + tpw(bk),
                        inc=[("sMM", 1)], label=f"vtp{i}_{g}", dur=120)
                    vdw = [("sMM", f"vtp{i}_{g}")]
                    if i >= 2:
                        vdw.append(("sVX", f"ema{i-2}_{g}"))
                    sm = evac(p, g, vb_c2[:, s2, g, :], ptpb[bk],
                              waits=vdw, label=f"vde{i}_{g}",
                              scale=ctx.alpha_sb[:, g:g + 1])
                    tp_user[bk] = (sm, f"vde{i}_{g}")
                for g in range(G):
                    emw = [(("sEV" if g % 2 == 0 else "sVX"), f"vde{i}_{g}")]
                    if i >= 1:
                        emw.append(("sVX", f"ema{i-1}_{g}"))
                    if i >= 2:
                        emw.append(("sMM", f"emtp{i-2}_{g}"))
                    if c > 0:
                        init = ema2[:, (i - 1) % 2, g, C - 1:C]
                    else:
                        init = 0.0
                    p.step("ve", lambda e, g=g, s2=s2, init=init: e.tensor_tensor_scan(
                        ema2[:, s2, g, :],
                        ctx.a_sb[:, g:g + 1].broadcast_to([128, C]),
                        vb_c2[:, s2, g, :], init, ALU.mult, ALU.add),
                        waits=emw, inc=[("sVX", 1)], label=f"ema{i}_{g}", dur=220)
                    bk = g % 4
                    p.step("te", lambda e, g=g, bk=bk, s2=s2: e.transpose(
                        ptp[bk][:, :], ema2[:, s2, g, :], ctx.ident[:, :]),
                        waits=[("sVX", f"ema{i}_{g}")] + tpw(bk),
                        inc=[("sMM", 1)], label=f"emtp{i}_{g}", dur=120)
                    edw = [("sMM", f"emtp{i}_{g}")]
                    if g < 2 and i >= 2:
                        edw.append((f"vs{i % 2}", f"vsp{i-2}"))
                    sm = evac(p, g + 1, vst2[:, s2, g * 128:(g + 1) * 128],
                              ptp[bk][:, :], waits=edw, label=f"emde{i}_{g}")
                    tp_user[bk] = (sm, f"emde{i}_{g}")
                p.dma("sp", io["v_s"][b, c, :, :], vst2[:, s2, :],
                      sem=f"vs{i % 2}",
                      waits=[("sEV", f"emde{i}_{G-1}"), ("sVX", f"emde{i}_{G-2}")],
                      label=f"vsp{i}")
                for hh in range(2):
                    kw = [("sVX", f"hde{i}_{G-1}"), ("sEV", f"hde{i}_{G-2}")]
                    if i == 0:
                        kw.append(("sEV", "wf1"))
                    if i >= 1 and hh == 0:
                        kw.append(("sEV", f"bex{i-1}"))
                    if i >= 1:
                        kw.append((("sEV", f"kev{i-1}_0") if hh == 0
                                   else ("sVX", f"kev{i-1}_1")))
                    p.step("te", lambda e, hh=hh, mmh=mmh: mmh(e, wk_sb, pk, hh),
                           waits=kw, inc=[("sMM", 1)], label=f"kmm{i}_{hh}",
                           dur=1700)
                ksw = [("sMM", f"kmm{i}_0"), ("sMM", f"kmm{i}_1")]
                if i >= 2:
                    ksw.append(("sVX", f"kn{i-2}"))
                p.chain("se", [
                    lambda e, stat=stat: e.activation(
                        kk_scr[:, :], pk[:, :], AF.Square, accum_out=stat[:, 7:8]),
                    lambda e, stat=stat: e.activation(stat[:, 8:9], stat[:, 7:8],
                                                      AF.Ln),
                    lambda e, stat=stat: e.activation(stat[:, 8:9], stat[:, 8:9],
                                                      AF.Exp, scale=0.5)],
                    waits=ksw, inc=[("sEV", 1)], label=f"ksq{i}")
                knw = [("sEV", f"ksq{i}")]
                if i >= 2:
                    knw += [("sEV", f"kev{i-2}_0"), ("sVX", f"kev{i-2}_1")]
                p.chain("ve", [
                    lambda e, stat=stat: e.tensor_scalar_max(stat[:, 8:9],
                                                             stat[:, 8:9], 1e-12),
                    lambda e, stat=stat: e.reciprocal(stat[:, 9:10], stat[:, 8:9])],
                    waits=knw, inc=[("sVX", 1)], label=f"kn{i}")
                kevw0 = [("sVX", f"kn{i}")]
                kevw1 = [("sVX", f"kn{i}")]
                if i >= 2:
                    kevw0.append((f"kst{i % 2}", f"kst{i-2}"))
                    kevw1.append((f"kst{i % 2}", f"kst{i-2}"))
                p.step("se", lambda e, k_tm=k_tm, stat=stat: e.activation(
                    k_tm[:, 0:512], pk[:, 0:512], AF.Copy, scale=stat[:, 9:10]),
                    waits=kevw0, inc=[("sEV", 1)], label=f"kev{i}_0", dur=580)
                p.step("ve", lambda e, k_tm=k_tm, stat=stat: e.tensor_scalar_mul(
                    k_tm[:, 512:1024], pk[:, 512:1024], stat[:, 9:10]),
                    waits=kevw1, inc=[("sVX", 1)], label=f"kev{i}_1", dur=580)
                p.dma("se", io["k_s"][b, c, :, :], k_tm, sem=f"kst{i % 2}",
                      waits=[("sEV", f"kev{i}_0"), ("sVX", f"kev{i}_1")],
                      label=f"kst{i}")
                bw2 = [("sVX", f"hde{i}_{G-1}"), ("sEV", f"hde{i}_{G-2}"),
                       ("sEV", f"kev{i}_0"), ("sVX", f"kev{i}_1")]
                if i == 0:
                    bw2.append(("sVX", "bwf"))

                def bmm(e, h_dm=h_dm):
                    r = None
                    for g in range(G):
                        r = e.matmul(pk[:, 0:1], h_dm[:, g, :], bw_sb[:, g, :],
                                     start=(g == 0), stop=(g == G - 1))
                    return r
                bexw = [("sMM", f"bmm{i}")]
                if i >= 1:
                    bexw.append(("sVX", f"bfin{i-1}"))
                p.step("te", bmm, waits=bw2, inc=[("sMM", 1)], label=f"bmm{i}",
                       dur=450)
                p.step("se", lambda e: e.activation(bcell[:, :], pk[:, 0:1], AF.Exp,
                                                    scale=-1.0, bias=ctx.bbn_col[:, :]),
                       waits=bexw, inc=[("sEV", 1)], label=f"bex{i}")
                p.chain("ve", [
                    lambda e: e.tensor_scalar_add(bcell[:, :], bcell[:, :], 1.0),
                    lambda e, i=i: e.reciprocal(ctx.beta_st[:, i:i + 1], bcell[:, :]),
                    lambda e, i=i: e.tensor_scalar_mul(ctx.betan_st[:, i:i + 1],
                                                       ctx.beta_st[:, i:i + 1], -1.0)],
                    waits=[("sEV", f"bex{i}")], inc=[("sVX", 1)], label=f"bfin{i}")
        p.cur_group = None
        p.quiesce("p1")
        p.emit(reorder=True)



def emit_wquant(ctx):
    nc, io = ctx.nc, ctx.io
    with ExitStack() as st:
        lsb = lambda n, s, d: st.enter_context(nc.sbuf_tensor(n, s, d))
        wt = lsb("wqt", [128, 4, II], F32)
        z16 = lsb("wqz", [128, 2, II], F16)
        qw = lsb("wqq", [128, 2, II], BF16)
        col = lsb("wqc", [128, 2, 4], F32)
        p = Prog(ctx)
        p.mark("wq0")
        jobs = []
        for (src_t, dst_t, wfree) in ((io["fwg"], io["wgq_s"], D),
                                      (io["fwu"], io["wuq_s"], D),
                                      (io["fwd"], io["wdq_s"], II)):
            for j in range(src_t.shape[0] // 128):
                jobs.append((src_t, dst_t, wfree, j))
        for idx, (src_t, dst_t, wfree, j) in enumerate(jobs):
            sl = idx % 2
            sl4 = idx % 4
            p.cur_group = ("W", idx)
            cl = col[:, sl, :]
            lw = ([("sEV", f"wqz{idx-4}"), ("sVX", f"wqm{idx-4}")]
                  if idx >= 4 else [])
            p.dma("sp", wt[:, sl4, 0:wfree], src_t[j * 128:(j + 1) * 128, :],
                  sem=f"wql{sl4}", waits=lw, label=f"wql{idx}", xfer=8000)
            mw = [(f"wql{sl4}", f"wql{idx}")]
            if idx >= 2:
                mw.append(("sVX", f"wqi{idx-2}"))
            p.step("ve", lambda e, sl4=sl4, cl=cl, wfree=wfree: e.tensor_reduce(
                cl[:, 0:1], wt[:, sl4, 0:wfree], AX.X, ALU.max,
                apply_absolute_value=True),
                waits=mw, inc=[("sVX", 1)], label=f"wqm{idx}", dur=1400)
            iw = [("sVX", f"wqm{idx}")]
            if idx >= 2:
                iw += [("sVX", f"wqs{idx-2}"), ("sEV", f"wqz{idx-2}")]
            p.chain("ve", [
                lambda e, cl=cl: e.tensor_scalar_mul(cl[:, 1:2], cl[:, 0:1],
                                                     1.0 / 127.0),
                lambda e, cl=cl: e.reciprocal(cl[:, 2:3], cl[:, 1:2])],
                waits=iw, inc=[("sVX", 1)], label=f"wqi{idx}")
            zw = [("sVX", f"wqi{idx}")]
            if idx >= 2:
                zw.append(("sVX", f"wqs{idx-2}"))
            p.step("se", lambda e, sl=sl, sl4=sl4, cl=cl, wfree=wfree:
                   e.activation(
                z16[:, sl, 0:wfree], wt[:, sl4, 0:wfree], AF.Identity,
                scale=cl[:, 2:3], bias=1536.0),
                waits=zw, inc=[("sEV", 1)], label=f"wqz{idx}", dur=2400)
            sw = [("sEV", f"wqz{idx}")]
            if idx >= 2:
                sw.append((f"wqs{idx % 2}", f"wqst{idx-2}"))
            p.step("ve", lambda e, sl=sl, cl=cl, wfree=wfree: e.tensor_scalar(
                qw[:, sl, 0:wfree], z16[:, sl, 0:wfree], -1536.0, cl[:, 1:2],
                ALU.add, ALU.mult),
                waits=sw, inc=[("sVX", 1)], label=f"wqs{idx}", dur=1400)
            p.dma("se", dst_t[j * 128:(j + 1) * 128, :], qw[:, sl, 0:wfree],
                  sem=f"wqs{idx % 2}", waits=[("sVX", f"wqs{idx}")],
                  label=f"wqst{idx}", xfer=6000)
        p.cur_group = None
        p.quiesce("wq")
        p.emit(reorder=True)




def emit_p2a(ctx):
    nc, io = ctx.nc, ctx.io
    with ExitStack() as st:
        lsb = lambda n, s, d: st.enter_context(nc.sbuf_tensor(n, s, d))
        lps = lambda n, s: st.enter_context(nc.psum_tensor(n, s, F32))
        S_f = lsb("S_f", [128, NB, G, D], F32)
        S_b = lsb("S_b", [128, NB, G, D], BF16)
        q_tm4 = lsb("p2q", [128, 4, D], BF16)
        k_tm4 = lsb("p2k", [128, 4, D], BF16)
        v_tm4 = lsb("p2v", [128, 4, D], BF16)
        q_dm2 = lsb("p2qd", [128, 4, G, C], BF16)
        k_dm2 = lsb("p2kd", [128, 4, G, C], BF16)
        U_bf2 = lsb("p2U", [128, 4, D], BF16)
        Pt_bf = lsb("p2Pt", [128, 128], BF16)
        Qt_bf = lsb("p2Qt", [128, 128], BF16)
        RHS_bf2 = lsb("p2R", [128, 2, D], BF16)
        O_bf2 = lsb("p2O", [128, 2, D], BF16)
        vb_f2 = lsb("p2vb", [128, 2, D], F32)
        A_f2 = lsb("p2A", [128, 2, 128], F32)
        M_bf2 = lsb("p2M", [128, 2, 128], BF16)
        W_bf2 = lsb("p2W", [128, 2, 128], BF16)
        X_bf2 = [[lsb(f"p2X{j}_{s}", [128, 128], BF16) for j in range(2)]
                 for s in range(2)]
        Y_bf2 = [[lsb(f"p2Y{j}_{s}", [128, 128], BF16) for j in range(2)]
                 for s in range(2)]
        P_bf2 = lsb("p2P", [128, 2, 128], BF16)
        pW = [lps(f"p2w{j}", [128, 512]) for j in range(4)]
        pXY = [(lps("p2g", [128, 128]), lps("p2t", [128, 128])),
               (lps("p2n1", [128, 128]), lps("p2n2", [128, 128]))]
        pXb = [x[:, :].bitcast(BF16)[:, 0:128] for x, _ in pXY]
        pYb = [y[:, :].bitcast(BF16)[:, 0:128] for _, y in pXY]

        p = Prog(ctx)
        p.mark("p2a0")
        p.step("pl", lambda e: (e.memset(S_f[:, :, :, :], 0.0),
                                e.memset(S_b[:, :, :, :], 0.0)),
               inc=[("sVX", 1)], label="s_zero")

        pu = {}

        def bw(key):
            return [(pu[key][0], pu[key][1])] if key in pu else []

        def setu(key, sem, lab):
            pu[key] = (sem, lab)

        Slast = {0: ("sVX", "s_zero"), 1: ("sVX", "s_zero")}
        insts = [(c, b) for c in range(NCH) for b in range(NB)]
        NI = len(insts)

        def idx(pp):
            c2, b2 = insts[pp]
            return b2 * NCH + c2

        def su_of(pp):
            return pp + 2 if (pp // 2) % 2 == 0 else pp

        def a_load(pos):
            c, b = insts[pos]
            n = f"{idx(pos)}"
            sl = pos % 4
            j = idx(pos - 4) if pos >= 4 else None
            p.dma("sp", q_tm4[:, sl, :], io["q_s"][b, c, :, :], sem=f"aq{sl}",
                  waits=[("sMM", f"qtp{j}_{G-1}")] if j is not None else [],
                  label=f"ql{n}")
            yield
            p.dma("sp", k_tm4[:, sl, :], io["k_s"][b, c, :, :], sem=f"ak{sl}",
                  waits=[("sMM", f"su{idx(su_of(pos - 4))}_{G-1}")]
                  if j is not None else [],
                  label=f"kl{n}")
            yield
            p.dma("sp", v_tm4[:, sl, :], io["v_s"][b, c, :, :], sem=f"av{sl}",
                  waits=[("sVX", f"vb{j}")] if j is not None else [],
                  label=f"vl{n}")
            yield

        def a_comp(pos):
            n = f"{idx(pos)}"
            sl = pos % 4
            ip = pos % 2
            ceven = (pos // 2) % 2 == 0
            q_tm = q_tm4[:, sl, :]
            k_tm = k_tm4[:, sl, :]
            q_dm = q_dm2[:, sl, :, :]
            k_dm = k_dm2[:, sl, :, :]
            A_f = A_f2[:, ip, :]
            M_bf = M_bf2[:, ip, :]
            W_bf = W_bf2[:, ip, :]
            X_bf = X_bf2[ip]
            Y_bf = Y_bf2[ip]
            P_bf = P_bf2[:, ip, :]
            pX, pY = pXY[ip]
            pX_b, pY_b = pXb[ip], pYb[ip]
            kX, kY = f"pX{ip}", f"pY{ip}"
            prev2 = idx(pos - 2) if pos >= 2 else None
            for g in range(G):
                qw = [(f"aq{sl}", f"ql{n}")] if g == 0 else []
                p.step("te", lambda e, g=g, q_tm=q_tm: e.transpose(
                    pX_b, q_tm[:, g * 128:(g + 1) * 128], ctx.ident_bf[:, :]),
                    waits=qw + bw(kX), inc=[("sMM", 1)], label=f"qtp{n}_{g}",
                    dur=120)
                ew = [("sMM", f"qtp{n}_{g}")]
                if g == 0 and pos >= 4:
                    j4 = idx(pos - 4)
                    ew += [("sMM", f"o{j4}_1"), ("sMM", f"kq{j4}")]
                sm = evac(p, g, q_dm[:, g, :], pX_b,
                          waits=ew, label=f"qde{n}_{g}")
                setu(kX, sm, f"qde{n}_{g}")
                yield
                kw = [(f"ak{sl}", f"kl{n}")] if g == 0 else []
                p.step("te", lambda e, g=g, k_tm=k_tm: e.transpose(
                    pY_b, k_tm[:, g * 128:(g + 1) * 128], ctx.ident_bf[:, :]),
                    waits=kw + bw(kY), inc=[("sMM", 1)], label=f"ktp{n}_{g}",
                    dur=120)
                ew = [("sMM", f"ktp{n}_{g}")]
                if g == 0 and pos >= 4:
                    j4 = idx(pos - 4)
                    ew += [("sMM", f"ks{j4}_1"), ("sMM", f"kq{j4}")]
                    if (pos - 4) // 2 % 2 == 0:
                        ew.append(("sMM", f"ptq{idx(pos-2)}"))
                sm = evac(p, g + 1, k_dm[:, g, :], pY_b,
                          waits=ew, label=f"kde{n}_{g}")
                setu(kY, sm, f"kde{n}_{g}")
                yield
            def gmm(e, k_dm=k_dm):
                r = None
                for g in range(G):
                    r = e.matmul(pX[:, :], k_dm[:, g, :], k_dm[:, g, :],
                                 start=(g == 0), stop=(g == G - 1))
                return r
            p.step("te", gmm,
                   waits=[("sEV", f"kde{n}_{G-1}"), ("sVX", f"kde{n}_{G-2}")]
                   + bw(kX), inc=[("sMM", 1)], label=f"gmm{n}", dur=900)
            yield
            aw = [("sMM", f"gmm{n}")]
            if prev2 is not None:
                aw.append(("sMM", f"nt{prev2}_5c"))
            i = idx(pos)
            p.chain("ve", [
                lambda e, A_f=A_f: e.tensor_tensor(A_f, pX[:, :], ctx.tril_s[:, :],
                                                   ALU.mult),
                lambda e, i=i, A_f=A_f: e.tensor_scalar_mul(A_f, A_f,
                                                            ctx.beta_st[:, i:i + 1]),
                lambda e, A_f=A_f, M_bf=M_bf: e.tensor_tensor(
                    M_bf, A_f, ctx.ident[:, :], ALU.add)],
                waits=aw, inc=[("sVX", 1)], label=f"am{n}")
            setu(kX, "sVX", f"am{n}")
            p.step("te", lambda e, A_f=A_f: e.transpose(pY[:, :], A_f,
                                                        ctx.ident[:, :]),
                   waits=[("sVX", f"am{n}")] + bw(kY),
                   inc=[("sMM", 1)], label=f"atp{n}", dur=120)
            yield
            xw = [("sMM", f"atp{n}")]
            if prev2 is not None:
                xw.append(("sMM", f"u{prev2}_1"))
            p.step("ve", lambda e, X_bf=X_bf, Y_bf=Y_bf, A_f=A_f: (
                e.scalar_tensor_tensor(X_bf[0][:, :], pY[:, :], -1.0,
                                       ctx.ident[:, :], ALU.mult, ALU.add),
                e.scalar_tensor_tensor(Y_bf[0][:, :], A_f, -1.0,
                                       ctx.ident[:, :], ALU.mult, ALU.add)),
                waits=xw, inc=[("sVX", 1)], label=f"xy0{n}", dur=450)
            setu(kY, "sVX", f"xy0{n}")
            def kqmm(e, k_dm=k_dm, q_dm=q_dm):
                r = None
                for g in range(G):
                    r = e.matmul(pX[:, :], k_dm[:, g, :], q_dm[:, g, :],
                                 start=(g == 0), stop=(g == G - 1))
                return r
            p.step("te", kqmm,
                   waits=[("sVX", f"qde{n}_{G-1}"), ("sEV", f"qde{n}_{G-2}")]
                   + bw(kX), inc=[("sMM", 1)], label=f"kq{n}", dur=900)
            yield
            ww = [("sMM", f"kq{n}")]
            if prev2 is not None:
                ww.append(("sMM", f"o{prev2}_1"))
            p.step("ve", lambda e, W_bf=W_bf: e.tensor_tensor(
                W_bf, pX[:, :], ctx.triu_i[:, :], ALU.mult),
                waits=ww, inc=[("sVX", 1)], label=f"wm{n}")
            setu(kX, "sVX", f"wm{n}")
            cx = 0
            for it in range(6):
                nw = [("sVX", f"xy0{n}")] if it == 0 else \
                    [("sVX", f"nt{n}_{it-1}v")]
                p.step("te", lambda e, cx=cx, M_bf=M_bf, X_bf=X_bf: e.matmul(
                    pY[:, :], M_bf, X_bf[cx][:, :], start=True, stop=True),
                    waits=nw + bw(kY), inc=[("sMM", 1)], label=f"nt{n}_{it}a",
                    dur=150)
                yield
                p.step("se", lambda e, P_bf=P_bf: e.activation(P_bf, pY[:, :],
                                                               AF.Copy),
                       waits=[("sMM", f"nt{n}_{it}a")], inc=[("sEV", 1)],
                       label=f"nt{n}_{it}p", dur=400)
                setu(kY, "sEV", f"nt{n}_{it}p")
                p.step("te", lambda e, cx=cx, Y_bf=Y_bf, P_bf=P_bf: e.matmul(
                    pX[:, :], Y_bf[cx][:, :], P_bf, start=True, stop=True),
                    waits=[("sEV", f"nt{n}_{it}p")] + bw(kX),
                    inc=[("sMM", 1)], label=f"nt{n}_{it}b", dur=150)
                yield
                p.step("te", lambda e, cx=cx, Y_bf=Y_bf, P_bf=P_bf: e.matmul(
                    pY[:, :], P_bf, Y_bf[cx][:, :], start=True, stop=True),
                    waits=[("sEV", f"nt{n}_{it}p")] + bw(kY),
                    inc=[("sMM", 1)], dur=150,
                    label=(f"nt{n}_5c" if it == 5 else f"nt{n}_{it}c"))
                yield
                p.step("ve", lambda e, cx=cx, X_bf=X_bf, Y_bf=Y_bf: (
                    e.scalar_tensor_tensor(X_bf[1 - cx][:, :], X_bf[cx][:, :], 2.0,
                                           pX[:, :], ALU.mult, ALU.subtract),
                    e.scalar_tensor_tensor(Y_bf[1 - cx][:, :], Y_bf[cx][:, :], 2.0,
                                           pY[:, :], ALU.mult, ALU.subtract)),
                    waits=[("sMM", f"nt{n}_{it}b"),
                           ("sMM", f"nt{n}_5c" if it == 5 else f"nt{n}_{it}c")],
                    inc=[("sVX", 1)], label=f"nt{n}_{it}v", dur=500)
                setu(kX, "sVX", f"nt{n}_{it}v")
                setu(kY, "sVX", f"nt{n}_{it}v")
                cx = 1 - cx

        def b_stage(pos):
            c, b = insts[pos]
            i = idx(pos)
            n = f"{i}"
            sl = pos % 4
            ip = pos % 2
            ceven = (pos // 2) % 2 == 0
            k_tm = k_tm4[:, sl, :]
            v_tm = v_tm4[:, sl, :]
            q_dm = q_dm2[:, sl, :, :]
            k_dm = k_dm2[:, sl, :, :]
            U_bf = U_bf2[:, sl, :]
            RHS_bf = RHS_bf2[:, ip, :]
            O_bf = O_bf2[:, ip, :]
            vb_f = vb_f2[:, ip, :]
            W_bf = W_bf2[:, ip, :]
            X_bf = X_bf2[ip]
            prev2 = idx(pos - 2) if pos >= 2 else None
            nfin = f"nt{n}_5v"
            if not ceven:
                sl0 = (pos - 2) % 4
                n0 = f"{idx(pos - 2)}"
                k_dm0 = k_dm2[:, sl0, :, :]
                k_tm0 = k_tm4[:, sl0, :]
                U_bf0 = U_bf2[:, sl0, :]
                def ptmm(e, k_dm=k_dm, k_dm0=k_dm0):
                    r = None
                    for g in range(G):
                        r = e.matmul(pW[2][:, 0:128], k_dm0[:, g, :],
                                     k_dm[:, g, :], start=(g == 0),
                                     stop=(g == G - 1))
                    return r
                p.step("te", ptmm,
                       waits=[("sEV", f"kde{n}_{G-1}"), ("sVX", f"kde{n}_{G-2}")]
                       + bw("pW2"), inc=[("sMM", 1)], label=f"pt{n}", dur=900)
                yield

                def qtmm(e, q_dm=q_dm, k_dm0=k_dm0):
                    r = None
                    for g in range(G):
                        r = e.matmul(pW[3][:, 0:128], k_dm0[:, g, :],
                                     q_dm[:, g, :], start=(g == 0),
                                     stop=(g == G - 1))
                    return r
                p.step("te", qtmm,
                       waits=[("sVX", f"qde{n}_{G-1}"), ("sEV", f"qde{n}_{G-2}")]
                       + bw("pW3"), inc=[("sMM", 1)], label=f"ptq{n}", dur=900)
                ptw = [("sMM", f"pt{n}")]
                if pos >= 4:
                    ptw.append(("sMM", f"ks{idx(pos-4)}_1"))
                p.step("se", lambda e: e.activation(Pt_bf[:, :], pW[2][:, 0:128],
                                                    AF.Copy),
                       waits=ptw, inc=[("sEV", 1)], label=f"pte{n}")
                setu("pW2", "sEV", f"pte{n}")
                qtw = [("sMM", f"ptq{n}")]
                if pos >= 4:
                    qtw.append(("sMM", f"o{idx(pos-4)}_1"))
                p.step("ve", lambda e: e.tensor_copy(Qt_bf[:, :], pW[3][:, 0:128]),
                       waits=qtw, inc=[("sVX", 1)], label=f"qte{n}")
                setu("pW3", "sVX", f"qte{n}")
                yield
            for hh in range(2):
                def ksmm(e, hh=hh, b=b, k_dm=k_dm, ceven=ceven, pos=pos):
                    r = None
                    for g in range(G):
                        r = e.matmul(pW[hh][:, :], k_dm[:, g, :],
                                     S_b[:, b, g, hh * 512:(hh + 1) * 512],
                                     start=(g == 0),
                                     stop=(ceven and g == G - 1))
                    if not ceven:
                        r = e.matmul(pW[hh][:, :], Pt_bf[:, :],
                                     U_bf2[:, (pos - 2) % 4,
                                           hh * 512:(hh + 1) * 512],
                                     start=False, stop=True)
                    return r
                ksw = [Slast[b], ("sEV", f"kde{n}_{G-1}"),
                       ("sVX", f"kde{n}_{G-2}")] + bw(f"pW{hh}")
                if not ceven:
                    ksw += [("sEV", f"pte{n}"), ("sEV", f"ue{n0}")]
                p.step("te", ksmm, waits=ksw,
                       inc=[("sMM", 1)], label=f"ks{n}_{hh}", dur=1900)
                yield
            vw = [(f"av{sl}", f"vl{n}")]
            if prev2 is not None:
                vw.append(("sVX", f"rhs{prev2}"))
            p.step("ve", lambda e, i=i, vb_f=vb_f, v_tm=v_tm: e.tensor_scalar_mul(
                vb_f, v_tm, ctx.beta_st[:, i:i + 1]),
                waits=vw, inc=[("sVX", 1)], label=f"vb{i}")
            rhw = [("sVX", f"vb{i}"), ("sMM", f"ks{n}_0"), ("sMM", f"ks{n}_1")]
            if prev2 is not None:
                rhw.append(("sMM", f"u{prev2}_1"))
            p.step("ve", lambda e, i=i, RHS_bf=RHS_bf, vb_f=vb_f: (
                e.scalar_tensor_tensor(RHS_bf[:, 0:512], pW[0][:, :],
                                       ctx.betan_st[:, i:i + 1], vb_f[:, 0:512],
                                       ALU.mult, ALU.add),
                e.scalar_tensor_tensor(RHS_bf[:, 512:1024], pW[1][:, :],
                                       ctx.betan_st[:, i:i + 1], vb_f[:, 512:1024],
                                       ALU.mult, ALU.add)),
                waits=rhw, inc=[("sVX", 1)], label=f"rhs{n}", dur=500)
            setu("pW0", "sVX", f"rhs{n}")
            setu("pW1", "sVX", f"rhs{n}")
            yield
            for hh in range(2):
                p.step("te", lambda e, hh=hh, X_bf=X_bf, RHS_bf=RHS_bf:
                       e.matmul(pW[2 + hh][:, :], X_bf[0][:, :],
                                RHS_bf[:, hh * 512:(hh + 1) * 512],
                                start=True, stop=True),
                       waits=[("sVX", f"rhs{n}"), ("sVX", nfin)] + bw(f"pW{2+hh}"),
                       inc=[("sMM", 1)], label=f"u{i}_{hh}")
                yield
            uew = [("sMM", f"u{i}_0"), ("sMM", f"u{i}_1")]
            if pos >= 4:
                uew.append(("sMM", f"su{idx(su_of(pos-4))}_{G-1}"))
            p.step("se", lambda e, U_bf=U_bf: (
                e.activation(U_bf[:, 0:512], pW[2][:, :], AF.Copy),
                e.activation(U_bf[:, 512:1024], pW[3][:, :], AF.Copy)),
                waits=uew, inc=[("sEV", 1)], label=f"ue{n}", dur=1100)
            setu("pW2", "sEV", f"ue{n}")
            setu("pW3", "sEV", f"ue{n}")
            for hh in range(2):
                def omm(e, hh=hh, b=b, q_dm=q_dm, W_bf=W_bf, U_bf=U_bf,
                        ceven=ceven, pos=pos):
                    r = None
                    for g in range(G):
                        r = e.matmul(pW[hh][:, :], q_dm[:, g, :],
                                     S_b[:, b, g, hh * 512:(hh + 1) * 512],
                                     start=(g == 0), stop=False)
                    if not ceven:
                        r = e.matmul(pW[hh][:, :], Qt_bf[:, :],
                                     U_bf2[:, (pos - 2) % 4,
                                           hh * 512:(hh + 1) * 512],
                                     start=False, stop=False)
                    r = e.matmul(pW[hh][:, :], W_bf,
                                 U_bf[:, hh * 512:(hh + 1) * 512],
                                 start=False, stop=True)
                    return r
                ow = [("sVX", f"wm{n}"), ("sEV", f"ue{n}"),
                      ("sVX", f"qde{n}_{G-1}"), ("sEV", f"qde{n}_{G-2}")]
                if not ceven:
                    ow += [("sVX", f"qte{n}"), ("sEV", f"ue{n0}")]
                p.step("te", omm, waits=ow + bw(f"pW{hh}"),
                       inc=[("sMM", 1)], label=f"o{n}_{hh}", dur=2000)
                yield
            oew = [("sMM", f"o{n}_0"), ("sMM", f"o{n}_1")]
            if prev2 is not None:
                oew.append((f"aos{ip}", f"ost{prev2}"))
            p.step("se", lambda e, O_bf=O_bf: (
                e.activation(O_bf[:, 0:512], pW[0][:, :], AF.Copy),
                e.activation(O_bf[:, 512:1024], pW[1][:, :], AF.Copy)),
                waits=oew, inc=[("sEV", 1)], label=f"oe{n}", dur=1100)
            setu("pW0", "sEV", f"oe{n}")
            setu("pW1", "sEV", f"oe{n}")
            p.dma("se", io["o_s"][b, c, :, :], O_bf, sem=f"aos{ip}",
                  waits=[("sEV", f"oe{n}")], label=f"ost{i}")
            if ceven:
                return
            for g in range(G):
                b0i, b1i = (2, 3) if g % 2 == 0 else (0, 1)

                def summ(e, g=g, b0i=b0i, b1i=b1i, k_tm=k_tm, k_tm0=k_tm0,
                         U_bf=U_bf, U_bf0=U_bf0):
                    e.matmul(pW[b0i][:, :], k_tm0[:, g * 128:(g + 1) * 128],
                             U_bf0[:, 0:512], start=True, stop=False)
                    e.matmul(pW[b0i][:, :], k_tm[:, g * 128:(g + 1) * 128],
                             U_bf[:, 0:512], start=False, stop=True)
                    e.matmul(pW[b1i][:, :], k_tm0[:, g * 128:(g + 1) * 128],
                             U_bf0[:, 512:1024], start=True, stop=False)
                    return e.matmul(pW[b1i][:, :], k_tm[:, g * 128:(g + 1) * 128],
                                    U_bf[:, 512:1024], start=False, stop=True)
                suw = [("sEV", f"ue{n}"), ("sEV", f"ue{n0}")] \
                    + bw(f"pW{b0i}") + bw(f"pW{b1i}")
                if g == 0:
                    suw += [(f"ak{sl}", f"kl{n}"), (f"ak{sl0}", f"kl{n0}")]
                if g < 2:
                    suw += [("sEV", f"oe{n}")]
                p.step("te", summ, waits=suw, inc=[("sMM", 1)], label=f"su{n}_{g}",
                       dur=2200)
                p.step("ve", lambda e, g=g, b=b, b0i=b0i, b1i=b1i: (
                    e.tensor_tensor(S_f[:, b, g, 0:512], S_f[:, b, g, 0:512],
                                    pW[b0i][:, :], ALU.add),
                    e.tensor_tensor(S_f[:, b, g, 512:1024], S_f[:, b, g, 512:1024],
                                    pW[b1i][:, :], ALU.add)),
                    waits=[("sMM", f"su{n}_{g}")], inc=[("sVX", 1)],
                    label=f"sa{n}_{g}", dur=1300)
                setu(f"pW{b0i}", "sVX", f"sa{n}_{g}")
                setu(f"pW{b1i}", "sVX", f"sa{n}_{g}")
                p.step("se", lambda e, g=g, b=b: e.activation(
                    S_b[:, b, g, :], S_f[:, b, g, :], AF.Copy),
                    waits=[("sVX", f"sa{n}_{g}")], inc=[("sEV", 1)],
                    label=f"sc{n}_{g}", dur=1200)
                yield
            Slast[b] = ("sEV", f"sc{n}_{G-1}")

        for pos in range(NI):
            for gname, gen in (("L", a_load(pos)), ("A", a_comp(pos)),
                               ("B", b_stage(pos))):
                p.cur_group = (gname, pos)
                for _ in gen:
                    pass
        p.cur_group = None
        p.quiesce("p2a")
        p.emit(reorder=True)





def emit_p2bi(ctx):
    nc, io = ctx.nc, ctx.io
    with ExitStack() as st:
        lsb = lambda n, s, d: st.enter_context(nc.sbuf_tensor(n, s, d))
        lps = lambda n, s: st.enter_context(nc.psum_tensor(n, s, F32))
        wo_sb = lsb("wo_sb", [128, G, D], BF16)
        o_tm = lsb("bio", [128, 4, D], BF16)
        x0t = lsb("bix", [128, 4, D], F32)
        o_dm2 = lsb("biod", [128, 2, G, C], BF16)
        lnt2 = lsb("biln", [128, 2, D], F32)
        ax_f2 = lsb("biax", [128, 2, D], F32)
        scrap = lsb("bisc", [128, D], BF16)
        stat2 = lsb("bist", [128, 2, 4], F32)
        pWo = [lps("biw0", [128, 512]), lps("biw1", [128, 512])]
        ptp = [lps(f"bit{j}", [128, 128]) for j in range(4)]
        ptpb = [t[:, :].bitcast(BF16)[:, 0:128] for t in ptp]

        p = Prog(ctx)
        p.mark("bi0")
        p.dma("pl", wo_sb[:, :, :], io["wo"].rearrange("(g p) d -> p g d", p=128),
              sem="wo", label="biwl", xfer=12000)
        pu = {}

        def bw(key):
            return [pu[key]] if key in pu else []

        for i in range(NCHUNKS):
            b, c = divmod(i, NCH)
            n = f"{i}"
            sl = i % 4
            s2 = i % 2
            p.cur_group = ("I", i)
            o_dm = o_dm2[:, s2, :, :]
            lnt = lnt2[:, s2, :]
            ax_f = ax_f2[:, s2, :]
            stat = stat2[:, s2, :]
            lw = [("sMM", f"biot{i-4}_{G-1}")] if i >= 4 else []
            p.dma("sp", o_tm[:, sl, :], io["o_s"][b, c, :, :], sem=f"bo{i % 4}",
                  waits=lw, label=f"bio{n}", xfer=2500)
            xw = [("sEV", f"biln{i-4}")] if i >= 4 else []
            p.dma("sp", x0t[:, sl, :], io["x"][b, c * C:(c + 1) * C, :],
                  sem=f"bx{i % 4}", waits=xw, label=f"bix{n}", xfer=4000)
            lnw = [(f"bx{i % 4}", f"bix{n}")]
            if i >= 2:
                lnw.append(("sVX", f"biax{i-2}"))
            p.step("se", lambda e, sl=sl, i=i, lnt=lnt: e.activation(
                lnt, x0t[:, sl, :], AF.Identity,
                scale=ctx.r1_st[:, i:i + 1], bias=ctx.b1_st[:, i:i + 1]),
                waits=lnw, inc=[("sEV", 1)], label=f"biln{n}", dur=1150)
            for g in range(G):
                bk = g % 4
                p.step("te", lambda e, g=g, sl=sl, bk=bk: e.transpose(
                    ptpb[bk], o_tm[:, sl, g * 128:(g + 1) * 128], ctx.ident_bf[:, :]),
                    waits=[(f"bo{i % 4}", f"bio{n}")] + bw(f"tp{bk}"),
                    inc=[("sMM", 1)], label=f"biot{i}_{g}", dur=120)
                ew = [("sMM", f"biot{i}_{g}")]
                if g < 2 and i >= 2:
                    ew.append(("sMM", f"biwo{i-2}_1"))
                sm = evac(p, g, o_dm[:, g, :], ptpb[bk],
                          waits=ew, label=f"biod{n}_{g}")
                pu[f"tp{bk}"] = (sm, f"biod{n}_{g}")
            for hh in range(2):
                def womm(e, hh=hh, o_dm=o_dm):
                    r = None
                    for g in range(G):
                        r = e.matmul(pWo[hh][:, :], o_dm[:, g, :],
                                     wo_sb[:, g, hh * 512:(hh + 1) * 512],
                                     start=(g == 0), stop=(g == G - 1))
                    return r
                p.step("te", womm,
                       waits=[("sVX", f"biod{n}_{G-1}"), ("sEV", f"biod{n}_{G-2}"),
                              ("wo", "biwl")] + bw(f"w{hh}"),
                       inc=[("sMM", 1)], label=f"biwo{n}_{hh}", dur=1800)
            p.step("ve", lambda e, ax_f=ax_f, lnt=lnt: (
                e.tensor_tensor(ax_f[:, 0:512], lnt[:, 0:512], pWo[0][:, :], ALU.add),
                e.tensor_tensor(ax_f[:, 512:1024], lnt[:, 512:1024], pWo[1][:, :],
                                ALU.add)),
                waits=[("sEV", f"biln{n}"), ("sMM", f"biwo{n}_0"),
                       ("sMM", f"biwo{n}_1")]
                + ([("sEV", f"bir2{i-2}"), (f"bax{s2}", f"biaxs{i-2}")]
                   if i >= 2 else []),
                inc=[("sVX", 1)], label=f"biax{i}", dur=1100)
            pu["w0"] = ("sVX", f"biax{i}")
            pu["w1"] = ("sVX", f"biax{i}")
            sqw = [("sVX", f"biax{i}")]
            if i >= 2:
                sqw.append(("sVX", f"bir2v{i-2}"))
            p.chain("se", [
                lambda e, ax_f=ax_f, stat=stat: e.activation(
                    scrap[:, :], ax_f, AF.Square, accum_out=stat[:, 0:1]),
                lambda e, stat=stat: e.activation(stat[:, 1:2], stat[:, 0:1], AF.Ln,
                                                  scale=1.0 / D, bias=EPS_RMS),
                lambda e, stat=stat: e.activation(stat[:, 1:2], stat[:, 1:2], AF.Exp,
                                                  scale=-0.5)],
                waits=sqw, inc=[("sEV", 1)], label=f"bir2{n}")
            p.step("ve", lambda e, i=i, stat=stat: e.tensor_copy(
                ctx.r2_st[:, i:i + 1], stat[:, 1:2]),
                waits=[("sEV", f"bir2{n}")], inc=[("sVX", 1)], label=f"bir2v{i}")
            p.dma("se", io["ax_s"][b, c * C:(c + 1) * C, :], ax_f, sem=f"bax{s2}",
                  waits=[("sVX", f"biax{i}")], label=f"biaxs{i}", xfer=4000)
        p.cur_group = None
        p.quiesce("bi")
        p.emit(reorder=True)




def emit_p2bii(ctx):
    nc, io = ctx.nc, ctx.io
    with ExitStack() as st:
        lsb = lambda n, s, d: st.enter_context(nc.sbuf_tensor(n, s, d))
        lps = lambda n, s: st.enter_context(nc.psum_tensor(n, s, F32))
        wg_sb = lsb("wg_sb", [128, G, HID], BF16)
        wu_sb = lsb("wu_sb", [128, G, HID], BF16)
        wd_sb = lsb("wd_sb", [128, HG, D], BF16)

        with ExitStack() as st2:
            wstage = st2.enter_context(nc.sbuf_tensor("wstage2", [128, G, HID], BF16))
            p = Prog(ctx)
            p.mark("bw0")
            for wi, (srct, dst) in enumerate([(io["awg"], wg_sb), (io["awu"], wu_sb)]):
                w = [("sEV", f"bwf{wi-1}")] if wi else []
                p.dma("pl", wstage[:, :, :], srct.rearrange("(g p) d -> p g d", p=128),
                      sem="bw2l", waits=w, label=f"bwl{wi}")

                def fold(e, dst=dst):
                    r = None
                    for g in range(G):
                        r = e.activation(dst[:, g, :], wstage[:, g, :], AF.Copy,
                                         scale=ctx.n2w_sb[:, g:g + 1])
                    return r
                p.step("se", fold, waits=[("bw2l", f"bwl{wi}")], inc=[("sEV", 1)],
                       label=f"bwf{wi}")
            p.dma("pl", wd_sb[:, :, :], io["awd"].rearrange("(g p) d -> p g d", p=128),
                  sem="bw2l", waits=[("sEV", "bwf1")], label="bwd")
            p.quiesce("bw")
            p.emit()

        ax_t = lsb("c2ax", [128, 4, D], F32)
        x0t = lsb("c2x0", [128, 4, D], F32)
        h2_2 = lsb("c2h2", [128, 2, D], BF16)
        h2_dm2s = lsb("c2hd", [128, 2, G, 4 * C], BF16)
        sil2 = lsb("c2sl", [128, 2, 512], F32)
        m_dm2s = lsb("c2md", [128, 2, HG, 4 * C], BF16)
        x1_f2 = lsb("c2x1", [128, 2, D], F32)
        dif_f2 = lsb("c2df", [128, 2, D], F32)
        scrap = lsb("c2sc", [128, D], BF16)
        stat2 = lsb("c2st", [128, 2, 6], F32)
        pGa = [lps("c2g0", [128, 512]), lps("c2g1", [128, 512])]
        pUa = [lps("c2u0", [128, 512]), lps("c2u1", [128, 512])]
        pDn = [lps("c2d0", [128, 512]), lps("c2d1", [128, 512])]
        ptp = [lps("c2t0", [128, 128]), lps("c2t1", [128, 128])]
        ptpb = [t[:, :].bitcast(BF16)[:, 0:128] for t in ptp]

        p = Prog(ctx)
        p.mark("c20")
        pu = {}

        def bw(key):
            return [pu[key]] if key in pu else []

        NSC = NCHUNKS // 4
        for j in range(NSC):
            for t in range(4):
                i = 4 * j + t
                b, c = divmod(i, NCH)
                n = f"{i}"
                sl = i % 4
                p.cur_group = ("M", i)
                h2 = h2_2[:, i % 2, :]
                lw = ([("sEV", f"c2h2{i-4}"), ("sVX", f"c2x1c{i-4}")]
                      if i >= 4 else [])
                p.dma("sp", ax_t[:, sl, :], io["ax_s"][b, c * C:(c + 1) * C, :],
                      sem=f"cax{i % 4}", waits=lw, label=f"c2al{n}", xfer=4000)
                xw = [("sVX", f"c2x1c{i-4}")] if i >= 4 else []
                p.dma("sp", x0t[:, sl, :], io["x"][b, c * C:(c + 1) * C, :],
                      sem=f"ccx{i % 4}", waits=xw, label=f"c2xl{n}", xfer=4000)
                hw = [(f"cax{i % 4}", f"c2al{n}")]
                if i >= 2:
                    hw.append(("sMM", f"c2ht{i-2}_{G-1}"))
                p.step("se", lambda e, sl=sl, i=i, h2=h2: e.activation(
                    h2, ax_t[:, sl, :], AF.Copy, scale=ctx.r2_st[:, i:i + 1]),
                    waits=hw, inc=[("sEV", 1)], label=f"c2h2{i}", dur=1100)
                for g in range(G):
                    bk = g % 2
                    p.step("te", lambda e, g=g, bk=bk, h2=h2: e.transpose(
                        ptpb[bk], h2[:, g * 128:(g + 1) * 128],
                        ctx.ident_bf[:, :]),
                        waits=[("sEV", f"c2h2{i}")] + bw(f"tp{bk}"),
                        inc=[("sMM", 1)], label=f"c2ht{i}_{g}", dur=120)
                    ew = [("sMM", f"c2ht{i}_{g}")]
                    if g < 2 and j >= 2:
                        ew.append(("sMM", f"c2gu{j-2}_{HG-1}"))
                    sm = evac(p, g, h2_dm2s[:, j % 2, g, t * C:(t + 1) * C],
                              ptpb[bk], waits=ew, label=f"c2hd{n}_{g}")
                    pu[f"tp{bk}"] = (sm, f"c2hd{n}_{g}")
            p.cur_group = ("S", j)
            hdw = []
            for t in range(4):
                hdw += [("sVX", f"c2hd{4*j+t}_{G-1}"),
                        ("sEV", f"c2hd{4*j+t}_{G-2}")]
            for hg in range(HG):
                bk = hg % 2

                def gmm(e, hg=hg, bk=bk, j=j):
                    r = None
                    for g in range(G):
                        r = e.matmul(pGa[bk][:, :],
                                     wg_sb[:, g, hg * 128:(hg + 1) * 128],
                                     h2_dm2s[:, j % 2, g, :],
                                     start=(g == 0), stop=(g == G - 1))
                    for g in range(G):
                        r = e.matmul(pUa[bk][:, :],
                                     wu_sb[:, g, hg * 128:(hg + 1) * 128],
                                     h2_dm2s[:, j % 2, g, :],
                                     start=(g == 0), stop=(g == G - 1))
                    return r
                p.step("te", gmm,
                       waits=(hdw if hg == 0 else []) + bw(f"g{bk}")
                       + bw(f"u{bk}"),
                       inc=[("sMM", 1)], label=f"c2gu{j}_{hg}", dur=3400)
                sw = [("sMM", f"c2gu{j}_{hg}")] + bw(f"sil{bk}")
                p.step("se", lambda e, bk=bk: e.activation(
                    sil2[:, bk, :], pGa[bk][:, :], AF.Silu),
                    waits=sw, inc=[("sEV", 1)], label=f"c2si{j}_{hg}", dur=550)
                pu[f"g{bk}"] = ("sEV", f"c2si{j}_{hg}")
                mw = [("sEV", f"c2si{j}_{hg}")]
                if hg == 0 and j >= 2:
                    mw.append(("sMM", f"c2dn{4*(j-1)-1}_1"))
                p.step("ve", lambda e, hg=hg, bk=bk, j=j: e.tensor_tensor(
                    m_dm2s[:, j % 2, hg, :], sil2[:, bk, :], pUa[bk][:, :],
                    ALU.mult),
                    waits=mw, inc=[("sVX", 1)], label=f"c2m{j}_{hg}", dur=550)
                pu[f"u{bk}"] = ("sVX", f"c2m{j}_{hg}")
                pu[f"sil{bk}"] = ("sVX", f"c2m{j}_{hg}")
            for t in range(4):
                i = 4 * j + t
                b, c = divmod(i, NCH)
                n = f"{i}"
                sl = i % 4
                s2 = i % 2
                p.cur_group = ("D", i)
                x1_f = x1_f2[:, s2, :]
                dif_f = dif_f2[:, s2, :]
                stat = stat2[:, s2, :]
                for hh in range(2):
                    def dmm(e, hh=hh, t=t, j=j):
                        r = None
                        for hg in range(HG):
                            r = e.matmul(pDn[hh][:, :],
                                         m_dm2s[:, j % 2, hg, t * C:(t + 1) * C],
                                         wd_sb[:, hg, hh * 512:(hh + 1) * 512],
                                         start=(hg == 0), stop=(hg == HG - 1))
                        return r
                    p.step("te", dmm,
                           waits=[("sVX", f"c2m{j}_{HG-1}")] + bw(f"d{hh}"),
                           inc=[("sMM", 1)], label=f"c2dn{n}_{hh}", dur=3400)
                p.chain("ve", [
                    lambda e, sl=sl, x1_f=x1_f: e.tensor_tensor(
                        x1_f, x0t[:, sl, :], ax_t[:, sl, :], ALU.add),
                    lambda e, x1_f=x1_f: e.tensor_tensor(
                        x1_f[:, 0:512], x1_f[:, 0:512], pDn[0][:, :], ALU.add),
                    lambda e, x1_f=x1_f: e.tensor_tensor(
                        x1_f[:, 512:1024], x1_f[:, 512:1024], pDn[1][:, :],
                        ALU.add)],
                    waits=[(f"ccx{i % 4}", f"c2xl{n}"), ("sMM", f"c2dn{n}_0"),
                           ("sMM", f"c2dn{n}_1")]
                    + ([(f"cx1{s2}", f"c2xs{i-2}"), ("sEV", f"c2dif{i-2}")]
                       if i >= 2 else []),
                    inc=[("sVX", 1)], label=f"c2x1c{i}")
                pu["d0"] = ("sVX", f"c2x1c{i}")
                pu["d1"] = ("sVX", f"c2x1c{i}")
                s1w = [("sVX", f"c2x1c{i}")]
                if i >= 2:
                    s1w.append(("sVX", f"c2mv{i-2}"))
                p.step("ve", lambda e, stat=stat, x1_f=x1_f: e.reduce_sum(
                    stat[:, 0:1], x1_f, AX.X),
                    waits=s1w, inc=[("sVX", 1)], label=f"c2s1{n}", dur=450)
                p.step("se", lambda e, stat=stat, x1_f=x1_f: e.activation(
                    scrap[:, :], x1_f, AF.Square, accum_out=stat[:, 1:2]),
                    waits=[("sVX", f"c2x1c{i}")] + ([("sVX", f"c2mv{i-2}")]
                                                    if i >= 2 else []),
                    inc=[("sEV", 1)], label=f"c2s2{n}", dur=1150)
                mvw = [("sVX", f"c2s1{n}"), ("sEV", f"c2s2{n}")]
                if i >= 2:
                    mvw.append(("sEV", f"c2dif{i-2}"))
                p.chain("ve", [
                    lambda e, i=i, stat=stat: e.tensor_scalar_mul(
                        ctx.mu2_st[:, i:i + 1], stat[:, 0:1], 1.0 / D),
                    lambda e, i=i, stat=stat: e.tensor_scalar_mul(
                        stat[:, 3:4], ctx.mu2_st[:, i:i + 1], -1.0),
                    lambda e, i=i, stat=stat: e.scalar_tensor_tensor(
                        stat[:, 2:3], ctx.mu2_st[:, i:i + 1], -1.0,
                        ctx.mu2_st[:, i:i + 1], ALU.mult, ALU.mult),
                    lambda e, i=i, stat=stat: e.scalar_tensor_tensor(
                        ctx.var2_st[:, i:i + 1], stat[:, 1:2], 1.0 / D,
                        stat[:, 2:3], ALU.mult, ALU.add)],
                    waits=mvw, inc=[("sVX", 1)], label=f"c2mv{i}")
                dfw = [("sVX", f"c2mv{i}")]
                if i >= 2:
                    dfw.append(("sVX", f"c2dm{i-2}"))
                p.step("se", lambda e, dif_f=dif_f, x1_f=x1_f, stat=stat:
                       e.activation(dif_f, x1_f, AF.Identity, scale=1.0,
                                    bias=stat[:, 3:4]),
                       waits=dfw, inc=[("sEV", 1)], label=f"c2dif{i}", dur=1150)
                p.step("ve", lambda e, i=i, dif_f=dif_f: e.tensor_reduce(
                    ctx.mx_st[:, i:i + 1], dif_f, AX.X, ALU.max,
                    apply_absolute_value=True),
                    waits=[("sEV", f"c2dif{i}")], inc=[("sVX", 1)],
                    label=f"c2dm{i}", dur=500)
                p.dma("se", io["x1_s"][b, c * C:(c + 1) * C, :], x1_f,
                      sem=f"cx1{s2}", waits=[("sVX", f"c2x1c{i}")],
                      label=f"c2xs{i}", xfer=4000)
        p.cur_group = None
        p.quiesce("c2")
        p.emit(reorder=True)




def emit_p2b(ctx):
    nc, io = ctx.nc, ctx.io
    with ExitStack() as st:
        lsb = lambda n, s, d: st.enter_context(nc.sbuf_tensor(n, s, d))
        lps = lambda n, s: st.enter_context(nc.psum_tensor(n, s, F32))
        wg_sb = lsb("wg_sb", [128, G, HID], BF16)
        wu_sb = lsb("wu_sb", [128, G, HID], BF16)
        wd_sb = lsb("wd_sb", [128, HG, D], BF16)

        with ExitStack() as st2:
            wstage = st2.enter_context(nc.sbuf_tensor("wstage2", [128, G, HID], BF16))
            p = Prog(ctx)
            p.mark("bw0")
            for wi, (srct, dst) in enumerate([(io["awg"], wg_sb), (io["awu"], wu_sb)]):
                w = [("sEV", f"bwf{wi-1}")] if wi else []
                p.dma("pl", wstage[:, :, :], srct.rearrange("(g p) d -> p g d", p=128),
                      sem="bw2l", waits=w, label=f"bwl{wi}")

                def fold(e, dst=dst):
                    r = None
                    for g in range(G):
                        r = e.activation(dst[:, g, :], wstage[:, g, :], AF.Copy,
                                         scale=ctx.n2w_sb[:, g:g + 1])
                    return r
                p.step("se", fold, waits=[("bw2l", f"bwl{wi}")], inc=[("sEV", 1)],
                       label=f"bwf{wi}")
            p.dma("pl", wd_sb[:, :, :], io["awd"].rearrange("(g p) d -> p g d", p=128),
                  sem="bw2l", waits=[("sEV", "bwf1")], label="bwd")
            p.quiesce("bw")
            p.emit()

        wo_sb = lsb("wo_sb", [128, G, D], BF16)
        o_tm = lsb("bio", [128, 2, D], BF16)
        x0t = lsb("bix", [128, 4, D], F32)
        o_dm1 = lsb("biod", [128, G, C], BF16)
        ax_f4 = lsb("biax", [128, 4, D], F32)
        bstat2 = lsb("bist", [128, 2, 4], F32)
        h2_1 = lsb("c2h2", [128, D], BF16)
        h2_dm2s = lsb("c2hd", [128, 2, G, 4 * C], BF16)
        sil2 = lsb("c2sl", [128, 2, 512], F32)
        m_dm = lsb("c2md", [128, HG, 4 * C], BF16)
        x1_f2 = lsb("c2x1", [128, 2, D], F32)
        dif_f1 = lsb("c2df", [128, D], F32)
        scrap = lsb("c2sc", [128, D], BF16)
        stat2 = lsb("c2st", [128, 2, 6], F32)
        pGa = [lps("c2g0", [128, 512]), lps("c2g1", [128, 512])]
        pUa = [lps("c2u0", [128, 512]), lps("c2u1", [128, 512])]
        pDn = [lps("c2d0", [128, 512]), lps("c2d1", [128, 512])]
        ptp = [lps("c2t0", [128, 128]), lps("c2t1", [128, 128])]
        ptpb = [t[:, :].bitcast(BF16)[:, 0:128] for t in ptp]

        p = Prog(ctx)
        p.mark("c20")
        p.dma("pl", wo_sb[:, :, :], io["wo"].rearrange("(g p) d -> p g d", p=128),
              sem="wo", label="biwl", xfer=12000)
        pu = {}

        def bw(key):
            return [pu[key]] if key in pu else []

        def bi_chunk(i):
            b, c = divmod(i, NCH)
            n = f"{i}"
            sl = i % 4
            s2 = i % 2
            p.cur_group = ("I", i)
            o_dm = o_dm1[:, :, :]
            ax_f = ax_f4[:, sl, :]
            stat = bstat2[:, s2, :]
            lw = [("sMM", f"biot{i-2}_{G-1}")] if i >= 2 else []
            p.dma("sp", o_tm[:, s2, :], io["o_s"][b, c, :, :], sem=f"bo{i % 2}",
                  waits=lw, label=f"bio{n}", xfer=2500)
            xw = ([("sEV", f"biln{i-4}"), ("sVX", f"c2x1c{i-4}")]
                  if i >= 4 else [])
            p.dma("sp", x0t[:, sl, :], io["x"][b, c * C:(c + 1) * C, :],
                  sem=f"bx{i % 4}", waits=xw, label=f"bix{n}", xfer=4000)
            lnw = [(f"bx{i % 4}", f"bix{n}")]
            if i >= 4:
                lnw += [("sEV", f"bir2{i-4}"), ("sEV", f"c2h2{i-4}"),
                        ("sVX", f"c2x1c{i-4}")]
            p.step("se", lambda e, ax_f=ax_f, sl=sl, i=i: e.activation(
                ax_f, x0t[:, sl, :], AF.Identity,
                scale=ctx.r1_st[:, i:i + 1], bias=ctx.b1_st[:, i:i + 1]),
                waits=lnw, inc=[("sEV", 1)], label=f"biln{n}", dur=1150)
            for g in range(G):
                bk = g % 2
                p.step("te", lambda e, g=g, s2=s2, bk=bk: e.transpose(
                    ptpb[bk], o_tm[:, s2, g * 128:(g + 1) * 128],
                    ctx.ident_bf[:, :]),
                    waits=[(f"bo{i % 2}", f"bio{n}")] + bw(f"tp{bk}"),
                    inc=[("sMM", 1)], label=f"biot{i}_{g}", dur=120)
                ew = [("sMM", f"biot{i}_{g}")]
                if g < 2 and i >= 1:
                    ew.append(("sMM", f"biwo{i-1}_1"))
                sm = evac(p, g, o_dm[:, g, :], ptpb[bk],
                          waits=ew, label=f"biod{n}_{g}")
                pu[f"tp{bk}"] = (sm, f"biod{n}_{g}")
            for hh in range(2):
                def womm(e, hh=hh, o_dm=o_dm):
                    r = None
                    for g in range(G):
                        r = e.matmul(pDn[hh][:, :], o_dm[:, g, :],
                                     wo_sb[:, g, hh * 512:(hh + 1) * 512],
                                     start=(g == 0), stop=(g == G - 1))
                    return r
                p.step("te", womm,
                       waits=[("sVX", f"biod{n}_{G-1}"), ("sEV", f"biod{n}_{G-2}"),
                              ("wo", "biwl")] + bw(f"d{hh}"),
                       inc=[("sMM", 1)], label=f"biwo{n}_{hh}", dur=1800)
            axw = [("sEV", f"biln{n}"), ("sMM", f"biwo{n}_0"),
                   ("sMM", f"biwo{n}_1")]
            p.step("ve", lambda e, ax_f=ax_f: (
                e.tensor_tensor(ax_f[:, 0:512], ax_f[:, 0:512], pDn[0][:, :],
                                ALU.add),
                e.tensor_tensor(ax_f[:, 512:1024], ax_f[:, 512:1024],
                                pDn[1][:, :], ALU.add)),
                waits=axw, inc=[("sVX", 1)], label=f"biax{i}", dur=1100)
            pu["d0"] = ("sVX", f"biax{i}")
            pu["d1"] = ("sVX", f"biax{i}")
            sqw = [("sVX", f"biax{i}")]
            if i >= 2:
                sqw.append(("sVX", f"bir2v{i-2}"))
            p.chain("se", [
                lambda e, ax_f=ax_f, stat=stat: e.activation(
                    scrap[:, :], ax_f, AF.Square, accum_out=stat[:, 0:1]),
                lambda e, stat=stat: e.activation(stat[:, 1:2], stat[:, 0:1],
                                                  AF.Ln, scale=1.0 / D,
                                                  bias=EPS_RMS),
                lambda e, stat=stat: e.activation(stat[:, 1:2], stat[:, 1:2],
                                                  AF.Exp, scale=-0.5)],
                waits=sqw, inc=[("sEV", 1)], label=f"bir2{n}")
            p.step("ve", lambda e, i=i, stat=stat: e.tensor_copy(
                ctx.r2_st[:, i:i + 1], stat[:, 1:2]),
                waits=[("sEV", f"bir2{n}")], inc=[("sVX", 1)],
                label=f"bir2v{i}")

        NSC = NCHUNKS // 4
        for j in range(NSC):
            for t in range(4):
                bi_chunk(4 * j + t)
            for t in range(4):
                i = 4 * j + t
                n = f"{i}"
                sl = i % 4
                p.cur_group = ("M", i)
                h2 = h2_1[:, :]
                hw = [("sVX", f"biax{i}"), ("sVX", f"bir2v{i}")]
                if i >= 1:
                    hw.append(("sMM", f"c2ht{i-1}_{G-1}"))
                p.step("se", lambda e, sl=sl, i=i, h2=h2: e.activation(
                    h2, ax_f4[:, sl, :], AF.Copy, scale=ctx.r2_st[:, i:i + 1]),
                    waits=hw, inc=[("sEV", 1)], label=f"c2h2{i}", dur=1100)
                for g in range(G):
                    bk = g % 2
                    p.step("te", lambda e, g=g, bk=bk, h2=h2: e.transpose(
                        ptpb[bk], h2[:, g * 128:(g + 1) * 128],
                        ctx.ident_bf[:, :]),
                        waits=[("sEV", f"c2h2{i}")] + bw(f"tp{bk}"),
                        inc=[("sMM", 1)], label=f"c2ht{i}_{g}", dur=120)
                    ew = [("sMM", f"c2ht{i}_{g}")]
                    if g < 2 and j >= 2:
                        ew.append(("sMM", f"c2gu{j-2}_{HG-1}"))
                    sm = evac(p, g, h2_dm2s[:, j % 2, g, t * C:(t + 1) * C],
                              ptpb[bk], waits=ew, label=f"c2hd{n}_{g}")
                    pu[f"tp{bk}"] = (sm, f"c2hd{n}_{g}")
            p.cur_group = ("S", j)
            hdw = []
            for t in range(4):
                hdw += [("sVX", f"c2hd{4*j+t}_{G-1}"),
                        ("sEV", f"c2hd{4*j+t}_{G-2}")]
            for hg in range(HG):
                bk = hg % 2

                def gmm(e, hg=hg, bk=bk, j=j):
                    r = None
                    for g in range(G):
                        r = e.matmul(pGa[bk][:, :],
                                     wg_sb[:, g, hg * 128:(hg + 1) * 128],
                                     h2_dm2s[:, j % 2, g, :],
                                     start=(g == 0), stop=(g == G - 1))
                    for g in range(G):
                        r = e.matmul(pUa[bk][:, :],
                                     wu_sb[:, g, hg * 128:(hg + 1) * 128],
                                     h2_dm2s[:, j % 2, g, :],
                                     start=(g == 0), stop=(g == G - 1))
                    return r
                p.step("te", gmm,
                       waits=(hdw if hg == 0 else []) + bw(f"g{bk}")
                       + bw(f"u{bk}"),
                       inc=[("sMM", 1)], label=f"c2gu{j}_{hg}", dur=3400)
                sw = [("sMM", f"c2gu{j}_{hg}")] + bw(f"sil{bk}")
                p.step("se", lambda e, bk=bk: e.activation(
                    sil2[:, bk, :], pGa[bk][:, :], AF.Silu),
                    waits=sw, inc=[("sEV", 1)], label=f"c2si{j}_{hg}", dur=550)
                pu[f"g{bk}"] = ("sEV", f"c2si{j}_{hg}")
                mw = [("sEV", f"c2si{j}_{hg}")]
                if hg == 0 and j >= 1:
                    mw.append(("sMM", f"c2dn{4*j-1}_1"))
                p.step("ve", lambda e, hg=hg, bk=bk: e.tensor_tensor(
                    m_dm[:, hg, :], sil2[:, bk, :], pUa[bk][:, :], ALU.mult),
                    waits=mw, inc=[("sVX", 1)], label=f"c2m{j}_{hg}", dur=550)
                pu[f"u{bk}"] = ("sVX", f"c2m{j}_{hg}")
                pu[f"sil{bk}"] = ("sVX", f"c2m{j}_{hg}")
            for t in range(4):
                i = 4 * j + t
                b, c = divmod(i, NCH)
                n = f"{i}"
                sl = i % 4
                s2 = i % 2
                p.cur_group = ("D", i)
                x1_f = x1_f2[:, s2, :]
                dif_f = dif_f1[:, :]
                stat = stat2[:, s2, :]
                for hh in range(2):
                    def dmm(e, hh=hh, t=t):
                        r = None
                        for hg in range(HG):
                            r = e.matmul(pDn[hh][:, :],
                                         m_dm[:, hg, t * C:(t + 1) * C],
                                         wd_sb[:, hg, hh * 512:(hh + 1) * 512],
                                         start=(hg == 0), stop=(hg == HG - 1))
                        return r
                    p.step("te", dmm,
                           waits=[("sVX", f"c2m{j}_{HG-1}")] + bw(f"d{hh}"),
                           inc=[("sMM", 1)], label=f"c2dn{n}_{hh}", dur=3400)
                p.chain("ve", [
                    lambda e, sl=sl, x1_f=x1_f: e.tensor_tensor(
                        x1_f, x0t[:, sl, :], ax_f4[:, sl, :], ALU.add),
                    lambda e, x1_f=x1_f: e.tensor_tensor(
                        x1_f[:, 0:512], x1_f[:, 0:512], pDn[0][:, :], ALU.add),
                    lambda e, x1_f=x1_f: e.tensor_tensor(
                        x1_f[:, 512:1024], x1_f[:, 512:1024], pDn[1][:, :],
                        ALU.add)],
                    waits=[(f"bx{i % 4}", f"bix{n}"), ("sVX", f"biax{i}"),
                           ("sMM", f"c2dn{n}_0"), ("sMM", f"c2dn{n}_1")]
                    + ([(f"cx1{s2}", f"c2xs{i-2}"), ("sEV", f"c2dif{i-2}")]
                       if i >= 2 else []),
                    inc=[("sVX", 1)], label=f"c2x1c{i}")
                pu["d0"] = ("sVX", f"c2x1c{i}")
                pu["d1"] = ("sVX", f"c2x1c{i}")
                s1w = [("sVX", f"c2x1c{i}")]
                if i >= 2:
                    s1w.append(("sVX", f"c2mv{i-2}"))
                p.step("ve", lambda e, stat=stat, x1_f=x1_f: e.reduce_sum(
                    stat[:, 0:1], x1_f, AX.X),
                    waits=s1w, inc=[("sVX", 1)], label=f"c2s1{n}", dur=450)
                p.step("se", lambda e, stat=stat, x1_f=x1_f: e.activation(
                    scrap[:, :], x1_f, AF.Square, accum_out=stat[:, 1:2]),
                    waits=[("sVX", f"c2x1c{i}")] + ([("sVX", f"c2mv{i-2}")]
                                                    if i >= 2 else []),
                    inc=[("sEV", 1)], label=f"c2s2{n}", dur=1150)
                mvw = [("sVX", f"c2s1{n}"), ("sEV", f"c2s2{n}")]
                if i >= 2:
                    mvw.append(("sEV", f"c2dif{i-2}"))
                p.chain("ve", [
                    lambda e, i=i, stat=stat: e.tensor_scalar_mul(
                        ctx.mu2_st[:, i:i + 1], stat[:, 0:1], 1.0 / D),
                    lambda e, i=i, stat=stat: e.tensor_scalar_mul(
                        stat[:, 3:4], ctx.mu2_st[:, i:i + 1], -1.0),
                    lambda e, i=i, stat=stat: e.scalar_tensor_tensor(
                        stat[:, 2:3], ctx.mu2_st[:, i:i + 1], -1.0,
                        ctx.mu2_st[:, i:i + 1], ALU.mult, ALU.mult),
                    lambda e, i=i, stat=stat: e.scalar_tensor_tensor(
                        ctx.var2_st[:, i:i + 1], stat[:, 1:2], 1.0 / D,
                        stat[:, 2:3], ALU.mult, ALU.add)],
                    waits=mvw, inc=[("sVX", 1)], label=f"c2mv{i}")
                dfw = [("sVX", f"c2mv{i}")]
                if i >= 1:
                    dfw.append(("sVX", f"c2dm{i-1}"))
                p.step("se", lambda e, dif_f=dif_f, x1_f=x1_f, stat=stat:
                       e.activation(dif_f, x1_f, AF.Identity, scale=1.0,
                                    bias=stat[:, 3:4]),
                       waits=dfw, inc=[("sEV", 1)], label=f"c2dif{i}", dur=1150)
                p.step("ve", lambda e, i=i, dif_f=dif_f: e.tensor_reduce(
                    ctx.mx_st[:, i:i + 1], dif_f, AX.X, ALU.max,
                    apply_absolute_value=True),
                    waits=[("sEV", f"c2dif{i}")], inc=[("sVX", 1)],
                    label=f"c2dm{i}", dur=500)
                p.dma("se", io["x1_s"][b, c * C:(c + 1) * C, :], x1_f,
                      sem=f"cx1{s2}", waits=[("sVX", f"c2x1c{i}")],
                      label=f"c2xs{i}", xfer=4000)
        p.cur_group = None
        p.quiesce("c2")
        p.emit(reorder=True)


def emit_p3(ctx):
    nc, io = ctx.nc, ctx.io
    SLICES = [(s * 512, min(II, (s + 1) * 512)) for s in range((II + 511) // 512)]
    with ExitStack() as st:
        lsb = lambda n, s, d: st.enter_context(nc.sbuf_tensor(n, s, d))
        lps = lambda n, s: st.enter_context(nc.psum_tensor(n, s, F32))

        st0 = ExitStack()
        pcol = st0.enter_context(nc.psum_tensor("p3c", [128, 8], F32))
        ptmp = st0.enter_context(nc.psum_tensor("p3pt", [128, 128], F32))
        p = Prog(ctx)
        p.mark("p30")
        p.chain("se", [
            lambda e: e.activation(ctx.r2_st[:, :], ctx.var2_st[:, :], AF.Ln,
                                   bias=EPS_LN),
            lambda e: e.activation(ctx.r2_st[:, :], ctx.r2_st[:, :], AF.Exp,
                                   scale=-0.5)],
            inc=[("sEV", 1)], label="r2p")
        p.chain("ve", [
            lambda e: e.tensor_tensor(ctx.b1_st[:, :], ctx.mx_st[:, :],
                                      ctx.r2_st[:, :], ALU.mult),
            lambda e: e.tensor_reduce(ctx.s1_col[:, :], ctx.b1_st[:, :], AX.X,
                                      ALU.max)],
            waits=[("sEV", "r2p")], inc=[("sVX", 1)], label="lm1")
        p.step("te", lambda e: e.transpose(ptmp[0:1, 0:128], ctx.s1_col[:, :],
                                           ctx.ident[:, :]),
               waits=[("sVX", "lm1")], inc=[("sMM", 1)], label="lmt1")
        p.step("ve", lambda e: e.tensor_reduce(ctx.ccb[0:1, 0:1],
                                               ptmp[0:1, 0:128], AX.X, ALU.max),
               waits=[("sMM", "lmt1")], inc=[("sVX", 1)], label="lmr1")
        p.step("ve", lambda e: e.tensor_copy(ctx.ccb[0:1, :],
                                             ctx.ccb[0:1, 0:1].broadcast_to([1, 128])),
               waits=[("sVX", "lmr1")], inc=[("sVX", 1)], label="ccf1")
        p.dma("sp", io["cc1_in"].ap(), ctx.ccb[0:1, :], sem="cc",
              waits=[("sVX", "ccf1")], label="cci1")
        p.step("pl", lambda e: e.collective_compute(
            "AllReduce", ALU.max, replica_groups=[list(range(N_CORES))],
            ins=[io["cc1_in"].ap().opt()], outs=[io["cc1_out"].ap().opt()]),
            waits=[("cc", "cci1")], inc=[("sCC", 1)], label="ccr1")
        p.dma("sp", ctx.sig_cell[0:1, 5:6], io["cc1_out"].ap()[0:1, 0:1], sem="cc",
              waits=[("sCC", "ccr1")], label="ccl1")
        p.chain("ve", [
            lambda e: e.tensor_tensor(ctx.sig_cell[0:1, 6:7], ctx.sig_cell[0:1, 5:6],
                                      ctx.sig_cell[0:1, 0:1], ALU.mult),
            lambda e: e.tensor_scalar_mul(ctx.sig_cell[0:1, 6:7],
                                          ctx.sig_cell[0:1, 6:7], 1.0 / 127.0),
            lambda e: e.reciprocal(ctx.sig_cell[0:1, 7:8], ctx.sig_cell[0:1, 6:7])],
            waits=[("cc", "ccl1")], inc=[("sVX", 1)], label="s1c")
        p.step("te", lambda e: (
            e.matmul(pcol[:, 0:1], ctx.ones_row[:, :], ctx.sig_cell[0:1, 6:7],
                     start=True, stop=True),
            e.matmul(pcol[:, 1:2], ctx.ones_row[:, :], ctx.sig_cell[0:1, 7:8],
                     start=True, stop=True)),
            waits=[("sVX", "s1c")], inc=[("sMM", 1)], label="s1b")
        p.step("ve", lambda e: (
            e.tensor_copy(ctx.s1_col[:, :], pcol[:, 0:1]),
            e.tensor_copy(ctx.is1_col[:, :], pcol[:, 1:2])),
            waits=[("sMM", "s1b")], inc=[("sVX", 1)], label="s1e")
        p.step("ve", lambda e: e.tensor_scalar_mul(ctx.qs_st[:, :], ctx.r2_st[:, :],
                                                   ctx.is1_col[:, :]),
               waits=[("sVX", "s1e")], inc=[("sVX", 1)], label="qsc")
        p.chain("ve", [
            lambda e: e.tensor_tensor(ctx.qb_st[:, :], ctx.mu2_st[:, :],
                                      ctx.qs_st[:, :], ALU.mult),
            lambda e: e.tensor_scalar_mul(ctx.qb_st[:, :], ctx.qb_st[:, :], -1.0)],
            waits=[("sVX", "qsc")], inc=[("sVX", 1)], label="qbc")
        p.quiesce("p3p")
        p.emit()
        st0.close()

        with ExitStack() as st2:
            l2 = lambda n, s, d: st2.enter_context(nc.sbuf_tensor(n, s, d))
            lps2 = lambda n, s: st2.enter_context(nc.psum_tensor(n, s, F32))
            wgT = l2("wgT", [128, G, II], BF16)
            wuT = l2("wuT", [128, G, II], BF16)
            x1t = l2("p3x1", [128, 4, D], F32)
            u_f2 = l2("p3u", [128, 2, D], F32)
            z16_2 = l2("p3z", [128, 2, D], F16)
            xq2 = l2("p3xq", [128, 2, D], BF16)
            xq_dm2s = l2("p3xd", [128, 2, G, 4 * C], BF16)
            sil2 = l2("p3sl", [128, 2, 512], F32)
            t_scr = l2("p3tf", [128, 2, 512], F32)
            t_bf2s = l2("p3tb", [128, 2, IG, 4 * C], BF16)
            mxc = l2("p3mx", [128, IG], F32)
            pGa = [lps2("p3g0", [128, 512]), lps2("p3g1", [128, 512])]
            pUa = [lps2("p3u0", [128, 512]), lps2("p3u1", [128, 512])]
            ptp = [lps2(f"p3t{j}", [128, 128]) for j in range(4)]
            ptpb = [t[:, :].bitcast(BF16)[:, 0:128] for t in ptp]
            NS = len(SLICES)

            p = Prog(ctx)
            p.mark("p3a0")
            for g in range(G):
                p.dma("sp", wgT[:, g, :], io["wgq_s"][:, g * 128:(g + 1) * 128],
                      sem="wgt", transpose=True, label=f"wgT{g}", xfer=6000)
                p.dma("sp", wuT[:, g, :], io["wuq_s"][:, g * 128:(g + 1) * 128],
                      sem="wgt", transpose=True, label=f"wuT{g}", xfer=6000)
            pu = {}

            def bw(key):
                return [pu[key]] if key in pu else []

            NSC3 = NCHUNKS // 4
            for j in range(NSC3):
                for t in range(4):
                    i = 4 * j + t
                    b, c = divmod(i, NCH)
                    n = f"{i}"
                    sl = i % 4
                    s2 = i % 2
                    p.cur_group = ("A", i)
                    u_f = u_f2[:, s2, :]
                    z16 = z16_2[:, s2, :]
                    xq = xq2[:, s2, :]
                    lw = [("sEV", f"p3uf{i-4}")] if i >= 4 else []
                    p.dma("sp", x1t[:, sl, :], io["x1_s"][b, c * C:(c + 1) * C, :],
                          sem=f"px{i % 4}", waits=lw, label=f"p3xl{n}", xfer=4000)
                    ufw = [(f"px{i % 4}", f"p3xl{n}")]
                    if i >= 2:
                        ufw += [("sVX", f"p3cl{i-2}"), ("sEV", f"p3z{i-2}")]
                    p.step("se", lambda e, sl=sl, i=i, u_f=u_f: e.activation(
                        u_f, x1t[:, sl, :], AF.Identity,
                        scale=ctx.qs_st[:, i:i + 1], bias=ctx.qb_st[:, i:i + 1]),
                        waits=ufw, inc=[("sEV", 1)], label=f"p3uf{i}", dur=1150)
                    p.step("ve", lambda e, u_f=u_f: e.tensor_scalar(
                        u_f, u_f, 127.0, -128.0, ALU.min, ALU.max),
                        waits=[("sEV", f"p3uf{i}")], inc=[("sVX", 1)],
                        label=f"p3cl{i}", dur=600)
                    zw = [("sVX", f"p3cl{i}")]
                    if i >= 2:
                        zw.append(("sVX", f"p3xq{i-2}"))
                    p.step("se", lambda e, z16=z16, u_f=u_f: e.activation(
                        z16, u_f, AF.Copy, bias=RBIAS),
                        waits=zw, inc=[("sEV", 1)], label=f"p3z{i}", dur=1150)
                    xqw = [("sEV", f"p3z{i}")]
                    if i >= 2:
                        xqw.append(("sMM", f"p3xt{i-2}_{G-1}"))
                    p.step("ve", lambda e, xq=xq, z16=z16: e.tensor_scalar(
                        xq, z16, -RBIAS, ctx.s1_col[:, :], ALU.add, ALU.mult),
                        waits=xqw, inc=[("sVX", 1)], label=f"p3xq{i}", dur=600)
                    if ctx.debug:
                        p.dma("sp", io["xq_dbg"][b, c, :, :], xq, sem="dbg",
                              waits=[("sVX", f"p3xq{i}")], label=f"p3xqd{i}")
                    for g in range(G):
                        bk = g % 4
                        p.step("te", lambda e, g=g, bk=bk, xq=xq: e.transpose(
                            ptpb[bk], xq[:, g * 128:(g + 1) * 128],
                            ctx.ident_bf[:, :]),
                            waits=[("sVX", f"p3xq{i}")] + bw(f"tp{bk}"),
                            inc=[("sMM", 1)], label=f"p3xt{i}_{g}", dur=120)
                        ew = [("sMM", f"p3xt{i}_{g}")]
                        if g < 2 and j >= 2:
                            ew.append(("sMM", f"p3gu{j-2}_{IG-1}"))
                        sm = evac(p, g, xq_dm2s[:, j % 2, g, t * C:(t + 1) * C],
                                  ptpb[bk], waits=ew, label=f"p3xd{n}_{g}")
                        pu[f"tp{bk}"] = (sm, f"p3xd{n}_{g}")
                p.cur_group = ("T", j)
                xdw = [("wgt", f"wuT{G-1}")]
                for t in range(4):
                    xdw += [("sVX", f"p3xd{4*j+t}_{G-1}"),
                            ("sEV", f"p3xd{4*j+t}_{G-2}")]
                for blk in range(IG):
                    bk = blk % 2

                    def gmm(e, blk=blk, bk=bk, j=j):
                        r = None
                        for g in range(G):
                            r = e.matmul(pGa[bk][:, :],
                                         wgT[:, g, blk * 128:(blk + 1) * 128],
                                         xq_dm2s[:, j % 2, g, :],
                                         start=(g == 0), stop=(g == G - 1))
                        for g in range(G):
                            r = e.matmul(pUa[bk][:, :],
                                         wuT[:, g, blk * 128:(blk + 1) * 128],
                                         xq_dm2s[:, j % 2, g, :],
                                         start=(g == 0), stop=(g == G - 1))
                        return r
                    p.step("te", gmm,
                           waits=(xdw if blk == 0 else []) + bw(f"g{bk}")
                           + bw(f"u{bk}"),
                           inc=[("sMM", 1)], label=f"p3gu{j}_{blk}", dur=3400)
                    sw = [("sMM", f"p3gu{j}_{blk}")] + bw(f"sil{bk}")
                    p.step("se", lambda e, bk=bk: e.activation(
                        sil2[:, bk, :], pGa[bk][:, :], AF.Silu),
                        waits=sw, inc=[("sEV", 1)], label=f"p3si{j}_{blk}",
                        dur=550)
                    pu[f"g{bk}"] = ("sEV", f"p3si{j}_{blk}")
                    mw = [("sEV", f"p3si{j}_{blk}")] + bw(f"tsc{bk}")
                    if blk < 2 and j >= 1:
                        mw.append(("sVX", f"p3rx{j-1}_{IG-1}"))
                    p.step("ve", lambda e, bk=bk: e.tensor_tensor(
                        t_scr[:, bk, :], sil2[:, bk, :], pUa[bk][:, :], ALU.mult),
                        waits=mw, inc=[("sVX", 1)], label=f"p3m{j}_{blk}",
                        dur=550)
                    pu[f"u{bk}"] = ("sVX", f"p3m{j}_{blk}")
                    pu[f"sil{bk}"] = ("sVX", f"p3m{j}_{blk}")
                    cw = [("sVX", f"p3m{j}_{blk}")]
                    if blk < 2 and j >= 2:
                        cw.append((f"pts{j % 2}", f"p3ts{j-2}"))
                    p.step("se", lambda e, blk=blk, bk=bk, j=j: e.activation(
                        t_bf2s[:, j % 2, blk, :], t_scr[:, bk, :], AF.Copy),
                        waits=cw, inc=[("sEV", 1)], label=f"p3tc{j}_{blk}",
                        dur=550)
                    pu[f"tsc{bk}"] = ("sEV", f"p3tc{j}_{blk}")
                    rxw = [("sVX", f"p3m{j}_{blk}")]
                    if blk == 0 and j >= 1:
                        rxw.append(("sVX", f"p3rm{j-1}"))
                    p.step("ve", lambda e, blk=blk, bk=bk: e.tensor_reduce(
                        mxc[:, blk:blk + 1], t_scr[:, bk, :], AX.X, ALU.max,
                        apply_absolute_value=True),
                        waits=rxw, inc=[("sVX", 1)], label=f"p3rx{j}_{blk}",
                        dur=300)
                p.step("ve", lambda e, j=j: e.tensor_reduce(
                    ctx.mn_st[:, j:j + 1], mxc[:, :], AX.X, ALU.max),
                    waits=[("sVX", f"p3rx{j}_{IG-1}")], inc=[("sVX", 1)],
                    label=f"p3rm{j}")
                p.dma("se", io["t_s"][j], t_bf2s[:, j % 2, :, :],
                      sem=f"pts{j % 2}",
                      waits=[("sEV", f"p3tc{j}_{IG-1}")], label=f"p3ts{j}",
                      xfer=15000)
            p.cur_group = None
            p.step("ve", lambda e: e.tensor_reduce(
                ctx.s2_col[:, :], ctx.mn_st[:, 0:NCHUNKS // 4], AX.X, ALU.max),
                waits=[("sVX", f"p3rm{j}") for j in range(NCHUNKS // 4)],
                inc=[("sVX", 1)], label="lm2")
            p.step("te", lambda e: e.transpose(ptp[0][0:1, 0:128],
                                               ctx.s2_col[:, :], ctx.ident[:, :]),
                   waits=[("sVX", "lm2")] + bw("tp0"), inc=[("sMM", 1)],
                   label="lmt2")
            p.step("ve", lambda e: e.tensor_reduce(ctx.ccb[0:1, 0:1],
                                                   ptp[0][0:1, 0:128], AX.X,
                                                   ALU.max),
                   waits=[("sMM", "lmt2")], inc=[("sVX", 1)], label="lmr2")
            p.step("ve", lambda e: e.tensor_copy(
                ctx.ccb[0:1, :], ctx.ccb[0:1, 0:1].broadcast_to([1, 128])),
                waits=[("sVX", "lmr2")], inc=[("sVX", 1)], label="ccf2")
            p.dma("sp", io["cc2_in"].ap(), ctx.ccb[0:1, :], sem="cc",
                  waits=[("sVX", "ccf2")], label="cci2")
            p.step("pl", lambda e: e.collective_compute(
                "AllReduce", ALU.max, replica_groups=[list(range(N_CORES))],
                ins=[io["cc2_in"].ap().opt()], outs=[io["cc2_out"].ap().opt()]),
                waits=[("cc", "cci2")], inc=[("sCC", 1)], label="ccr2")
            p.dma("sp", ctx.sig_cell[0:1, 5:6], io["cc2_out"].ap()[0:1, 0:1],
                  sem="cc", waits=[("sCC", "ccr2")], label="ccl2")
            p.chain("ve", [
                lambda e: e.tensor_tensor(ctx.sig_cell[0:1, 6:7],
                                          ctx.sig_cell[0:1, 5:6],
                                          ctx.sig_cell[0:1, 1:2], ALU.mult),
                lambda e: e.tensor_scalar_mul(ctx.sig_cell[0:1, 6:7],
                                              ctx.sig_cell[0:1, 6:7], 1.0 / 127.0),
                lambda e: e.reciprocal(ctx.sig_cell[0:1, 7:8],
                                       ctx.sig_cell[0:1, 6:7])],
                waits=[("cc", "ccl2")], inc=[("sVX", 1)], label="s2c")
            p.step("te", lambda e: (
                e.matmul(pGa[0][:, 0:1], ctx.ones_row[:, :], ctx.sig_cell[0:1, 6:7],
                         start=True, stop=True),
                e.matmul(pGa[1][:, 0:1], ctx.ones_row[:, :], ctx.sig_cell[0:1, 7:8],
                         start=True, stop=True)),
                waits=[("sVX", "s2c")] + bw("g0") + bw("g1"),
                inc=[("sMM", 1)], label="s2b")
            p.step("ve", lambda e: (
                e.tensor_copy(ctx.s2_col[:, :], pGa[0][:, 0:1]),
                e.tensor_copy(ctx.is2_col[:, :], pGa[1][:, 0:1])),
                waits=[("sMM", "s2b")], inc=[("sVX", 1)], label="s2e")
            p.quiesce("p3a")
            p.emit(reorder=True)

        with ExitStack() as st3:
            l3 = lambda n, s, d: st3.enter_context(nc.sbuf_tensor(n, s, d))
            lps3 = lambda n, s: st3.enter_context(nc.psum_tensor(n, s, F32))
            wdT = l3("wdT", [128, IG, D], BF16)
            t_ld = l3("q3t", [128, 4, IG, C], BF16)
            x1t = l3("q3x", [128, 4, D], F32)
            u2_2 = l3("q3u", [128, 2, IG, C], F32)
            z16b2 = l3("q3z", [128, 2, IG, C], F16)
            hq2 = l3("q3h", [128, 2, IG, C], BF16)
            y_f2 = l3("q3y", [128, 2, D], F32)
            pDn = [lps3("q3d0", [128, 512]), lps3("q3d1", [128, 512])]

            p = Prog(ctx)
            p.mark("p3b0")
            for ig in range(IG):
                p.dma("sp", wdT[:, ig, :], io["wdq_s"][:, ig * 128:(ig + 1) * 128],
                      sem="wdt", transpose=True, label=f"wdT{ig}", xfer=6000)
            pu = {}

            def bw(key):
                return [pu[key]] if key in pu else []

            if ctx.debug:
                p.step("ve", lambda e: (
                    e.tensor_copy(y_f2[:, 0, 0:1], ctx.s1_col[:, :]),
                    e.tensor_copy(y_f2[:, 0, 1:2], ctx.is1_col[:, :]),
                    e.tensor_copy(y_f2[:, 0, 2:3], ctx.s2_col[:, :]),
                    e.tensor_copy(y_f2[:, 0, 3:4], ctx.is2_col[:, :])),
                    inc=[("sVX", 1)], label="cdbg")
                p.dma("sp", io["col_dbg"], y_f2[:, 0, 0:4], sem="cdb",
                      waits=[("sVX", "cdbg")], label="cdbgd")

            for i in range(NCHUNKS):
                b, c = divmod(i, NCH)
                j4, t4 = divmod(i, 4)
                n = f"{i}"
                sl = i % 4
                s2 = i % 2
                p.cur_group = ("B", i)
                u2 = u2_2[:, s2, :, :]
                z16b = z16b2[:, s2, :, :]
                hq = hq2[:, s2, :, :]
                y_f = y_f2[:, s2, :]
                lw = [("sEV", f"q3u{i-4}")] if i >= 4 else []
                if i == 0 and ctx.debug:
                    lw = lw + [("cdb", "cdbgd")]
                p.dma("sp", t_ld[:, sl, :, :],
                      io["t_s"][j4, :, :, t4 * C:(t4 + 1) * C], sem=f"qt{i % 4}",
                      waits=lw, label=f"q3tl{n}", xfer=6000)
                xw = [("sVX", f"q3y{i-4}")] if i >= 4 else []
                p.dma("sp", x1t[:, sl, :], io["x1_s"][b, c * C:(c + 1) * C, :],
                      sem=f"qx{i % 4}", waits=xw, label=f"q3xl{n}", xfer=4000)
                uw = [(f"qt{i % 4}", f"q3tl{n}")]
                if i >= 2:
                    uw += [("sVX", f"q3cl{i-2}"), ("sEV", f"q3z{i-2}")]
                p.step("se", lambda e, sl=sl, u2=u2: e.activation(
                    u2, t_ld[:, sl, :, :], AF.Copy, scale=ctx.is2_col[:, :]),
                    waits=uw, inc=[("sEV", 1)], label=f"q3u{i}", dur=2300)
                p.step("ve", lambda e, u2=u2: e.tensor_scalar(
                    u2, u2, 127.0, -128.0, ALU.min, ALU.max),
                    waits=[("sEV", f"q3u{i}")], inc=[("sVX", 1)],
                    label=f"q3cl{i}", dur=1500)
                zw = [("sVX", f"q3cl{i}")]
                if i >= 2:
                    zw.append(("sVX", f"q3hq{i-2}"))
                p.step("se", lambda e, z16b=z16b, u2=u2: e.activation(
                    z16b, u2, AF.Copy, bias=RBIAS),
                    waits=zw, inc=[("sEV", 1)], label=f"q3z{i}", dur=2400)
                hqw = [("sEV", f"q3z{i}")]
                if i >= 2:
                    hqw.append(("sMM", f"q3dn{i-2}_1"))
                p.step("ve", lambda e, hq=hq, z16b=z16b: e.tensor_scalar(
                    hq, z16b, -RBIAS, ctx.s2_col[:, :], ALU.add, ALU.mult),
                    waits=hqw, inc=[("sVX", 1)], label=f"q3hq{i}", dur=1300)
                if ctx.debug:
                    p.dma("sp", io["hq_dbg"][b, c, :, :], hq, sem="dbg",
                          waits=[("sVX", f"q3hq{i}")], label=f"q3hqd{i}")
                for hh in range(2):
                    def dmm(e, hh=hh, hq=hq):
                        r = None
                        for ig in range(IG):
                            r = e.matmul(pDn[hh][:, :], hq[:, ig, :],
                                         wdT[:, ig, hh * 512:(hh + 1) * 512],
                                         start=(ig == 0), stop=(ig == IG - 1))
                        return r
                    p.step("te", dmm,
                           waits=[("sVX", f"q3hq{i}"), ("wdt", f"wdT{IG-1}")]
                           + bw(f"d{hh}"),
                           inc=[("sMM", 1)], label=f"q3dn{n}_{hh}", dur=4700)
                yw = [(f"qx{i % 4}", f"q3xl{n}"), ("sMM", f"q3dn{n}_0"),
                      ("sMM", f"q3dn{n}_1")]
                if i >= 2:
                    yw.append((f"qy{s2}", f"q3ys{i-2}"))
                p.chain("ve", [
                    lambda e, sl=sl, y_f=y_f: e.tensor_tensor(y_f[:, 0:512],
                                                              x1t[:, sl, 0:512],
                                                              pDn[0][:, :],
                                                              ALU.add),
                    lambda e, sl=sl, y_f=y_f: e.tensor_tensor(y_f[:, 512:1024],
                                                              x1t[:, sl, 512:1024],
                                                              pDn[1][:, :],
                                                              ALU.add)],
                    waits=yw, inc=[("sVX", 1)], label=f"q3y{i}")
                pu["d0"] = ("sVX", f"q3y{i}")
                pu["d1"] = ("sVX", f"q3y{i}")
                p.dma("sp", io["y"][b, c * C:(c + 1) * C, :], y_f,
                      sem=f"qy{i % 2}", waits=[("sVX", f"q3y{i}")],
                      label=f"q3ys{i}", xfer=4000)
            p.cur_group = None
            p.quiesce("p3b")
            p.emit(reorder=True)



def emit_debug_dump(ctx):
    io = ctx.io
    p = Prog(ctx)
    p.mark("d0")
    for j, s in enumerate([ctx.r1_st, ctx.b1_st, ctx.beta_st, ctx.betan_st,
                           ctx.r2_st, ctx.mu2_st, ctx.var2_st, ctx.mx_st,
                           ctx.mn_st, ctx.qs_st]):
        p.dma("sp", io["st_dbg"][:, j, :], s[:, :], sem="dbg")
    p.quiesce("dbg")
    p.emit()


def emit_final(ctx):
    pass


_NC_CACHE = {}


def kernel(**inputs):
    if "full" not in _NC_CACHE:
        _NC_CACHE["full"] = build()
    nc = _NC_CACHE["full"]
    x = np.asarray(inputs["L0_attn_x"])
    in_maps = []
    for cid in range(N_CORES):
        m = {k: np.ascontiguousarray(np.asarray(v)) for k, v in inputs.items()
             if k != "L0_attn_x"}
        m["L0_attn_x"] = np.ascontiguousarray(x[cid * NB:(cid + 1) * NB])
        in_maps.append(m)
    res = run_bass_kernel_spmd(nc, in_maps, core_ids=list(range(N_CORES)))
    return np.concatenate([np.asarray(r["y"]) for r in res.results], axis=0)


# version 28 (final)
# speedup vs baseline: 1.0637x; 1.0637x over previous
import numpy as np
from contextlib import ExitStack

import concourse.bass as bass
import concourse.mybir as mybir
from concourse.bass_utils import run_bass_kernel_spmd

F32 = mybir.dt.float32
BF16 = mybir.dt.bfloat16
F16 = mybir.dt.float16
AF = mybir.ActivationFunctionType
ALU = mybir.AluOpType
AX = mybir.AxisListType

N_CORES = 8
B_TOT, T, D = 16, 2048, 1024
NB = B_TOT // N_CORES
C = 128
NCH = T // C
G = D // 128
HID = 2048
HG = HID // 128
II = 2816
IG = II // 128
EPS_LN = 1e-5
EPS_RMS = 1e-6
RBIAS = 1536.0
NCHUNKS = NB * NCH

DMA_SEM = {"sp": "dS", "se": "dA", "pl": "dP"}


class Prog:

    def __init__(self, ctx):
        self.ctx = ctx
        self.steps = []
        self.labels = {}
        self.counts = dict(ctx.semvals)
        self.counts0 = dict(ctx.semvals)
        self.dma_sems = {}
        self.marks = {}
        self.cur_group = None

    def _snap(self, label):
        assert label not in self.labels, f"dup label {label}"
        self.labels[label] = dict(self.counts)

    def _chk(self, waits):
        for (s, lab) in waits:
            assert lab in self.labels, f"wait on unknown/future label {lab}"

    def step(self, eng, fn, waits=(), inc=(), label=None, dur=250, dma=False):
        self._chk(waits)
        self.steps.append(dict(eng=eng, fn=fn, waits=tuple(waits),
                               incs=tuple(inc), label=label, dur=dur, dma=dma,
                               group=self.cur_group))
        for (s, k) in inc:
            self.counts[s] = self.counts.get(s, self.ctx.semvals.get(s, 0)) + k
        if label:
            self._snap(label)

    def dma(self, queue, out, in_, sem=None, waits=(), label=None, transpose=False,
            slow=False, xfer=2500):
        assert sem is not None, "dma needs an explicit stream sem"
        self.ctx.get_sem(sem)
        self.dma_sems[sem] = queue

        def fn(e, out=out, in_=in_, tr=transpose, slow=slow):
            if tr:
                return e.dma_start(out=out, in_=in_, transpose=True)
            if slow:
                return e.dma_start(out=out, in_=in_, allow_slow_non_contiguous=True)
            return e.dma_start(out=out, in_=in_)
        self.step(queue, fn, waits=waits, inc=[(sem, 16)], label=label,
                  dur=150, dma=xfer)

    def quiesce(self, tag):
        self.mark(f"_qz_{tag}")
        per_q = {}
        for sem, q in self.dma_sems.items():
            per_q.setdefault(q, []).append(sem)
        for q, sems in per_q.items():
            self.step(q, lambda e: None,
                      waits=[(s, f"_qz_{tag}") for s in sems])
        self.dma_sems = {}

    _chain_ctr = [0]

    def chain(self, eng, fns, waits=(), inc=(), label=None):
        n = len(fns)
        prev = None
        for j, fn in enumerate(fns):
            w = list(waits) if j == 0 else [("sCH", prev)]
            if j == n - 1:
                self.step(eng, fn, waits=w, inc=inc, label=label)
            else:
                Prog._chain_ctr[0] += 1
                lb = f"_ch{Prog._chain_ctr[0]}"
                self.step(eng, fn, waits=w, inc=[("sCH", 1)], label=lb)
                prev = lb

    def mark(self, label):
        self._snap(label)
        self.marks[label] = len(self.steps)

    def at(self, sem, label):
        return self.labels[label][sem]

    def _schedule(self):
        n = len(self.steps)
        prod = {}
        for j, st in enumerate(self.steps):
            if st["label"] is not None:
                prod[st["label"]] = j
        preds = [[] for _ in range(n)]
        succs = [[] for _ in range(n)]

        def edge(a, b, lat):
            preds[b].append((a, lat))
            succs[a].append(b)

        inc_idx = {}
        for j, st in enumerate(self.steps):
            for (s, _) in st["incs"]:
                inc_idx.setdefault(s, []).append(j)
        for j, st in enumerate(self.steps):
            for (s, lab) in st["waits"]:
                if lab in prod:
                    a = prod[lab]
                    pa = self.steps[a]
                    assert any(ss == s for (ss, _) in pa["incs"]), \
                        f"reorder: wait ({s},{lab}) not produced at label"
                    lat = pa["dma"] if pa["dma"] else \
                        (50 if pa["eng"] == st["eng"] else 400)
                    edge(a, j, lat)
                else:
                    mp = self.marks[lab]
                    for a in inc_idx.get(s, []):
                        if a < mp:
                            pa = self.steps[a]
                            edge(a, j, pa["dma"] if pa["dma"] else 400)
        last = {}
        for j, st in enumerate(self.steps):
            key = (st["group"], st["eng"])
            if key in last:
                edge(last[key], j, 0)
            last[key] = j
        lastd = {}
        for j, st in enumerate(self.steps):
            if st["dma"]:
                sm = st["incs"][0][0]
                if sm in lastd:
                    edge(lastd[sm], j, 0)
                lastd[sm] = j
        prio = [0.0] * n
        for j in range(n - 1, -1, -1):
            m = 0.0
            for b in succs[j]:
                if prio[b] > m:
                    m = prio[b]
            prio[j] = m + self.steps[j]["dur"]
        indeg = [len(preds[j]) for j in range(n)]
        est = [0.0] * n
        avail = [j for j in range(n) if indeg[j] == 0]
        fin = [0.0] * n
        R = {}
        order = []
        for _ in range(n):
            best = None
            bstart = None
            for j in avail:
                stt = max(R.get(self.steps[j]["eng"], 0.0), est[j])
                if bstart is None or stt < bstart:
                    bstart = stt
            for j in avail:
                stt = max(R.get(self.steps[j]["eng"], 0.0), est[j])
                if stt <= bstart + 200.0:
                    if best is None or prio[j] > prio[best]:
                        best = j
            j = best
            avail.remove(j)
            stt = max(R.get(self.steps[j]["eng"], 0.0), est[j])
            fin[j] = stt + self.steps[j]["dur"]
            R[self.steps[j]["eng"]] = fin[j]
            order.append(j)
            done = set()
            for b in succs[j]:
                if b in done:
                    continue
                done.add(b)
                ndec = 0
                for (a, lat) in preds[b]:
                    if a == j:
                        if fin[j] + lat > est[b]:
                            est[b] = fin[j] + lat
                        ndec += 1
                indeg[b] -= ndec
                if indeg[b] == 0:
                    avail.append(b)
        assert len(order) == n, "scheduler failed to order all steps"
        return order

    def emit(self, reorder=False):
        ctx = self.ctx
        steps = self.steps
        labels = self.labels
        if reorder:
            order = self._schedule()
            steps = [self.steps[j] for j in order]
            counts = dict(self.counts0)
            labels = {}
            for st in steps:
                for (s, k) in st["incs"]:
                    counts[s] = counts.get(s, 0) + k
                if st["label"]:
                    labels[st["label"]] = dict(counts)
            for lab, mp in self.marks.items():
                labels[lab] = dict(self.counts0) if mp == 0 else dict(counts)
        with ctx.nc.Block() as blk:
            def make(eng_name):
                def body(e):
                    seen = {}
                    for st in steps:
                        if st["eng"] != eng_name:
                            continue
                        need = {}
                        for (s, lab) in st["waits"]:
                            v = labels[lab][s]
                            if v > need.get(s, -1):
                                need[s] = v
                        for s, v in need.items():
                            if v > seen.get(s, -1):
                                e.wait_ge(ctx.sems[s], v)
                                seen[s] = v
                        r = st["fn"](e)
                        if st["incs"]:
                            last = r[-1] if isinstance(r, tuple) else r
                            assert last is not None, "step with incs emitted no op"
                            for (s, k) in st["incs"]:
                                last.then_inc(ctx.sems[s], k)
                return body
            blk.sync(make("sp"))
            blk.tensor(make("te"))
            blk.vector(make("ve"))
            blk.scalar(make("se"))
            blk.gpsimd(make("pl"))
        for s in self.counts:
            ctx.semvals[s] = self.counts[s]



def evac(p, par, out, in_, waits=(), label=None, scale=None):
    if par % 2 == 0:
        if scale is None:
            fn = lambda e: e.activation(out, in_, AF.Copy)
        else:
            fn = lambda e: e.activation(out, in_, AF.Copy, scale=scale)
        p.step("se", fn, waits=waits, inc=[("sEV", 1)], label=label)
        return "sEV"
    if scale is None:
        fn = lambda e: e.tensor_copy(out, in_)
    else:
        fn = lambda e: e.tensor_scalar_mul(out, in_, scale)
    p.step("ve", fn, waits=waits, inc=[("sVX", 1)], label=label)
    return "sVX"


class Ctx:
    pass


def declare_io(nc, debug):
    i = {}
    i["x"] = nc.dram_tensor("L0_attn_x", [NB, T, D], F32, kind="ExternalInput").ap()
    i["n1w"] = nc.dram_tensor("L0_attn_norm1_w", [D], F32, kind="ExternalInput").ap()
    i["wq"] = nc.dram_tensor("L0_attn_Wq", [D, D], F32, kind="ExternalInput").ap()
    i["wk"] = nc.dram_tensor("L0_attn_Wk", [D, D], F32, kind="ExternalInput").ap()
    i["wv"] = nc.dram_tensor("L0_attn_Wv", [D, D], F32, kind="ExternalInput").ap()
    i["bw"] = nc.dram_tensor("L0_attn_beta_w", [D, 1], F32, kind="ExternalInput").ap()
    i["bb"] = nc.dram_tensor("L0_attn_beta_b", [1], F32, kind="ExternalInput").ap()
    i["wo"] = nc.dram_tensor("L0_attn_Wo", [D, D], F32, kind="ExternalInput").ap()
    i["al"] = nc.dram_tensor("L0_attn_alpha_logit", [D], F32, kind="ExternalInput").ap()
    i["n2w"] = nc.dram_tensor("L0_attn_norm2_w", [D], F32, kind="ExternalInput").ap()
    i["awg"] = nc.dram_tensor("L0_attn_w_gate", [D, HID], F32, kind="ExternalInput").ap()
    i["awu"] = nc.dram_tensor("L0_attn_w_up", [D, HID], F32, kind="ExternalInput").ap()
    i["awd"] = nc.dram_tensor("L0_attn_w_down", [HID, D], F32, kind="ExternalInput").ap()
    i["cx"] = nc.dram_tensor("L0_ffn_clip_x", [1], F32, kind="ExternalInput").ap()
    i["cd"] = nc.dram_tensor("L0_ffn_clip_down", [1], F32, kind="ExternalInput").ap()
    i["fwg"] = nc.dram_tensor("L0_ffn_w_gate", [II, D], F32, kind="ExternalInput").ap()
    i["fwu"] = nc.dram_tensor("L0_ffn_w_up", [II, D], F32, kind="ExternalInput").ap()
    i["fwd"] = nc.dram_tensor("L0_ffn_w_down", [D, II], F32, kind="ExternalInput").ap()
    i["y"] = nc.dram_tensor("y", [NB, T, D], F32, kind="ExternalOutput").ap()
    dk = "ExternalOutput" if debug else "Internal"
    i["q_s"] = nc.dram_tensor("q_s", [NB, NCH, C, D], BF16, kind=dk).ap()
    i["k_s"] = nc.dram_tensor("k_s", [NB, NCH, C, D], BF16, kind=dk).ap()
    i["v_s"] = nc.dram_tensor("v_s", [NB, NCH, C, D], BF16, kind=dk).ap()
    i["o_s"] = nc.dram_tensor("o_s", [NB, NCH, C, D], BF16, kind=dk).ap()
    i["ax_s"] = nc.dram_tensor("ax_s", [NB, T, D], F32, kind=dk).ap()
    i["x1_s"] = nc.dram_tensor("x1_s", [NB, T, D], F32, kind=dk).ap()
    i["t_s"] = nc.dram_tensor("t_s", [NCHUNKS // 4, 128, IG, 4 * C], BF16,
                              kind=dk).ap()
    i["wgq_s"] = nc.dram_tensor("wgq_s", [II, D], BF16, kind=dk).ap()
    i["wuq_s"] = nc.dram_tensor("wuq_s", [II, D], BF16, kind=dk).ap()
    i["wdq_s"] = nc.dram_tensor("wdq_s", [D, II], BF16, kind=dk).ap()
    if debug:
        i["st_dbg"] = nc.dram_tensor("st_dbg", [128, 10, NCHUNKS], F32,
                                     kind="ExternalOutput").ap()
        i["hq_dbg"] = nc.dram_tensor("hq_dbg", [NB, NCH, C, II], BF16,
                                     kind="ExternalOutput").ap()
        i["xq_dbg"] = nc.dram_tensor("xq_dbg", [NB, NCH, C, D], BF16,
                                     kind="ExternalOutput").ap()
        i["col_dbg"] = nc.dram_tensor("col_dbg", [128, 4], F32,
                                      kind="ExternalOutput").ap()
    i["cc1_in"] = nc.dram_tensor("cc1_in", [1, 128], F32)
    i["cc1_out"] = nc.dram_tensor("cc1_out", [1, 128], F32)
    i["cc2_in"] = nc.dram_tensor("cc2_in", [1, 128], F32)
    i["cc2_out"] = nc.dram_tensor("cc2_out", [1, 128], F32)
    return i


def build(debug=False, phases=("p1", "w", "p2a", "p2bi", "p2bii", "p3")):
    nc = bass.Bass("TRN2", target_bir_lowering=False)
    ctx = Ctx()
    ctx.nc = nc
    ctx.debug = debug
    ctx.io = declare_io(nc, debug)

    with ExitStack() as stack:
        ctx.stack = stack
        ctx.sb = lambda n, s, d: stack.enter_context(nc.sbuf_tensor(n, s, d))
        ctx.sems = {}
        ctx.semvals = {}

        def get_sem(name):
            if name not in ctx.sems:
                ctx.sems[name] = stack.enter_context(nc.semaphore(name))
                ctx.semvals.setdefault(name, 0)
            return ctx.sems[name]
        ctx.get_sem = get_sem
        for s in ("sMM", "sEV", "sVX", "sCC", "sCH"):
            get_sem(s)

        sb = ctx.sb
        ctx.ident = sb("ident", [128, 128], F32)
        ctx.ident_bf = sb("ident_bf", [128, 128], BF16)
        ctx.tril_s = sb("tril_s", [128, 128], F32)
        ctx.triu_i = sb("triu_i", [128, 128], F32)
        ctx.ones_row = sb("ones_row", [1, 128], F32)
        ctx.r1_st = sb("r1_st", [128, NCHUNKS], F32)
        ctx.b1_st = sb("b1_st", [128, NCHUNKS], F32)
        ctx.beta_st = sb("beta_st", [128, NCHUNKS], F32)
        ctx.betan_st = sb("betan_st", [128, NCHUNKS], F32)
        ctx.r2_st = sb("r2_st", [128, NCHUNKS], F32)
        ctx.mu2_st = sb("mu2_st", [128, NCHUNKS], F32)
        ctx.var2_st = sb("var2_st", [128, NCHUNKS], F32)
        ctx.mx_st = sb("mx_st", [128, NCHUNKS], F32)
        ctx.mn_st = sb("mn_st", [128, NCHUNKS], F32)
        ctx.alpha_sb = sb("alpha_sb", [128, G], F32)
        ctx.a_sb = sb("a_sb", [128, G], F32)
        ctx.n1w_sb = sb("n1w_sb", [128, G], F32)
        ctx.n2w_sb = sb("n2w_sb", [128, G], F32)
        ctx.bbn_col = sb("bbn_col", [128, 1], F32)
        ctx.sig_cell = sb("sig_cell", [1, 8], F32)
        ctx.qs_st = sb("qs_st", [128, NCHUNKS], F32)
        ctx.qb_st = sb("qb_st", [128, NCHUNKS], F32)
        ctx.s1_col = sb("s1_col", [128, 1], F32)
        ctx.is1_col = sb("is1_col", [128, 1], F32)
        ctx.ccb = sb("ccb", [1, 128], F32)
        ctx.s2_col = sb("s2_col", [128, 1], F32)
        ctx.is2_col = sb("is2_col", [128, 1], F32)

        emit_init(ctx)
        if "p1" in phases:
            emit_p1(ctx)
        if "w" in phases:
            emit_wquant(ctx)
        if "p2a" in phases:
            emit_p2a(ctx)
        if "p2bi" in phases:
            emit_p2bi(ctx)
        if "p2bii" in phases:
            emit_p2bii(ctx)
        if "p3" in phases:
            emit_p3(ctx)
        if debug:
            emit_debug_dump(ctx)
        emit_final(ctx)

    return nc


CONST_VALS = (EPS_LN, EPS_LN * EPS_RMS, EPS_RMS, RBIAS)


def emit_init(ctx):
    nc, io = ctx.nc, ctx.io
    ctx.const_tiles = ctx.sb("const_tiles", [128, len(CONST_VALS)], F32)
    for j, v in enumerate(CONST_VALS):
        nc.const_aps.register(F32, v, ctx.const_tiles[:, j:j + 1])
    with ExitStack() as st:
        ri = st.enter_context(nc.sbuf_tensor("ri", [128, 128], F32))
        ci = st.enter_context(nc.sbuf_tensor("ci", [128, 128], F32))
        pcol = st.enter_context(nc.psum_tensor("pcol", [128, 8], F32))
        p = Prog(ctx)
        p.mark("start")
        def setconsts(e):
            r = None
            for j, v in enumerate(CONST_VALS):
                r = e.memset(ctx.const_tiles[:, j:j + 1], v)
            return r
        p.step("pl", setconsts, inc=[("sVX", 1)], label="i_const")
        p.dma("sp", ctx.n1w_sb[:, :], io["n1w"].rearrange("(g p) -> p g", p=128), sem="ild", slow=True)
        p.dma("sp", ctx.n2w_sb[:, :], io["n2w"].rearrange("(g p) -> p g", p=128), sem="ild", slow=True)
        p.dma("sp", ctx.alpha_sb[:, :], io["al"].rearrange("(g p) -> p g", p=128), sem="ild", slow=True)
        p.dma("sp", ctx.sig_cell[0:1, 0:1], io["cx"].unsqueeze(0), sem="ild")
        p.dma("sp", ctx.sig_cell[0:1, 1:2], io["cd"].unsqueeze(0), sem="ild")
        p.dma("sp", ctx.sig_cell[0:1, 4:5], io["bb"].unsqueeze(0), sem="ild", label="i_ld")
        p.step("pl", lambda e: (
            e.iota(ri[:, :], [[0, 128]], channel_multiplier=1,
                   allow_small_or_imprecise_dtypes=True),
            e.iota(ci[:, :], [[1, 128]], channel_multiplier=0,
                   allow_small_or_imprecise_dtypes=True)),
            inc=[("sVX", 1)], label="i_iota")
        p.chain("ve", [
            lambda e: (
                e.tensor_tensor(ctx.tril_s[:, :], ri[:, :], ci[:, :], ALU.is_gt),
                e.tensor_tensor(ctx.triu_i[:, :], ri[:, :], ci[:, :], ALU.is_le),
                e.memset(ctx.ones_row[:, :], 1.0),
                e.tensor_tensor(ctx.ident[:, :], ri[:, :], ci[:, :], ALU.is_equal)),
            lambda e: e.tensor_copy(ctx.ident_bf[:, :], ctx.ident[:, :])],
            waits=[("sVX", "i_iota")], inc=[("sVX", 1)], label="i_masks")
        p.step("se", lambda e: (
            e.activation(ctx.a_sb[:, :], ctx.alpha_sb[:, :], AF.Exp, scale=-1.0),
            e.activation(ctx.sig_cell[0:1, 0:2], ctx.sig_cell[0:1, 0:2],
                         AF.Exp, scale=-1.0)),
            waits=[("ild", "i_ld")], inc=[("sEV", 1)], label="i_exp")
        p.step("ve", lambda e: (
            e.tensor_scalar_add(ctx.a_sb[:, :], ctx.a_sb[:, :], 1.0),
            e.tensor_scalar_add(ctx.sig_cell[0:1, 0:2], ctx.sig_cell[0:1, 0:2], 1.0)),
            waits=[("sEV", "i_exp")], inc=[("sVX", 1)], label="i_add1")
        p.step("ve", lambda e: (
            e.reciprocal(ctx.alpha_sb[:, :], ctx.a_sb[:, :]),
            e.reciprocal(ctx.sig_cell[0:1, 2:4], ctx.sig_cell[0:1, 0:2])),
            waits=[("sVX", "i_add1")], inc=[("sVX", 1)], label="i_recip")
        p.step("ve", lambda e: (
            e.tensor_copy(ctx.sig_cell[0:1, 0:2], ctx.sig_cell[0:1, 2:4]),
            e.tensor_scalar(ctx.a_sb[:, :], ctx.alpha_sb[:, :], -1.0, 1.0,
                            ALU.mult, ALU.add)),
            waits=[("sVX", "i_recip")], inc=[("sVX", 1)], label="i_sig")
        p.step("te", lambda e: e.matmul(pcol[:, 0:1], ctx.ones_row[:, :],
                                        ctx.sig_cell[0:1, 4:5], start=True, stop=True),
               waits=[("sVX", "i_masks"), ("ild", "i_ld")], inc=[("sMM", 1)],
               label="i_bbmm")
        p.step("ve", lambda e: e.tensor_scalar_mul(ctx.bbn_col[:, :], pcol[:, 0:1], -1.0),
               waits=[("sMM", "i_bbmm")], inc=[("sVX", 1)], label="i_bbn")
        p.quiesce("init")
        p.emit()


def emit_p1(ctx):
    nc, io = ctx.nc, ctx.io
    with ExitStack() as st:
        lsb = lambda n, s, d: st.enter_context(nc.sbuf_tensor(n, s, d))
        lps = lambda n, s: st.enter_context(nc.psum_tensor(n, s, F32))
        wq_sb = lsb("wq_sb", [128, G, D], BF16)
        wk_sb = lsb("wk_sb", [128, G, D], BF16)
        wv_sb = lsb("wv_sb", [128, G, D], BF16)
        bw_sb = lsb("bw_sb", [128, G, 1], BF16)

        wstage = lsb("wstage", [128, G, D], BF16)
        bstage = lsb("bstage", [128, G], F32)

        x0t = lsb("x0t", [128, 4, D], F32)
        h_bf2 = lsb("h_bf", [128, 2, D], BF16)
        kk_scr = lsb("kk_scr", [128, D], BF16)
        h_dm2 = lsb("h_dm", [128, 2, G, C], BF16)
        q_tm2 = lsb("q_tm", [128, 2, D], BF16)
        k_tm2 = lsb("k_tm", [128, 2, D], BF16)
        v_tm2 = lsb("v_tm", [128, 2, D], BF16)
        vb_c2 = lsb("vb_c", [128, 2, G, C], F32)
        ema2 = lsb("ema2", [128, 2, G, C], F32)
        vst2 = lsb("vst2", [128, 2, D], BF16)
        stat2 = lsb("stat", [128, 2, 10], F32)
        bcell = lsb("bcell", [128, 1], F32)
        ptp = [lps(f"p1tp{j}", [128, 128]) for j in range(4)]
        ptpb = [t[:, :].bitcast(BF16)[:, 0:128] for t in ptp]
        pA = lps("p1A", [128, D])
        pk = lps("p1k", [128, D])

        p = Prog(ctx)
        p.mark("p1s")
        p.cur_group = ("WL", 0)
        for wi, (srcw, dst) in enumerate([(io["wq"], wq_sb), (io["wk"], wk_sb),
                                          (io["wv"], wv_sb)]):
            w = [("sEV", f"wf{wi-1}")] if wi else []
            p.dma("pl", wstage[:, :, :], srcw.rearrange("(g p) d -> p g d", p=128),
                  sem="wl", waits=w, label=f"wl{wi}", xfer=9000)

            def fold(e, dst=dst):
                r = None
                for g in range(G):
                    r = e.activation(dst[:, g, :], wstage[:, g, :], AF.Copy,
                                     scale=ctx.n1w_sb[:, g:g + 1])
                return r
            p.step("se", fold, waits=[("wl", f"wl{wi}")], inc=[("sEV", 1)],
                   label=f"wf{wi}", dur=2200)
        p.dma("sp", bstage[:, :], io["bw"].rearrange("(g p) o -> p (g o)", p=128),
              sem="bwl", label="bwl", slow=True)
        p.chain("ve", [
            lambda e: e.tensor_tensor(bstage[:, :], bstage[:, :],
                                      ctx.n1w_sb[:, :], ALU.mult),
            lambda e: e.tensor_copy(bw_sb[:, :, 0], bstage[:, :])],
            waits=[("bwl", "bwl"), ("sEV", "wf2")], inc=[("sVX", 1)], label="bwf")
        tp_user = [None] * 4

        def tpw(bank):
            return [tp_user[bank]] if tp_user[bank] else []

        for b in range(NB):
            for c in range(NCH):
                i = b * NCH + c
                s2 = i % 2
                p.cur_group = ("C", i)
                h_bf = h_bf2[:, s2, :]
                h_dm = h_dm2[:, s2, :, :]
                q_tm = q_tm2[:, s2, :]
                k_tm = k_tm2[:, s2, :]
                v_tm = v_tm2[:, s2, :]
                stat = stat2[:, s2, :]
                w = [("sVX", f"h{i-4}")] if i >= 4 else []
                p.dma("sp", x0t[:, i % 4, :], io["x"][b, c * C:(c + 1) * C, :],
                      sem=f"x0{i % 4}", waits=w, label=f"x0{i}", xfer=4000)
                suw = [(f"x0{i % 4}", f"x0{i}")]
                if i >= 2:
                    suw.append(("sVX", f"mv{i-2}"))
                p.step("ve", lambda e, i=i, stat=stat: e.reduce_sum(
                    stat[:, 0:1], x0t[:, i % 4, :], AX.X),
                    waits=suw, inc=[("sVX", 1)], label=f"sum{i}", dur=450)
                sqw = [(f"x0{i % 4}", f"x0{i}")]
                if i >= 2:
                    sqw.append(("sVX", f"mv{i-2}"))
                p.step("se", lambda e, i=i, stat=stat: e.activation(
                    kk_scr[:, :], x0t[:, i % 4, :], AF.Square,
                    accum_out=stat[:, 1:2]),
                    waits=sqw, inc=[("sEV", 1)], label=f"sq{i}", dur=1150)
                mvw = [("sVX", f"sum{i}"), ("sEV", f"sq{i}")]
                if i >= 2:
                    mvw.append(("sVX", f"bc{i-2}"))
                p.chain("ve", [
                    lambda e, stat=stat: e.tensor_scalar_mul(stat[:, 2:3],
                                                             stat[:, 0:1], 1.0 / D),
                    lambda e, stat=stat: e.scalar_tensor_tensor(
                        stat[:, 3:4], stat[:, 2:3], -1.0, stat[:, 2:3],
                        ALU.mult, ALU.mult),
                    lambda e, stat=stat: e.scalar_tensor_tensor(
                        stat[:, 3:4], stat[:, 1:2], 1.0 / D, stat[:, 3:4],
                        ALU.mult, ALU.add)],
                    waits=mvw, inc=[("sVX", 1)], label=f"mv{i}")
                rsw = [("sVX", f"mv{i}")]
                if i >= 2:
                    rsw.append(("sVX", f"h{i-2}"))
                p.chain("se", [
                    lambda e, stat=stat: (
                        e.activation(stat[:, 4:5], stat[:, 3:4], AF.Ln, bias=EPS_LN),
                        e.activation(stat[:, 5:6], stat[:, 3:4], AF.Ln,
                                     scale=1.0 + EPS_RMS, bias=EPS_LN * EPS_RMS)),
                    lambda e, stat=stat: (
                        e.activation(stat[:, 4:5], stat[:, 4:5], AF.Exp, scale=-0.5),
                        e.activation(stat[:, 5:6], stat[:, 5:6], AF.Exp, scale=-0.5))],
                    waits=rsw, inc=[("sEV", 1)], label=f"rs{i}")
                p.step("ve", lambda e, i=i, stat=stat: (
                    e.tensor_copy(ctx.r1_st[:, i:i + 1], stat[:, 4:5]),
                    e.scalar_tensor_tensor(ctx.b1_st[:, i:i + 1], stat[:, 2:3],
                                           -1.0, stat[:, 4:5], ALU.mult, ALU.mult),
                    e.scalar_tensor_tensor(stat[:, 6:7], stat[:, 2:3],
                                           -1.0, stat[:, 5:6], ALU.mult, ALU.mult)),
                    waits=[("sEV", f"rs{i}")], inc=[("sVX", 1)], label=f"bc{i}")
                hw = [("sVX", f"bc{i}"), (f"x0{i % 4}", f"x0{i}")]
                if i >= 2:
                    hw.append(("sMM", f"htp{i-2}_{G-1}"))
                p.step("ve", lambda e, i=i, h_bf=h_bf, stat=stat: e.tensor_scalar(
                    h_bf, x0t[:, i % 4, :], stat[:, 5:6], stat[:, 6:7],
                    ALU.mult, ALU.add),
                    waits=hw, inc=[("sVX", 1)], label=f"h{i}", dur=600)
                for g in range(G):
                    bk = g % 4
                    p.step("te", lambda e, g=g, bk=bk, h_bf=h_bf: e.transpose(
                        ptpb[bk], h_bf[:, g * 128:(g + 1) * 128],
                        ctx.ident_bf[:, :]),
                        waits=[("sVX", f"h{i}")] + tpw(bk),
                        inc=[("sMM", 1)], label=f"htp{i}_{g}", dur=120)
                    ew = [("sMM", f"htp{i}_{g}")]
                    if g < 2 and i >= 2:
                        ew.append(("sMM", f"bmm{i-2}"))
                    sm = evac(p, g, h_dm[:, g, :], ptpb[bk],
                              waits=ew, label=f"hde{i}_{g}")
                    tp_user[bk] = (sm, f"hde{i}_{g}")
                def mmh(e, wsb, po, hh, h_dm=h_dm):
                    r = None
                    for g in range(G):
                        r = e.matmul(po[:, hh * 512:(hh + 1) * 512], h_dm[:, g, :],
                                     wsb[:, g, hh * 512:(hh + 1) * 512],
                                     start=(g == 0), stop=(g == G - 1))
                    return r
                for hh in range(2):
                    qw = [("sVX", f"hde{i}_{G-1}"), ("sEV", f"hde{i}_{G-2}")]
                    if i == 0:
                        qw.append(("sEV", "wf0"))
                    if i >= 1:
                        qw.append((("sEV", f"vev{i-1}_0") if hh == 0
                                   else ("sVX", f"vev{i-1}_1")))
                    p.step("te", lambda e, hh=hh, mmh=mmh: mmh(e, wq_sb, pA, hh),
                           waits=qw, inc=[("sMM", 1)], label=f"qmm{i}_{hh}",
                           dur=1700)
                qv0 = [("sMM", f"qmm{i}_0")]
                qv1 = [("sMM", f"qmm{i}_1")]
                if i >= 2:
                    qv0.append((f"qst{i % 2}", f"qst{i-2}"))
                    qv1.append((f"qst{i % 2}", f"qst{i-2}"))
                p.step("se", lambda e, q_tm=q_tm: e.activation(
                    q_tm[:, 0:512], pA[:, 0:512], AF.Copy),
                    waits=qv0, inc=[("sEV", 1)], label=f"qev{i}_0", dur=580)
                p.step("ve", lambda e, q_tm=q_tm: e.tensor_copy(
                    q_tm[:, 512:1024], pA[:, 512:1024]),
                    waits=qv1, inc=[("sVX", 1)], label=f"qev{i}_1", dur=580)
                p.dma("se", io["q_s"][b, c, :, :], q_tm, sem=f"qst{i % 2}",
                      waits=[("sEV", f"qev{i}_0"), ("sVX", f"qev{i}_1")],
                      label=f"qst{i}")
                for hh in range(2):
                    vw = [("sVX", f"hde{i}_{G-1}"), ("sEV", f"hde{i}_{G-2}")]
                    if i == 0:
                        vw.append(("sEV", "wf2"))
                    vw.append((("sEV", f"qev{i}_0") if hh == 0
                               else ("sVX", f"qev{i}_1")))
                    p.step("te", lambda e, hh=hh, mmh=mmh: mmh(e, wv_sb, pA, hh),
                           waits=vw, inc=[("sMM", 1)], label=f"vmm{i}_{hh}",
                           dur=1700)
                vv0 = [("sMM", f"vmm{i}_0")]
                vv1 = [("sMM", f"vmm{i}_1")]
                if i >= 2:
                    vv0.append(("sMM", f"vtp{i-2}_{G-1}"))
                    vv1.append(("sMM", f"vtp{i-2}_{G-1}"))
                p.step("se", lambda e, v_tm=v_tm: e.activation(
                    v_tm[:, 0:512], pA[:, 0:512], AF.Copy),
                    waits=vv0, inc=[("sEV", 1)], label=f"vev{i}_0", dur=580)
                p.step("ve", lambda e, v_tm=v_tm: e.tensor_copy(
                    v_tm[:, 512:1024], pA[:, 512:1024]),
                    waits=vv1, inc=[("sVX", 1)], label=f"vev{i}_1", dur=580)
                for g in range(G):
                    bk = g % 4
                    p.step("te", lambda e, g=g, bk=bk, v_tm=v_tm: e.transpose(
                        ptpb[bk], v_tm[:, g * 128:(g + 1) * 128],
                        ctx.ident_bf[:, :]),
                        waits=[("sEV", f"vev{i}_0"), ("sVX", f"vev{i}_1")]
                        + tpw(bk),
                        inc=[("sMM", 1)], label=f"vtp{i}_{g}", dur=120)
                    vdw = [("sMM", f"vtp{i}_{g}")]
                    if i >= 2:
                        vdw.append(("sVX", f"ema{i-2}_{g}"))
                    sm = evac(p, g, vb_c2[:, s2, g, :], ptpb[bk],
                              waits=vdw, label=f"vde{i}_{g}",
                              scale=ctx.alpha_sb[:, g:g + 1])
                    tp_user[bk] = (sm, f"vde{i}_{g}")
                for g in range(G):
                    emw = [(("sEV" if g % 2 == 0 else "sVX"), f"vde{i}_{g}")]
                    if i >= 1:
                        emw.append(("sVX", f"ema{i-1}_{g}"))
                    if i >= 2:
                        emw.append(("sMM", f"emtp{i-2}_{g}"))
                    if c > 0:
                        init = ema2[:, (i - 1) % 2, g, C - 1:C]
                    else:
                        init = 0.0
                    p.step("ve", lambda e, g=g, s2=s2, init=init: e.tensor_tensor_scan(
                        ema2[:, s2, g, :],
                        ctx.a_sb[:, g:g + 1].broadcast_to([128, C]),
                        vb_c2[:, s2, g, :], init, ALU.mult, ALU.add),
                        waits=emw, inc=[("sVX", 1)], label=f"ema{i}_{g}", dur=220)
                    bk = g % 4
                    p.step("te", lambda e, g=g, bk=bk, s2=s2: e.transpose(
                        ptp[bk][:, :], ema2[:, s2, g, :], ctx.ident[:, :]),
                        waits=[("sVX", f"ema{i}_{g}")] + tpw(bk),
                        inc=[("sMM", 1)], label=f"emtp{i}_{g}", dur=120)
                    edw = [("sMM", f"emtp{i}_{g}")]
                    if g < 2 and i >= 2:
                        edw.append((f"vs{i % 2}", f"vsp{i-2}"))
                    sm = evac(p, g + 1, vst2[:, s2, g * 128:(g + 1) * 128],
                              ptp[bk][:, :], waits=edw, label=f"emde{i}_{g}")
                    tp_user[bk] = (sm, f"emde{i}_{g}")
                p.dma("sp", io["v_s"][b, c, :, :], vst2[:, s2, :],
                      sem=f"vs{i % 2}",
                      waits=[("sEV", f"emde{i}_{G-1}"), ("sVX", f"emde{i}_{G-2}")],
                      label=f"vsp{i}")
                for hh in range(2):
                    kw = [("sVX", f"hde{i}_{G-1}"), ("sEV", f"hde{i}_{G-2}")]
                    if i == 0:
                        kw.append(("sEV", "wf1"))
                    if i >= 1 and hh == 0:
                        kw.append(("sEV", f"bex{i-1}"))
                    if i >= 1:
                        kw.append((("sEV", f"kev{i-1}_0") if hh == 0
                                   else ("sVX", f"kev{i-1}_1")))
                    p.step("te", lambda e, hh=hh, mmh=mmh: mmh(e, wk_sb, pk, hh),
                           waits=kw, inc=[("sMM", 1)], label=f"kmm{i}_{hh}",
                           dur=1700)
                ksw = [("sMM", f"kmm{i}_0"), ("sMM", f"kmm{i}_1")]
                if i >= 2:
                    ksw.append(("sVX", f"kn{i-2}"))
                p.chain("se", [
                    lambda e, stat=stat: e.activation(
                        kk_scr[:, :], pk[:, :], AF.Square, accum_out=stat[:, 7:8]),
                    lambda e, stat=stat: e.activation(stat[:, 8:9], stat[:, 7:8],
                                                      AF.Ln),
                    lambda e, stat=stat: e.activation(stat[:, 8:9], stat[:, 8:9],
                                                      AF.Exp, scale=0.5)],
                    waits=ksw, inc=[("sEV", 1)], label=f"ksq{i}")
                knw = [("sEV", f"ksq{i}")]
                if i >= 2:
                    knw += [("sEV", f"kev{i-2}_0"), ("sVX", f"kev{i-2}_1")]
                p.chain("ve", [
                    lambda e, stat=stat: e.tensor_scalar_max(stat[:, 8:9],
                                                             stat[:, 8:9], 1e-12),
                    lambda e, stat=stat: e.reciprocal(stat[:, 9:10], stat[:, 8:9])],
                    waits=knw, inc=[("sVX", 1)], label=f"kn{i}")
                kevw0 = [("sVX", f"kn{i}")]
                kevw1 = [("sVX", f"kn{i}")]
                if i >= 2:
                    kevw0.append((f"kst{i % 2}", f"kst{i-2}"))
                    kevw1.append((f"kst{i % 2}", f"kst{i-2}"))
                p.step("se", lambda e, k_tm=k_tm, stat=stat: e.activation(
                    k_tm[:, 0:512], pk[:, 0:512], AF.Copy, scale=stat[:, 9:10]),
                    waits=kevw0, inc=[("sEV", 1)], label=f"kev{i}_0", dur=580)
                p.step("ve", lambda e, k_tm=k_tm, stat=stat: e.tensor_scalar_mul(
                    k_tm[:, 512:1024], pk[:, 512:1024], stat[:, 9:10]),
                    waits=kevw1, inc=[("sVX", 1)], label=f"kev{i}_1", dur=580)
                p.dma("se", io["k_s"][b, c, :, :], k_tm, sem=f"kst{i % 2}",
                      waits=[("sEV", f"kev{i}_0"), ("sVX", f"kev{i}_1")],
                      label=f"kst{i}")
                bw2 = [("sVX", f"hde{i}_{G-1}"), ("sEV", f"hde{i}_{G-2}"),
                       ("sEV", f"kev{i}_0"), ("sVX", f"kev{i}_1")]
                if i == 0:
                    bw2.append(("sVX", "bwf"))

                def bmm(e, h_dm=h_dm):
                    r = None
                    for g in range(G):
                        r = e.matmul(pk[:, 0:1], h_dm[:, g, :], bw_sb[:, g, :],
                                     start=(g == 0), stop=(g == G - 1))
                    return r
                bexw = [("sMM", f"bmm{i}")]
                if i >= 1:
                    bexw.append(("sVX", f"bfin{i-1}"))
                p.step("te", bmm, waits=bw2, inc=[("sMM", 1)], label=f"bmm{i}",
                       dur=450)
                p.step("se", lambda e: e.activation(bcell[:, :], pk[:, 0:1], AF.Exp,
                                                    scale=-1.0, bias=ctx.bbn_col[:, :]),
                       waits=bexw, inc=[("sEV", 1)], label=f"bex{i}")
                p.chain("ve", [
                    lambda e: e.tensor_scalar_add(bcell[:, :], bcell[:, :], 1.0),
                    lambda e, i=i: e.reciprocal(ctx.beta_st[:, i:i + 1], bcell[:, :]),
                    lambda e, i=i: e.tensor_scalar_mul(ctx.betan_st[:, i:i + 1],
                                                       ctx.beta_st[:, i:i + 1], -1.0)],
                    waits=[("sEV", f"bex{i}")], inc=[("sVX", 1)], label=f"bfin{i}")
        p.cur_group = None
        p.quiesce("p1")
        p.emit(reorder=True)



def emit_wquant(ctx):
    nc, io = ctx.nc, ctx.io
    with ExitStack() as st:
        lsb = lambda n, s, d: st.enter_context(nc.sbuf_tensor(n, s, d))
        wt = lsb("wqt", [128, 4, II], F32)
        z16 = lsb("wqz", [128, 2, II], F16)
        qw = lsb("wqq", [128, 2, II], BF16)
        col = lsb("wqc", [128, 2, 4], F32)
        p = Prog(ctx)
        p.mark("wq0")
        jobs = []
        for (src_t, dst_t, wfree) in ((io["fwg"], io["wgq_s"], D),
                                      (io["fwu"], io["wuq_s"], D),
                                      (io["fwd"], io["wdq_s"], II)):
            for j in range(src_t.shape[0] // 128):
                jobs.append((src_t, dst_t, wfree, j))
        for idx, (src_t, dst_t, wfree, j) in enumerate(jobs):
            sl = idx % 2
            sl4 = idx % 4
            p.cur_group = ("W", idx)
            cl = col[:, sl, :]
            lw = ([("sEV", f"wqz{idx-4}"), ("sVX", f"wqm{idx-4}")]
                  if idx >= 4 else [])
            p.dma("sp", wt[:, sl4, 0:wfree], src_t[j * 128:(j + 1) * 128, :],
                  sem=f"wql{sl4}", waits=lw, label=f"wql{idx}", xfer=8000)
            mw = [(f"wql{sl4}", f"wql{idx}")]
            if idx >= 2:
                mw.append(("sVX", f"wqi{idx-2}"))
            p.step("ve", lambda e, sl4=sl4, cl=cl, wfree=wfree: e.tensor_reduce(
                cl[:, 0:1], wt[:, sl4, 0:wfree], AX.X, ALU.max,
                apply_absolute_value=True),
                waits=mw, inc=[("sVX", 1)], label=f"wqm{idx}", dur=1400)
            iw = [("sVX", f"wqm{idx}")]
            if idx >= 2:
                iw += [("sVX", f"wqs{idx-2}"), ("sEV", f"wqz{idx-2}")]
            p.chain("ve", [
                lambda e, cl=cl: e.tensor_scalar_mul(cl[:, 1:2], cl[:, 0:1],
                                                     1.0 / 127.0),
                lambda e, cl=cl: e.reciprocal(cl[:, 2:3], cl[:, 1:2])],
                waits=iw, inc=[("sVX", 1)], label=f"wqi{idx}")
            zw = [("sVX", f"wqi{idx}")]
            if idx >= 2:
                zw.append(("sVX", f"wqs{idx-2}"))
            p.step("se", lambda e, sl=sl, sl4=sl4, cl=cl, wfree=wfree:
                   e.activation(
                z16[:, sl, 0:wfree], wt[:, sl4, 0:wfree], AF.Identity,
                scale=cl[:, 2:3], bias=1536.0),
                waits=zw, inc=[("sEV", 1)], label=f"wqz{idx}", dur=2400)
            sw = [("sEV", f"wqz{idx}")]
            if idx >= 2:
                sw.append((f"wqs{idx % 2}", f"wqst{idx-2}"))
            p.step("ve", lambda e, sl=sl, cl=cl, wfree=wfree: e.tensor_scalar(
                qw[:, sl, 0:wfree], z16[:, sl, 0:wfree], -1536.0, cl[:, 1:2],
                ALU.add, ALU.mult),
                waits=sw, inc=[("sVX", 1)], label=f"wqs{idx}", dur=1400)
            p.dma("se", dst_t[j * 128:(j + 1) * 128, :], qw[:, sl, 0:wfree],
                  sem=f"wqs{idx % 2}", waits=[("sVX", f"wqs{idx}")],
                  label=f"wqst{idx}", xfer=6000)
        p.cur_group = None
        p.quiesce("wq")
        p.emit(reorder=True)




def emit_p2a(ctx):
    nc, io = ctx.nc, ctx.io
    with ExitStack() as st:
        lsb = lambda n, s, d: st.enter_context(nc.sbuf_tensor(n, s, d))
        lps = lambda n, s: st.enter_context(nc.psum_tensor(n, s, F32))
        S_f = lsb("S_f", [128, NB, G, D], F32)
        S_b = lsb("S_b", [128, NB, G, D], BF16)
        q_tm4 = lsb("p2q", [128, 4, D], BF16)
        k_tm4 = lsb("p2k", [128, 4, D], BF16)
        v_tm4 = lsb("p2v", [128, 4, D], BF16)
        q_dm2 = lsb("p2qd", [128, 4, G, C], BF16)
        k_dm2 = lsb("p2kd", [128, 4, G, C], BF16)
        U_bf2 = lsb("p2U", [128, 4, D], BF16)
        Pt_bf = lsb("p2Pt", [128, 128], BF16)
        Qt_bf = lsb("p2Qt", [128, 128], BF16)
        RHS_bf2 = lsb("p2R", [128, 2, D], BF16)
        O_bf2 = lsb("p2O", [128, 2, D], BF16)
        vb_f2 = lsb("p2vb", [128, 2, D], F32)
        A_f2 = lsb("p2A", [128, 2, 128], F32)
        M_bf2 = lsb("p2M", [128, 2, 128], BF16)
        W_bf2 = lsb("p2W", [128, 2, 128], BF16)
        X_bf2 = [[lsb(f"p2X{j}_{s}", [128, 128], BF16) for j in range(2)]
                 for s in range(2)]
        Y_bf2 = [[lsb(f"p2Y{j}_{s}", [128, 128], BF16) for j in range(2)]
                 for s in range(2)]
        P_bf2 = lsb("p2P", [128, 2, 128], BF16)
        pW = [lps(f"p2w{j}", [128, 512]) for j in range(4)]
        pXY = [(lps("p2g", [128, 128]), lps("p2t", [128, 128])),
               (lps("p2n1", [128, 128]), lps("p2n2", [128, 128]))]
        pXb = [x[:, :].bitcast(BF16)[:, 0:128] for x, _ in pXY]
        pYb = [y[:, :].bitcast(BF16)[:, 0:128] for _, y in pXY]

        p = Prog(ctx)
        p.mark("p2a0")
        p.step("pl", lambda e: (e.memset(S_f[:, :, :, :], 0.0),
                                e.memset(S_b[:, :, :, :], 0.0)),
               inc=[("sVX", 1)], label="s_zero")

        pu = {}

        def bw(key):
            return [(pu[key][0], pu[key][1])] if key in pu else []

        def setu(key, sem, lab):
            pu[key] = (sem, lab)

        Slast = {0: ("sVX", "s_zero"), 1: ("sVX", "s_zero")}
        insts = [(c, b) for c in range(NCH) for b in range(NB)]
        NI = len(insts)

        def idx(pp):
            c2, b2 = insts[pp]
            return b2 * NCH + c2

        def su_of(pp):
            return pp + 2 if (pp // 2) % 2 == 0 else pp

        def a_load(pos):
            c, b = insts[pos]
            n = f"{idx(pos)}"
            sl = pos % 4
            j = idx(pos - 4) if pos >= 4 else None
            p.dma("sp", q_tm4[:, sl, :], io["q_s"][b, c, :, :], sem=f"aq{sl}",
                  waits=[("sMM", f"qtp{j}_{G-1}")] if j is not None else [],
                  label=f"ql{n}")
            yield
            p.dma("sp", k_tm4[:, sl, :], io["k_s"][b, c, :, :], sem=f"ak{sl}",
                  waits=[("sMM", f"su{idx(su_of(pos - 4))}_{G-1}")]
                  if j is not None else [],
                  label=f"kl{n}")
            yield
            p.dma("sp", v_tm4[:, sl, :], io["v_s"][b, c, :, :], sem=f"av{sl}",
                  waits=[("sVX", f"vb{j}")] if j is not None else [],
                  label=f"vl{n}")
            yield

        def a_comp(pos):
            n = f"{idx(pos)}"
            sl = pos % 4
            ip = pos % 2
            ceven = (pos // 2) % 2 == 0
            q_tm = q_tm4[:, sl, :]
            k_tm = k_tm4[:, sl, :]
            q_dm = q_dm2[:, sl, :, :]
            k_dm = k_dm2[:, sl, :, :]
            A_f = A_f2[:, ip, :]
            M_bf = M_bf2[:, ip, :]
            W_bf = W_bf2[:, ip, :]
            X_bf = X_bf2[ip]
            Y_bf = Y_bf2[ip]
            P_bf = P_bf2[:, ip, :]
            pX, pY = pXY[ip]
            pX_b, pY_b = pXb[ip], pYb[ip]
            kX, kY = f"pX{ip}", f"pY{ip}"
            prev2 = idx(pos - 2) if pos >= 2 else None
            for g in range(G):
                qw = [(f"aq{sl}", f"ql{n}")] if g == 0 else []
                p.step("te", lambda e, g=g, q_tm=q_tm: e.transpose(
                    pX_b, q_tm[:, g * 128:(g + 1) * 128], ctx.ident_bf[:, :]),
                    waits=qw + bw(kX), inc=[("sMM", 1)], label=f"qtp{n}_{g}",
                    dur=120)
                ew = [("sMM", f"qtp{n}_{g}")]
                if g == 0 and pos >= 4:
                    j4 = idx(pos - 4)
                    ew += [("sMM", f"o{j4}_1"), ("sMM", f"kq{j4}")]
                sm = evac(p, g, q_dm[:, g, :], pX_b,
                          waits=ew, label=f"qde{n}_{g}")
                setu(kX, sm, f"qde{n}_{g}")
                yield
                kw = [(f"ak{sl}", f"kl{n}")] if g == 0 else []
                p.step("te", lambda e, g=g, k_tm=k_tm: e.transpose(
                    pY_b, k_tm[:, g * 128:(g + 1) * 128], ctx.ident_bf[:, :]),
                    waits=kw + bw(kY), inc=[("sMM", 1)], label=f"ktp{n}_{g}",
                    dur=120)
                ew = [("sMM", f"ktp{n}_{g}")]
                if g == 0 and pos >= 4:
                    j4 = idx(pos - 4)
                    ew += [("sMM", f"ks{j4}_1"), ("sMM", f"kq{j4}")]
                    if (pos - 4) // 2 % 2 == 0:
                        ew.append(("sMM", f"ptq{idx(pos-2)}"))
                sm = evac(p, g + 1, k_dm[:, g, :], pY_b,
                          waits=ew, label=f"kde{n}_{g}")
                setu(kY, sm, f"kde{n}_{g}")
                yield
            def gmm(e, k_dm=k_dm):
                r = None
                for g in range(G):
                    r = e.matmul(pX[:, :], k_dm[:, g, :], k_dm[:, g, :],
                                 start=(g == 0), stop=(g == G - 1))
                return r
            p.step("te", gmm,
                   waits=[("sEV", f"kde{n}_{G-1}"), ("sVX", f"kde{n}_{G-2}")]
                   + bw(kX), inc=[("sMM", 1)], label=f"gmm{n}", dur=900)
            yield
            aw = [("sMM", f"gmm{n}")]
            if prev2 is not None:
                aw.append(("sMM", f"nt{prev2}_5c"))
            i = idx(pos)
            p.chain("ve", [
                lambda e, A_f=A_f: e.tensor_tensor(A_f, pX[:, :], ctx.tril_s[:, :],
                                                   ALU.mult),
                lambda e, i=i, A_f=A_f: e.tensor_scalar_mul(A_f, A_f,
                                                            ctx.beta_st[:, i:i + 1]),
                lambda e, A_f=A_f, M_bf=M_bf: e.tensor_tensor(
                    M_bf, A_f, ctx.ident[:, :], ALU.add)],
                waits=aw, inc=[("sVX", 1)], label=f"am{n}")
            setu(kX, "sVX", f"am{n}")
            p.step("te", lambda e, A_f=A_f: e.transpose(pY[:, :], A_f,
                                                        ctx.ident[:, :]),
                   waits=[("sVX", f"am{n}")] + bw(kY),
                   inc=[("sMM", 1)], label=f"atp{n}", dur=120)
            yield
            xw = [("sMM", f"atp{n}")]
            if prev2 is not None:
                xw.append(("sMM", f"u{prev2}_1"))
            p.step("ve", lambda e, X_bf=X_bf, Y_bf=Y_bf, A_f=A_f: (
                e.scalar_tensor_tensor(X_bf[0][:, :], pY[:, :], -1.0,
                                       ctx.ident[:, :], ALU.mult, ALU.add),
                e.scalar_tensor_tensor(Y_bf[0][:, :], A_f, -1.0,
                                       ctx.ident[:, :], ALU.mult, ALU.add)),
                waits=xw, inc=[("sVX", 1)], label=f"xy0{n}", dur=450)
            setu(kY, "sVX", f"xy0{n}")
            def kqmm(e, k_dm=k_dm, q_dm=q_dm):
                r = None
                for g in range(G):
                    r = e.matmul(pX[:, :], k_dm[:, g, :], q_dm[:, g, :],
                                 start=(g == 0), stop=(g == G - 1))
                return r
            p.step("te", kqmm,
                   waits=[("sVX", f"qde{n}_{G-1}"), ("sEV", f"qde{n}_{G-2}")]
                   + bw(kX), inc=[("sMM", 1)], label=f"kq{n}", dur=900)
            yield
            ww = [("sMM", f"kq{n}")]
            if prev2 is not None:
                ww.append(("sMM", f"o{prev2}_1"))
            p.step("ve", lambda e, W_bf=W_bf: e.tensor_tensor(
                W_bf, pX[:, :], ctx.triu_i[:, :], ALU.mult),
                waits=ww, inc=[("sVX", 1)], label=f"wm{n}")
            setu(kX, "sVX", f"wm{n}")
            cx = 0
            for it in range(6):
                nw = [("sVX", f"xy0{n}")] if it == 0 else \
                    [("sVX", f"nt{n}_{it-1}v")]
                p.step("te", lambda e, cx=cx, M_bf=M_bf, X_bf=X_bf: e.matmul(
                    pY[:, :], M_bf, X_bf[cx][:, :], start=True, stop=True),
                    waits=nw + bw(kY), inc=[("sMM", 1)], label=f"nt{n}_{it}a",
                    dur=150)
                yield
                p.step("se", lambda e, P_bf=P_bf: e.activation(P_bf, pY[:, :],
                                                               AF.Copy),
                       waits=[("sMM", f"nt{n}_{it}a")], inc=[("sEV", 1)],
                       label=f"nt{n}_{it}p", dur=400)
                setu(kY, "sEV", f"nt{n}_{it}p")
                p.step("te", lambda e, cx=cx, Y_bf=Y_bf, P_bf=P_bf: e.matmul(
                    pX[:, :], Y_bf[cx][:, :], P_bf, start=True, stop=True),
                    waits=[("sEV", f"nt{n}_{it}p")] + bw(kX),
                    inc=[("sMM", 1)], label=f"nt{n}_{it}b", dur=150)
                yield
                p.step("te", lambda e, cx=cx, Y_bf=Y_bf, P_bf=P_bf: e.matmul(
                    pY[:, :], P_bf, Y_bf[cx][:, :], start=True, stop=True),
                    waits=[("sEV", f"nt{n}_{it}p")] + bw(kY),
                    inc=[("sMM", 1)], dur=150,
                    label=(f"nt{n}_5c" if it == 5 else f"nt{n}_{it}c"))
                yield
                p.step("ve", lambda e, cx=cx, X_bf=X_bf, Y_bf=Y_bf: (
                    e.scalar_tensor_tensor(X_bf[1 - cx][:, :], X_bf[cx][:, :], 2.0,
                                           pX[:, :], ALU.mult, ALU.subtract),
                    e.scalar_tensor_tensor(Y_bf[1 - cx][:, :], Y_bf[cx][:, :], 2.0,
                                           pY[:, :], ALU.mult, ALU.subtract)),
                    waits=[("sMM", f"nt{n}_{it}b"),
                           ("sMM", f"nt{n}_5c" if it == 5 else f"nt{n}_{it}c")],
                    inc=[("sVX", 1)], label=f"nt{n}_{it}v", dur=500)
                setu(kX, "sVX", f"nt{n}_{it}v")
                setu(kY, "sVX", f"nt{n}_{it}v")
                cx = 1 - cx

        def b_stage(pos):
            c, b = insts[pos]
            i = idx(pos)
            n = f"{i}"
            sl = pos % 4
            ip = pos % 2
            ceven = (pos // 2) % 2 == 0
            k_tm = k_tm4[:, sl, :]
            v_tm = v_tm4[:, sl, :]
            q_dm = q_dm2[:, sl, :, :]
            k_dm = k_dm2[:, sl, :, :]
            U_bf = U_bf2[:, sl, :]
            RHS_bf = RHS_bf2[:, ip, :]
            O_bf = O_bf2[:, ip, :]
            vb_f = vb_f2[:, ip, :]
            W_bf = W_bf2[:, ip, :]
            X_bf = X_bf2[ip]
            prev2 = idx(pos - 2) if pos >= 2 else None
            nfin = f"nt{n}_5v"
            if not ceven:
                sl0 = (pos - 2) % 4
                n0 = f"{idx(pos - 2)}"
                k_dm0 = k_dm2[:, sl0, :, :]
                k_tm0 = k_tm4[:, sl0, :]
                U_bf0 = U_bf2[:, sl0, :]
                def ptmm(e, k_dm=k_dm, k_dm0=k_dm0):
                    r = None
                    for g in range(G):
                        r = e.matmul(pW[2][:, 0:128], k_dm0[:, g, :],
                                     k_dm[:, g, :], start=(g == 0),
                                     stop=(g == G - 1))
                    return r
                p.step("te", ptmm,
                       waits=[("sEV", f"kde{n}_{G-1}"), ("sVX", f"kde{n}_{G-2}")]
                       + bw("pW2"), inc=[("sMM", 1)], label=f"pt{n}", dur=900)
                yield

                def qtmm(e, q_dm=q_dm, k_dm0=k_dm0):
                    r = None
                    for g in range(G):
                        r = e.matmul(pW[3][:, 0:128], k_dm0[:, g, :],
                                     q_dm[:, g, :], start=(g == 0),
                                     stop=(g == G - 1))
                    return r
                p.step("te", qtmm,
                       waits=[("sVX", f"qde{n}_{G-1}"), ("sEV", f"qde{n}_{G-2}")]
                       + bw("pW3"), inc=[("sMM", 1)], label=f"ptq{n}", dur=900)
                ptw = [("sMM", f"pt{n}")]
                if pos >= 4:
                    ptw.append(("sMM", f"ks{idx(pos-4)}_1"))
                p.step("se", lambda e: e.activation(Pt_bf[:, :], pW[2][:, 0:128],
                                                    AF.Copy),
                       waits=ptw, inc=[("sEV", 1)], label=f"pte{n}")
                setu("pW2", "sEV", f"pte{n}")
                qtw = [("sMM", f"ptq{n}")]
                if pos >= 4:
                    qtw.append(("sMM", f"o{idx(pos-4)}_1"))
                p.step("ve", lambda e: e.tensor_copy(Qt_bf[:, :], pW[3][:, 0:128]),
                       waits=qtw, inc=[("sVX", 1)], label=f"qte{n}")
                setu("pW3", "sVX", f"qte{n}")
                yield
            for hh in range(2):
                def ksmm(e, hh=hh, b=b, k_dm=k_dm, ceven=ceven, pos=pos):
                    r = None
                    for g in range(G):
                        r = e.matmul(pW[hh][:, :], k_dm[:, g, :],
                                     S_b[:, b, g, hh * 512:(hh + 1) * 512],
                                     start=(g == 0),
                                     stop=(ceven and g == G - 1))
                    if not ceven:
                        r = e.matmul(pW[hh][:, :], Pt_bf[:, :],
                                     U_bf2[:, (pos - 2) % 4,
                                           hh * 512:(hh + 1) * 512],
                                     start=False, stop=True)
                    return r
                ksw = [Slast[b], ("sEV", f"kde{n}_{G-1}"),
                       ("sVX", f"kde{n}_{G-2}")] + bw(f"pW{hh}")
                if not ceven:
                    ksw += [("sEV", f"pte{n}"), ("sEV", f"ue{n0}")]
                p.step("te", ksmm, waits=ksw,
                       inc=[("sMM", 1)], label=f"ks{n}_{hh}", dur=1900)
                yield
            vw = [(f"av{sl}", f"vl{n}")]
            if prev2 is not None:
                vw.append(("sVX", f"rhs{prev2}"))
            p.step("ve", lambda e, i=i, vb_f=vb_f, v_tm=v_tm: e.tensor_scalar_mul(
                vb_f, v_tm, ctx.beta_st[:, i:i + 1]),
                waits=vw, inc=[("sVX", 1)], label=f"vb{i}")
            rhw = [("sVX", f"vb{i}"), ("sMM", f"ks{n}_0"), ("sMM", f"ks{n}_1")]
            if prev2 is not None:
                rhw.append(("sMM", f"u{prev2}_1"))
            p.step("ve", lambda e, i=i, RHS_bf=RHS_bf, vb_f=vb_f: (
                e.scalar_tensor_tensor(RHS_bf[:, 0:512], pW[0][:, :],
                                       ctx.betan_st[:, i:i + 1], vb_f[:, 0:512],
                                       ALU.mult, ALU.add),
                e.scalar_tensor_tensor(RHS_bf[:, 512:1024], pW[1][:, :],
                                       ctx.betan_st[:, i:i + 1], vb_f[:, 512:1024],
                                       ALU.mult, ALU.add)),
                waits=rhw, inc=[("sVX", 1)], label=f"rhs{n}", dur=500)
            setu("pW0", "sVX", f"rhs{n}")
            setu("pW1", "sVX", f"rhs{n}")
            yield
            for hh in range(2):
                p.step("te", lambda e, hh=hh, X_bf=X_bf, RHS_bf=RHS_bf:
                       e.matmul(pW[2 + hh][:, :], X_bf[0][:, :],
                                RHS_bf[:, hh * 512:(hh + 1) * 512],
                                start=True, stop=True),
                       waits=[("sVX", f"rhs{n}"), ("sVX", nfin)] + bw(f"pW{2+hh}"),
                       inc=[("sMM", 1)], label=f"u{i}_{hh}")
                yield
            uew = [("sMM", f"u{i}_0"), ("sMM", f"u{i}_1")]
            if pos >= 4:
                uew.append(("sMM", f"su{idx(su_of(pos-4))}_{G-1}"))
            p.step("se", lambda e, U_bf=U_bf: (
                e.activation(U_bf[:, 0:512], pW[2][:, :], AF.Copy),
                e.activation(U_bf[:, 512:1024], pW[3][:, :], AF.Copy)),
                waits=uew, inc=[("sEV", 1)], label=f"ue{n}", dur=1100)
            setu("pW2", "sEV", f"ue{n}")
            setu("pW3", "sEV", f"ue{n}")
            for hh in range(2):
                def omm(e, hh=hh, b=b, q_dm=q_dm, W_bf=W_bf, U_bf=U_bf,
                        ceven=ceven, pos=pos):
                    r = None
                    for g in range(G):
                        r = e.matmul(pW[hh][:, :], q_dm[:, g, :],
                                     S_b[:, b, g, hh * 512:(hh + 1) * 512],
                                     start=(g == 0), stop=False)
                    if not ceven:
                        r = e.matmul(pW[hh][:, :], Qt_bf[:, :],
                                     U_bf2[:, (pos - 2) % 4,
                                           hh * 512:(hh + 1) * 512],
                                     start=False, stop=False)
                    r = e.matmul(pW[hh][:, :], W_bf,
                                 U_bf[:, hh * 512:(hh + 1) * 512],
                                 start=False, stop=True)
                    return r
                ow = [("sVX", f"wm{n}"), ("sEV", f"ue{n}"),
                      ("sVX", f"qde{n}_{G-1}"), ("sEV", f"qde{n}_{G-2}")]
                if not ceven:
                    ow += [("sVX", f"qte{n}"), ("sEV", f"ue{n0}")]
                p.step("te", omm, waits=ow + bw(f"pW{hh}"),
                       inc=[("sMM", 1)], label=f"o{n}_{hh}", dur=2000)
                yield
            oew = [("sMM", f"o{n}_0"), ("sMM", f"o{n}_1")]
            if prev2 is not None:
                oew.append((f"aos{ip}", f"ost{prev2}"))
            p.step("se", lambda e, O_bf=O_bf: (
                e.activation(O_bf[:, 0:512], pW[0][:, :], AF.Copy),
                e.activation(O_bf[:, 512:1024], pW[1][:, :], AF.Copy)),
                waits=oew, inc=[("sEV", 1)], label=f"oe{n}", dur=1100)
            setu("pW0", "sEV", f"oe{n}")
            setu("pW1", "sEV", f"oe{n}")
            p.dma("se", io["o_s"][b, c, :, :], O_bf, sem=f"aos{ip}",
                  waits=[("sEV", f"oe{n}")], label=f"ost{i}")
            if ceven:
                return
            for g in range(G):
                b0i, b1i = (2, 3) if g % 2 == 0 else (0, 1)

                def summ(e, g=g, b0i=b0i, b1i=b1i, k_tm=k_tm, k_tm0=k_tm0,
                         U_bf=U_bf, U_bf0=U_bf0):
                    e.matmul(pW[b0i][:, :], k_tm0[:, g * 128:(g + 1) * 128],
                             U_bf0[:, 0:512], start=True, stop=False)
                    e.matmul(pW[b0i][:, :], k_tm[:, g * 128:(g + 1) * 128],
                             U_bf[:, 0:512], start=False, stop=True)
                    e.matmul(pW[b1i][:, :], k_tm0[:, g * 128:(g + 1) * 128],
                             U_bf0[:, 512:1024], start=True, stop=False)
                    return e.matmul(pW[b1i][:, :], k_tm[:, g * 128:(g + 1) * 128],
                                    U_bf[:, 512:1024], start=False, stop=True)
                suw = [("sEV", f"ue{n}"), ("sEV", f"ue{n0}")] \
                    + bw(f"pW{b0i}") + bw(f"pW{b1i}")
                if g == 0:
                    suw += [(f"ak{sl}", f"kl{n}"), (f"ak{sl0}", f"kl{n0}")]
                if g < 2:
                    suw += [("sEV", f"oe{n}")]
                p.step("te", summ, waits=suw, inc=[("sMM", 1)], label=f"su{n}_{g}",
                       dur=2200)
                p.step("ve", lambda e, g=g, b=b, b0i=b0i, b1i=b1i: (
                    e.tensor_tensor(S_f[:, b, g, 0:512], S_f[:, b, g, 0:512],
                                    pW[b0i][:, :], ALU.add),
                    e.tensor_tensor(S_f[:, b, g, 512:1024], S_f[:, b, g, 512:1024],
                                    pW[b1i][:, :], ALU.add)),
                    waits=[("sMM", f"su{n}_{g}")], inc=[("sVX", 1)],
                    label=f"sa{n}_{g}", dur=1300)
                setu(f"pW{b0i}", "sVX", f"sa{n}_{g}")
                setu(f"pW{b1i}", "sVX", f"sa{n}_{g}")
                p.step("se", lambda e, g=g, b=b: e.activation(
                    S_b[:, b, g, :], S_f[:, b, g, :], AF.Copy),
                    waits=[("sVX", f"sa{n}_{g}")], inc=[("sEV", 1)],
                    label=f"sc{n}_{g}", dur=1200)
                yield
            Slast[b] = ("sEV", f"sc{n}_{G-1}")

        for pos in range(NI):
            for gname, gen in (("L", a_load(pos)), ("A", a_comp(pos)),
                               ("B", b_stage(pos))):
                p.cur_group = (gname, pos)
                for _ in gen:
                    pass
        p.cur_group = None
        p.quiesce("p2a")
        p.emit(reorder=True)





def emit_p2bi(ctx):
    nc, io = ctx.nc, ctx.io
    with ExitStack() as st:
        lsb = lambda n, s, d: st.enter_context(nc.sbuf_tensor(n, s, d))
        lps = lambda n, s: st.enter_context(nc.psum_tensor(n, s, F32))
        wo_sb = lsb("wo_sb", [128, G, D], BF16)
        o_tm = lsb("bio", [128, 4, D], BF16)
        x0t = lsb("bix", [128, 4, D], F32)
        o_dm2 = lsb("biod", [128, 2, G, C], BF16)
        lnt2 = lsb("biln", [128, 2, D], F32)
        ax_f2 = lsb("biax", [128, 2, D], F32)
        scrap = lsb("bisc", [128, D], BF16)
        stat2 = lsb("bist", [128, 2, 4], F32)
        pWo = [lps("biw0", [128, 512]), lps("biw1", [128, 512])]
        ptp = [lps(f"bit{j}", [128, 128]) for j in range(4)]
        ptpb = [t[:, :].bitcast(BF16)[:, 0:128] for t in ptp]

        p = Prog(ctx)
        p.mark("bi0")
        p.dma("pl", wo_sb[:, :, :], io["wo"].rearrange("(g p) d -> p g d", p=128),
              sem="wo", label="biwl", xfer=12000)
        pu = {}

        def bw(key):
            return [pu[key]] if key in pu else []

        for i in range(NCHUNKS):
            b, c = divmod(i, NCH)
            n = f"{i}"
            sl = i % 4
            s2 = i % 2
            p.cur_group = ("I", i)
            o_dm = o_dm2[:, s2, :, :]
            lnt = lnt2[:, s2, :]
            ax_f = ax_f2[:, s2, :]
            stat = stat2[:, s2, :]
            lw = [("sMM", f"biot{i-4}_{G-1}")] if i >= 4 else []
            p.dma("sp", o_tm[:, sl, :], io["o_s"][b, c, :, :], sem=f"bo{i % 4}",
                  waits=lw, label=f"bio{n}", xfer=2500)
            xw = [("sEV", f"biln{i-4}")] if i >= 4 else []
            p.dma("sp", x0t[:, sl, :], io["x"][b, c * C:(c + 1) * C, :],
                  sem=f"bx{i % 4}", waits=xw, label=f"bix{n}", xfer=4000)
            lnw = [(f"bx{i % 4}", f"bix{n}")]
            if i >= 2:
                lnw.append(("sVX", f"biax{i-2}"))
            p.step("se", lambda e, sl=sl, i=i, lnt=lnt: e.activation(
                lnt, x0t[:, sl, :], AF.Identity,
                scale=ctx.r1_st[:, i:i + 1], bias=ctx.b1_st[:, i:i + 1]),
                waits=lnw, inc=[("sEV", 1)], label=f"biln{n}", dur=1150)
            for g in range(G):
                bk = g % 4
                p.step("te", lambda e, g=g, sl=sl, bk=bk: e.transpose(
                    ptpb[bk], o_tm[:, sl, g * 128:(g + 1) * 128], ctx.ident_bf[:, :]),
                    waits=[(f"bo{i % 4}", f"bio{n}")] + bw(f"tp{bk}"),
                    inc=[("sMM", 1)], label=f"biot{i}_{g}", dur=120)
                ew = [("sMM", f"biot{i}_{g}")]
                if g < 2 and i >= 2:
                    ew.append(("sMM", f"biwo{i-2}_1"))
                sm = evac(p, g, o_dm[:, g, :], ptpb[bk],
                          waits=ew, label=f"biod{n}_{g}")
                pu[f"tp{bk}"] = (sm, f"biod{n}_{g}")
            for hh in range(2):
                def womm(e, hh=hh, o_dm=o_dm):
                    r = None
                    for g in range(G):
                        r = e.matmul(pWo[hh][:, :], o_dm[:, g, :],
                                     wo_sb[:, g, hh * 512:(hh + 1) * 512],
                                     start=(g == 0), stop=(g == G - 1))
                    return r
                p.step("te", womm,
                       waits=[("sVX", f"biod{n}_{G-1}"), ("sEV", f"biod{n}_{G-2}"),
                              ("wo", "biwl")] + bw(f"w{hh}"),
                       inc=[("sMM", 1)], label=f"biwo{n}_{hh}", dur=1800)
            p.step("ve", lambda e, ax_f=ax_f, lnt=lnt: (
                e.tensor_tensor(ax_f[:, 0:512], lnt[:, 0:512], pWo[0][:, :], ALU.add),
                e.tensor_tensor(ax_f[:, 512:1024], lnt[:, 512:1024], pWo[1][:, :],
                                ALU.add)),
                waits=[("sEV", f"biln{n}"), ("sMM", f"biwo{n}_0"),
                       ("sMM", f"biwo{n}_1")]
                + ([("sEV", f"bir2{i-2}"), (f"bax{s2}", f"biaxs{i-2}")]
                   if i >= 2 else []),
                inc=[("sVX", 1)], label=f"biax{i}", dur=1100)
            pu["w0"] = ("sVX", f"biax{i}")
            pu["w1"] = ("sVX", f"biax{i}")
            sqw = [("sVX", f"biax{i}")]
            if i >= 2:
                sqw.append(("sVX", f"bir2v{i-2}"))
            p.chain("se", [
                lambda e, ax_f=ax_f, stat=stat: e.activation(
                    scrap[:, :], ax_f, AF.Square, accum_out=stat[:, 0:1]),
                lambda e, stat=stat: e.activation(stat[:, 1:2], stat[:, 0:1], AF.Ln,
                                                  scale=1.0 / D, bias=EPS_RMS),
                lambda e, stat=stat: e.activation(stat[:, 1:2], stat[:, 1:2], AF.Exp,
                                                  scale=-0.5)],
                waits=sqw, inc=[("sEV", 1)], label=f"bir2{n}")
            p.step("ve", lambda e, i=i, stat=stat: e.tensor_copy(
                ctx.r2_st[:, i:i + 1], stat[:, 1:2]),
                waits=[("sEV", f"bir2{n}")], inc=[("sVX", 1)], label=f"bir2v{i}")
            p.dma("se", io["ax_s"][b, c * C:(c + 1) * C, :], ax_f, sem=f"bax{s2}",
                  waits=[("sVX", f"biax{i}")], label=f"biaxs{i}", xfer=4000)
        p.cur_group = None
        p.quiesce("bi")
        p.emit(reorder=True)




def emit_p2bii(ctx):
    nc, io = ctx.nc, ctx.io
    with ExitStack() as st:
        lsb = lambda n, s, d: st.enter_context(nc.sbuf_tensor(n, s, d))
        lps = lambda n, s: st.enter_context(nc.psum_tensor(n, s, F32))
        wg_sb = lsb("wg_sb", [128, G, HID], BF16)
        wu_sb = lsb("wu_sb", [128, G, HID], BF16)
        wd_sb = lsb("wd_sb", [128, HG, D], BF16)

        with ExitStack() as st2:
            wstage = st2.enter_context(nc.sbuf_tensor("wstage2", [128, G, HID], BF16))
            p = Prog(ctx)
            p.mark("bw0")
            for wi, (srct, dst) in enumerate([(io["awg"], wg_sb), (io["awu"], wu_sb)]):
                w = [("sEV", f"bwf{wi-1}")] if wi else []
                p.dma("pl", wstage[:, :, :], srct.rearrange("(g p) d -> p g d", p=128),
                      sem="bw2l", waits=w, label=f"bwl{wi}")

                def fold(e, dst=dst):
                    r = None
                    for g in range(G):
                        r = e.activation(dst[:, g, :], wstage[:, g, :], AF.Copy,
                                         scale=ctx.n2w_sb[:, g:g + 1])
                    return r
                p.step("se", fold, waits=[("bw2l", f"bwl{wi}")], inc=[("sEV", 1)],
                       label=f"bwf{wi}")
            p.dma("pl", wd_sb[:, :, :], io["awd"].rearrange("(g p) d -> p g d", p=128),
                  sem="bw2l", waits=[("sEV", "bwf1")], label="bwd")
            p.quiesce("bw")
            p.emit()

        ax_t = lsb("c2ax", [128, 4, D], F32)
        x0t = lsb("c2x0", [128, 4, D], F32)
        h2_2 = lsb("c2h2", [128, 2, D], BF16)
        h2_dm2s = lsb("c2hd", [128, 2, G, 4 * C], BF16)
        sil2 = lsb("c2sl", [128, 2, 512], F32)
        m_dm2s = lsb("c2md", [128, 2, HG, 4 * C], BF16)
        x1_f2 = lsb("c2x1", [128, 2, D], F32)
        dif_f2 = lsb("c2df", [128, 2, D], F32)
        scrap = lsb("c2sc", [128, D], BF16)
        stat2 = lsb("c2st", [128, 2, 6], F32)
        pGa = [lps("c2g0", [128, 512]), lps("c2g1", [128, 512])]
        pUa = [lps("c2u0", [128, 512]), lps("c2u1", [128, 512])]
        pDn = [lps("c2d0", [128, 512]), lps("c2d1", [128, 512])]
        ptp = [lps("c2t0", [128, 128]), lps("c2t1", [128, 128])]
        ptpb = [t[:, :].bitcast(BF16)[:, 0:128] for t in ptp]

        p = Prog(ctx)
        p.mark("c20")
        pu = {}

        def bw(key):
            return [pu[key]] if key in pu else []

        NSC = NCHUNKS // 4
        for j in range(NSC):
            for t in range(4):
                i = 4 * j + t
                b, c = divmod(i, NCH)
                n = f"{i}"
                sl = i % 4
                p.cur_group = ("M", i)
                h2 = h2_2[:, i % 2, :]
                lw = ([("sEV", f"c2h2{i-4}"), ("sVX", f"c2x1c{i-4}")]
                      if i >= 4 else [])
                p.dma("sp", ax_t[:, sl, :], io["ax_s"][b, c * C:(c + 1) * C, :],
                      sem=f"cax{i % 4}", waits=lw, label=f"c2al{n}", xfer=4000)
                xw = [("sVX", f"c2x1c{i-4}")] if i >= 4 else []
                p.dma("sp", x0t[:, sl, :], io["x"][b, c * C:(c + 1) * C, :],
                      sem=f"ccx{i % 4}", waits=xw, label=f"c2xl{n}", xfer=4000)
                hw = [(f"cax{i % 4}", f"c2al{n}")]
                if i >= 2:
                    hw.append(("sMM", f"c2ht{i-2}_{G-1}"))
                p.step("se", lambda e, sl=sl, i=i, h2=h2: e.activation(
                    h2, ax_t[:, sl, :], AF.Copy, scale=ctx.r2_st[:, i:i + 1]),
                    waits=hw, inc=[("sEV", 1)], label=f"c2h2{i}", dur=1100)
                for g in range(G):
                    bk = g % 2
                    p.step("te", lambda e, g=g, bk=bk, h2=h2: e.transpose(
                        ptpb[bk], h2[:, g * 128:(g + 1) * 128],
                        ctx.ident_bf[:, :]),
                        waits=[("sEV", f"c2h2{i}")] + bw(f"tp{bk}"),
                        inc=[("sMM", 1)], label=f"c2ht{i}_{g}", dur=120)
                    ew = [("sMM", f"c2ht{i}_{g}")]
                    if g < 2 and j >= 2:
                        ew.append(("sMM", f"c2gu{j-2}_{HG-1}"))
                    sm = evac(p, g, h2_dm2s[:, j % 2, g, t * C:(t + 1) * C],
                              ptpb[bk], waits=ew, label=f"c2hd{n}_{g}")
                    pu[f"tp{bk}"] = (sm, f"c2hd{n}_{g}")
            p.cur_group = ("S", j)
            hdw = []
            for t in range(4):
                hdw += [("sVX", f"c2hd{4*j+t}_{G-1}"),
                        ("sEV", f"c2hd{4*j+t}_{G-2}")]
            for hg in range(HG):
                bk = hg % 2

                def gmm(e, hg=hg, bk=bk, j=j):
                    r = None
                    for g in range(G):
                        r = e.matmul(pGa[bk][:, :],
                                     wg_sb[:, g, hg * 128:(hg + 1) * 128],
                                     h2_dm2s[:, j % 2, g, :],
                                     start=(g == 0), stop=(g == G - 1))
                    for g in range(G):
                        r = e.matmul(pUa[bk][:, :],
                                     wu_sb[:, g, hg * 128:(hg + 1) * 128],
                                     h2_dm2s[:, j % 2, g, :],
                                     start=(g == 0), stop=(g == G - 1))
                    return r
                p.step("te", gmm,
                       waits=(hdw if hg == 0 else []) + bw(f"g{bk}")
                       + bw(f"u{bk}"),
                       inc=[("sMM", 1)], label=f"c2gu{j}_{hg}", dur=3400)
                sw = [("sMM", f"c2gu{j}_{hg}")] + bw(f"sil{bk}")
                p.step("se", lambda e, bk=bk: e.activation(
                    sil2[:, bk, :], pGa[bk][:, :], AF.Silu),
                    waits=sw, inc=[("sEV", 1)], label=f"c2si{j}_{hg}", dur=550)
                pu[f"g{bk}"] = ("sEV", f"c2si{j}_{hg}")
                mw = [("sEV", f"c2si{j}_{hg}")]
                if hg == 0 and j >= 2:
                    mw.append(("sMM", f"c2dn{4*(j-1)-1}_1"))
                p.step("ve", lambda e, hg=hg, bk=bk, j=j: e.tensor_tensor(
                    m_dm2s[:, j % 2, hg, :], sil2[:, bk, :], pUa[bk][:, :],
                    ALU.mult),
                    waits=mw, inc=[("sVX", 1)], label=f"c2m{j}_{hg}", dur=550)
                pu[f"u{bk}"] = ("sVX", f"c2m{j}_{hg}")
                pu[f"sil{bk}"] = ("sVX", f"c2m{j}_{hg}")
            for t in range(4):
                i = 4 * j + t
                b, c = divmod(i, NCH)
                n = f"{i}"
                sl = i % 4
                s2 = i % 2
                p.cur_group = ("D", i)
                x1_f = x1_f2[:, s2, :]
                dif_f = dif_f2[:, s2, :]
                stat = stat2[:, s2, :]
                for hh in range(2):
                    def dmm(e, hh=hh, t=t, j=j):
                        r = None
                        for hg in range(HG):
                            r = e.matmul(pDn[hh][:, :],
                                         m_dm2s[:, j % 2, hg, t * C:(t + 1) * C],
                                         wd_sb[:, hg, hh * 512:(hh + 1) * 512],
                                         start=(hg == 0), stop=(hg == HG - 1))
                        return r
                    p.step("te", dmm,
                           waits=[("sVX", f"c2m{j}_{HG-1}")] + bw(f"d{hh}"),
                           inc=[("sMM", 1)], label=f"c2dn{n}_{hh}", dur=3400)
                p.chain("ve", [
                    lambda e, sl=sl, x1_f=x1_f: e.tensor_tensor(
                        x1_f, x0t[:, sl, :], ax_t[:, sl, :], ALU.add),
                    lambda e, x1_f=x1_f: e.tensor_tensor(
                        x1_f[:, 0:512], x1_f[:, 0:512], pDn[0][:, :], ALU.add),
                    lambda e, x1_f=x1_f: e.tensor_tensor(
                        x1_f[:, 512:1024], x1_f[:, 512:1024], pDn[1][:, :],
                        ALU.add)],
                    waits=[(f"ccx{i % 4}", f"c2xl{n}"), ("sMM", f"c2dn{n}_0"),
                           ("sMM", f"c2dn{n}_1")]
                    + ([(f"cx1{s2}", f"c2xs{i-2}"), ("sEV", f"c2dif{i-2}")]
                       if i >= 2 else []),
                    inc=[("sVX", 1)], label=f"c2x1c{i}")
                pu["d0"] = ("sVX", f"c2x1c{i}")
                pu["d1"] = ("sVX", f"c2x1c{i}")
                s1w = [("sVX", f"c2x1c{i}")]
                if i >= 2:
                    s1w.append(("sVX", f"c2mv{i-2}"))
                p.step("ve", lambda e, stat=stat, x1_f=x1_f: e.reduce_sum(
                    stat[:, 0:1], x1_f, AX.X),
                    waits=s1w, inc=[("sVX", 1)], label=f"c2s1{n}", dur=450)
                p.step("se", lambda e, stat=stat, x1_f=x1_f: e.activation(
                    scrap[:, :], x1_f, AF.Square, accum_out=stat[:, 1:2]),
                    waits=[("sVX", f"c2x1c{i}")] + ([("sVX", f"c2mv{i-2}")]
                                                    if i >= 2 else []),
                    inc=[("sEV", 1)], label=f"c2s2{n}", dur=1150)
                mvw = [("sVX", f"c2s1{n}"), ("sEV", f"c2s2{n}")]
                if i >= 2:
                    mvw.append(("sEV", f"c2dif{i-2}"))
                p.chain("ve", [
                    lambda e, i=i, stat=stat: e.tensor_scalar_mul(
                        ctx.mu2_st[:, i:i + 1], stat[:, 0:1], 1.0 / D),
                    lambda e, i=i, stat=stat: e.tensor_scalar_mul(
                        stat[:, 3:4], ctx.mu2_st[:, i:i + 1], -1.0),
                    lambda e, i=i, stat=stat: e.scalar_tensor_tensor(
                        stat[:, 2:3], ctx.mu2_st[:, i:i + 1], -1.0,
                        ctx.mu2_st[:, i:i + 1], ALU.mult, ALU.mult),
                    lambda e, i=i, stat=stat: e.scalar_tensor_tensor(
                        ctx.var2_st[:, i:i + 1], stat[:, 1:2], 1.0 / D,
                        stat[:, 2:3], ALU.mult, ALU.add)],
                    waits=mvw, inc=[("sVX", 1)], label=f"c2mv{i}")
                dfw = [("sVX", f"c2mv{i}")]
                if i >= 2:
                    dfw.append(("sVX", f"c2dm{i-2}"))
                p.step("se", lambda e, dif_f=dif_f, x1_f=x1_f, stat=stat:
                       e.activation(dif_f, x1_f, AF.Identity, scale=1.0,
                                    bias=stat[:, 3:4]),
                       waits=dfw, inc=[("sEV", 1)], label=f"c2dif{i}", dur=1150)
                p.step("ve", lambda e, i=i, dif_f=dif_f: e.tensor_reduce(
                    ctx.mx_st[:, i:i + 1], dif_f, AX.X, ALU.max,
                    apply_absolute_value=True),
                    waits=[("sEV", f"c2dif{i}")], inc=[("sVX", 1)],
                    label=f"c2dm{i}", dur=500)
                p.dma("se", io["x1_s"][b, c * C:(c + 1) * C, :], x1_f,
                      sem=f"cx1{s2}", waits=[("sVX", f"c2x1c{i}")],
                      label=f"c2xs{i}", xfer=4000)
        p.cur_group = None
        p.quiesce("c2")
        p.emit(reorder=True)




def emit_p3(ctx):
    nc, io = ctx.nc, ctx.io
    SLICES = [(s * 512, min(II, (s + 1) * 512)) for s in range((II + 511) // 512)]
    with ExitStack() as st:
        lsb = lambda n, s, d: st.enter_context(nc.sbuf_tensor(n, s, d))
        lps = lambda n, s: st.enter_context(nc.psum_tensor(n, s, F32))

        st0 = ExitStack()
        pcol = st0.enter_context(nc.psum_tensor("p3c", [128, 8], F32))
        ptmp = st0.enter_context(nc.psum_tensor("p3pt", [128, 128], F32))
        p = Prog(ctx)
        p.mark("p30")
        p.chain("se", [
            lambda e: e.activation(ctx.r2_st[:, :], ctx.var2_st[:, :], AF.Ln,
                                   bias=EPS_LN),
            lambda e: e.activation(ctx.r2_st[:, :], ctx.r2_st[:, :], AF.Exp,
                                   scale=-0.5)],
            inc=[("sEV", 1)], label="r2p")
        p.chain("ve", [
            lambda e: e.tensor_tensor(ctx.b1_st[:, :], ctx.mx_st[:, :],
                                      ctx.r2_st[:, :], ALU.mult),
            lambda e: e.tensor_reduce(ctx.s1_col[:, :], ctx.b1_st[:, :], AX.X,
                                      ALU.max)],
            waits=[("sEV", "r2p")], inc=[("sVX", 1)], label="lm1")
        p.step("te", lambda e: e.transpose(ptmp[0:1, 0:128], ctx.s1_col[:, :],
                                           ctx.ident[:, :]),
               waits=[("sVX", "lm1")], inc=[("sMM", 1)], label="lmt1")
        p.step("ve", lambda e: e.tensor_reduce(ctx.ccb[0:1, 0:1],
                                               ptmp[0:1, 0:128], AX.X, ALU.max),
               waits=[("sMM", "lmt1")], inc=[("sVX", 1)], label="lmr1")
        p.step("ve", lambda e: e.tensor_copy(ctx.ccb[0:1, :],
                                             ctx.ccb[0:1, 0:1].broadcast_to([1, 128])),
               waits=[("sVX", "lmr1")], inc=[("sVX", 1)], label="ccf1")
        p.dma("sp", io["cc1_in"].ap(), ctx.ccb[0:1, :], sem="cc",
              waits=[("sVX", "ccf1")], label="cci1")
        p.step("pl", lambda e: e.collective_compute(
            "AllReduce", ALU.max, replica_groups=[list(range(N_CORES))],
            ins=[io["cc1_in"].ap().opt()], outs=[io["cc1_out"].ap().opt()]),
            waits=[("cc", "cci1")], inc=[("sCC", 1)], label="ccr1")
        p.dma("sp", ctx.sig_cell[0:1, 5:6], io["cc1_out"].ap()[0:1, 0:1], sem="cc",
              waits=[("sCC", "ccr1")], label="ccl1")
        p.chain("ve", [
            lambda e: e.tensor_tensor(ctx.sig_cell[0:1, 6:7], ctx.sig_cell[0:1, 5:6],
                                      ctx.sig_cell[0:1, 0:1], ALU.mult),
            lambda e: e.tensor_scalar_mul(ctx.sig_cell[0:1, 6:7],
                                          ctx.sig_cell[0:1, 6:7], 1.0 / 127.0),
            lambda e: e.reciprocal(ctx.sig_cell[0:1, 7:8], ctx.sig_cell[0:1, 6:7])],
            waits=[("cc", "ccl1")], inc=[("sVX", 1)], label="s1c")
        p.step("te", lambda e: (
            e.matmul(pcol[:, 0:1], ctx.ones_row[:, :], ctx.sig_cell[0:1, 6:7],
                     start=True, stop=True),
            e.matmul(pcol[:, 1:2], ctx.ones_row[:, :], ctx.sig_cell[0:1, 7:8],
                     start=True, stop=True)),
            waits=[("sVX", "s1c")], inc=[("sMM", 1)], label="s1b")
        p.step("ve", lambda e: (
            e.tensor_copy(ctx.s1_col[:, :], pcol[:, 0:1]),
            e.tensor_copy(ctx.is1_col[:, :], pcol[:, 1:2])),
            waits=[("sMM", "s1b")], inc=[("sVX", 1)], label="s1e")
        p.step("ve", lambda e: e.tensor_scalar_mul(ctx.qs_st[:, :], ctx.r2_st[:, :],
                                                   ctx.is1_col[:, :]),
               waits=[("sVX", "s1e")], inc=[("sVX", 1)], label="qsc")
        p.chain("ve", [
            lambda e: e.tensor_tensor(ctx.qb_st[:, :], ctx.mu2_st[:, :],
                                      ctx.qs_st[:, :], ALU.mult),
            lambda e: e.tensor_scalar_mul(ctx.qb_st[:, :], ctx.qb_st[:, :], -1.0)],
            waits=[("sVX", "qsc")], inc=[("sVX", 1)], label="qbc")
        p.quiesce("p3p")
        p.emit()
        st0.close()

        with ExitStack() as st2:
            l2 = lambda n, s, d: st2.enter_context(nc.sbuf_tensor(n, s, d))
            lps2 = lambda n, s: st2.enter_context(nc.psum_tensor(n, s, F32))
            wgT = l2("wgT", [128, G, II], BF16)
            wuT = l2("wuT", [128, G, II], BF16)
            x1t = l2("p3x1", [128, 4, D], F32)
            u_f2 = l2("p3u", [128, 2, D], F32)
            z16_2 = l2("p3z", [128, 2, D], F16)
            xq2 = l2("p3xq", [128, 2, D], BF16)
            xq_dm2s = l2("p3xd", [128, 2, G, 4 * C], BF16)
            sil2 = l2("p3sl", [128, 2, 512], F32)
            t_scr = l2("p3tf", [128, 2, 512], F32)
            t_bf2s = l2("p3tb", [128, 2, IG, 4 * C], BF16)
            mxc = l2("p3mx", [128, IG], F32)
            pGa = [lps2("p3g0", [128, 512]), lps2("p3g1", [128, 512])]
            pUa = [lps2("p3u0", [128, 512]), lps2("p3u1", [128, 512])]
            ptp = [lps2(f"p3t{j}", [128, 128]) for j in range(4)]
            ptpb = [t[:, :].bitcast(BF16)[:, 0:128] for t in ptp]
            NS = len(SLICES)

            p = Prog(ctx)
            p.mark("p3a0")
            for g in range(G):
                p.dma("sp", wgT[:, g, :], io["wgq_s"][:, g * 128:(g + 1) * 128],
                      sem="wgt", transpose=True, label=f"wgT{g}", xfer=6000)
                p.dma("sp", wuT[:, g, :], io["wuq_s"][:, g * 128:(g + 1) * 128],
                      sem="wgt", transpose=True, label=f"wuT{g}", xfer=6000)
            pu = {}

            def bw(key):
                return [pu[key]] if key in pu else []

            NSC3 = NCHUNKS // 4
            for j in range(NSC3):
                for t in range(4):
                    i = 4 * j + t
                    b, c = divmod(i, NCH)
                    n = f"{i}"
                    sl = i % 4
                    s2 = i % 2
                    p.cur_group = ("A", i)
                    u_f = u_f2[:, s2, :]
                    z16 = z16_2[:, s2, :]
                    xq = xq2[:, s2, :]
                    lw = [("sEV", f"p3uf{i-4}")] if i >= 4 else []
                    p.dma("sp", x1t[:, sl, :], io["x1_s"][b, c * C:(c + 1) * C, :],
                          sem=f"px{i % 4}", waits=lw, label=f"p3xl{n}", xfer=4000)
                    ufw = [(f"px{i % 4}", f"p3xl{n}")]
                    if i >= 2:
                        ufw += [("sVX", f"p3cl{i-2}"), ("sEV", f"p3z{i-2}")]
                    p.step("se", lambda e, sl=sl, i=i, u_f=u_f: e.activation(
                        u_f, x1t[:, sl, :], AF.Identity,
                        scale=ctx.qs_st[:, i:i + 1], bias=ctx.qb_st[:, i:i + 1]),
                        waits=ufw, inc=[("sEV", 1)], label=f"p3uf{i}", dur=1150)
                    p.step("ve", lambda e, u_f=u_f: e.tensor_scalar(
                        u_f, u_f, 127.0, -128.0, ALU.min, ALU.max),
                        waits=[("sEV", f"p3uf{i}")], inc=[("sVX", 1)],
                        label=f"p3cl{i}", dur=600)
                    zw = [("sVX", f"p3cl{i}")]
                    if i >= 2:
                        zw.append(("sVX", f"p3xq{i-2}"))
                    p.step("se", lambda e, z16=z16, u_f=u_f: e.activation(
                        z16, u_f, AF.Copy, bias=RBIAS),
                        waits=zw, inc=[("sEV", 1)], label=f"p3z{i}", dur=1150)
                    xqw = [("sEV", f"p3z{i}")]
                    if i >= 2:
                        xqw.append(("sMM", f"p3xt{i-2}_{G-1}"))
                    p.step("ve", lambda e, xq=xq, z16=z16: e.tensor_scalar(
                        xq, z16, -RBIAS, ctx.s1_col[:, :], ALU.add, ALU.mult),
                        waits=xqw, inc=[("sVX", 1)], label=f"p3xq{i}", dur=600)
                    if ctx.debug:
                        p.dma("sp", io["xq_dbg"][b, c, :, :], xq, sem="dbg",
                              waits=[("sVX", f"p3xq{i}")], label=f"p3xqd{i}")
                    for g in range(G):
                        bk = g % 4
                        p.step("te", lambda e, g=g, bk=bk, xq=xq: e.transpose(
                            ptpb[bk], xq[:, g * 128:(g + 1) * 128],
                            ctx.ident_bf[:, :]),
                            waits=[("sVX", f"p3xq{i}")] + bw(f"tp{bk}"),
                            inc=[("sMM", 1)], label=f"p3xt{i}_{g}", dur=120)
                        ew = [("sMM", f"p3xt{i}_{g}")]
                        if g < 2 and j >= 2:
                            ew.append(("sMM", f"p3gu{j-2}_{IG-1}"))
                        sm = evac(p, g, xq_dm2s[:, j % 2, g, t * C:(t + 1) * C],
                                  ptpb[bk], waits=ew, label=f"p3xd{n}_{g}")
                        pu[f"tp{bk}"] = (sm, f"p3xd{n}_{g}")
                p.cur_group = ("T", j)
                xdw = [("wgt", f"wuT{G-1}")]
                for t in range(4):
                    xdw += [("sVX", f"p3xd{4*j+t}_{G-1}"),
                            ("sEV", f"p3xd{4*j+t}_{G-2}")]
                for blk in range(IG):
                    bk = blk % 2

                    def gmm(e, blk=blk, bk=bk, j=j):
                        r = None
                        for g in range(G):
                            r = e.matmul(pGa[bk][:, :],
                                         wgT[:, g, blk * 128:(blk + 1) * 128],
                                         xq_dm2s[:, j % 2, g, :],
                                         start=(g == 0), stop=(g == G - 1))
                        for g in range(G):
                            r = e.matmul(pUa[bk][:, :],
                                         wuT[:, g, blk * 128:(blk + 1) * 128],
                                         xq_dm2s[:, j % 2, g, :],
                                         start=(g == 0), stop=(g == G - 1))
                        return r
                    p.step("te", gmm,
                           waits=(xdw if blk == 0 else []) + bw(f"g{bk}")
                           + bw(f"u{bk}"),
                           inc=[("sMM", 1)], label=f"p3gu{j}_{blk}", dur=3400)
                    sw = [("sMM", f"p3gu{j}_{blk}")] + bw(f"sil{bk}")
                    p.step("se", lambda e, bk=bk: e.activation(
                        sil2[:, bk, :], pGa[bk][:, :], AF.Silu),
                        waits=sw, inc=[("sEV", 1)], label=f"p3si{j}_{blk}",
                        dur=550)
                    pu[f"g{bk}"] = ("sEV", f"p3si{j}_{blk}")
                    mw = [("sEV", f"p3si{j}_{blk}")] + bw(f"tsc{bk}")
                    if blk < 2 and j >= 1:
                        mw.append(("sVX", f"p3rx{j-1}_{IG-1}"))
                    p.step("ve", lambda e, bk=bk: e.tensor_tensor(
                        t_scr[:, bk, :], sil2[:, bk, :], pUa[bk][:, :], ALU.mult),
                        waits=mw, inc=[("sVX", 1)], label=f"p3m{j}_{blk}",
                        dur=550)
                    pu[f"u{bk}"] = ("sVX", f"p3m{j}_{blk}")
                    pu[f"sil{bk}"] = ("sVX", f"p3m{j}_{blk}")
                    cw = [("sVX", f"p3m{j}_{blk}")]
                    if blk < 2 and j >= 2:
                        cw.append((f"pts{j % 2}", f"p3ts{j-2}"))
                    p.step("se", lambda e, blk=blk, bk=bk, j=j: e.activation(
                        t_bf2s[:, j % 2, blk, :], t_scr[:, bk, :], AF.Copy),
                        waits=cw, inc=[("sEV", 1)], label=f"p3tc{j}_{blk}",
                        dur=550)
                    pu[f"tsc{bk}"] = ("sEV", f"p3tc{j}_{blk}")
                    rxw = [("sVX", f"p3m{j}_{blk}")]
                    if blk == 0 and j >= 1:
                        rxw.append(("sVX", f"p3rm{j-1}"))
                    p.step("ve", lambda e, blk=blk, bk=bk: e.tensor_reduce(
                        mxc[:, blk:blk + 1], t_scr[:, bk, :], AX.X, ALU.max,
                        apply_absolute_value=True),
                        waits=rxw, inc=[("sVX", 1)], label=f"p3rx{j}_{blk}",
                        dur=300)
                p.step("ve", lambda e, j=j: e.tensor_reduce(
                    ctx.mn_st[:, j:j + 1], mxc[:, :], AX.X, ALU.max),
                    waits=[("sVX", f"p3rx{j}_{IG-1}")], inc=[("sVX", 1)],
                    label=f"p3rm{j}")
                p.dma("se", io["t_s"][j], t_bf2s[:, j % 2, :, :],
                      sem=f"pts{j % 2}",
                      waits=[("sEV", f"p3tc{j}_{IG-1}")], label=f"p3ts{j}",
                      xfer=15000)
            p.cur_group = None
            p.step("ve", lambda e: e.tensor_reduce(
                ctx.s2_col[:, :], ctx.mn_st[:, 0:NCHUNKS // 4], AX.X, ALU.max),
                waits=[("sVX", f"p3rm{j}") for j in range(NCHUNKS // 4)],
                inc=[("sVX", 1)], label="lm2")
            p.step("te", lambda e: e.transpose(ptp[0][0:1, 0:128],
                                               ctx.s2_col[:, :], ctx.ident[:, :]),
                   waits=[("sVX", "lm2")] + bw("tp0"), inc=[("sMM", 1)],
                   label="lmt2")
            p.step("ve", lambda e: e.tensor_reduce(ctx.ccb[0:1, 0:1],
                                                   ptp[0][0:1, 0:128], AX.X,
                                                   ALU.max),
                   waits=[("sMM", "lmt2")], inc=[("sVX", 1)], label="lmr2")
            p.step("ve", lambda e: e.tensor_copy(
                ctx.ccb[0:1, :], ctx.ccb[0:1, 0:1].broadcast_to([1, 128])),
                waits=[("sVX", "lmr2")], inc=[("sVX", 1)], label="ccf2")
            p.dma("sp", io["cc2_in"].ap(), ctx.ccb[0:1, :], sem="cc",
                  waits=[("sVX", "ccf2")], label="cci2")
            p.step("pl", lambda e: e.collective_compute(
                "AllReduce", ALU.max, replica_groups=[list(range(N_CORES))],
                ins=[io["cc2_in"].ap().opt()], outs=[io["cc2_out"].ap().opt()]),
                waits=[("cc", "cci2")], inc=[("sCC", 1)], label="ccr2")
            p.dma("sp", ctx.sig_cell[0:1, 5:6], io["cc2_out"].ap()[0:1, 0:1],
                  sem="cc", waits=[("sCC", "ccr2")], label="ccl2")
            p.chain("ve", [
                lambda e: e.tensor_tensor(ctx.sig_cell[0:1, 6:7],
                                          ctx.sig_cell[0:1, 5:6],
                                          ctx.sig_cell[0:1, 1:2], ALU.mult),
                lambda e: e.tensor_scalar_mul(ctx.sig_cell[0:1, 6:7],
                                              ctx.sig_cell[0:1, 6:7], 1.0 / 127.0),
                lambda e: e.reciprocal(ctx.sig_cell[0:1, 7:8],
                                       ctx.sig_cell[0:1, 6:7])],
                waits=[("cc", "ccl2")], inc=[("sVX", 1)], label="s2c")
            p.step("te", lambda e: (
                e.matmul(pGa[0][:, 0:1], ctx.ones_row[:, :], ctx.sig_cell[0:1, 6:7],
                         start=True, stop=True),
                e.matmul(pGa[1][:, 0:1], ctx.ones_row[:, :], ctx.sig_cell[0:1, 7:8],
                         start=True, stop=True)),
                waits=[("sVX", "s2c")] + bw("g0") + bw("g1"),
                inc=[("sMM", 1)], label="s2b")
            p.step("ve", lambda e: (
                e.tensor_copy(ctx.s2_col[:, :], pGa[0][:, 0:1]),
                e.tensor_copy(ctx.is2_col[:, :], pGa[1][:, 0:1])),
                waits=[("sMM", "s2b")], inc=[("sVX", 1)], label="s2e")
            p.quiesce("p3a")
            p.emit(reorder=True)

        with ExitStack() as st3:
            l3 = lambda n, s, d: st3.enter_context(nc.sbuf_tensor(n, s, d))
            lps3 = lambda n, s: st3.enter_context(nc.psum_tensor(n, s, F32))
            wdT = l3("wdT", [128, IG, D], BF16)
            t_ld = l3("q3t", [128, 4, IG, C], BF16)
            x1t = l3("q3x", [128, 4, D], F32)
            u2_2 = l3("q3u", [128, 2, IG, C], F32)
            z16b2 = l3("q3z", [128, 2, IG, C], F16)
            hq2 = l3("q3h", [128, 2, IG, C], BF16)
            y_f2 = l3("q3y", [128, 2, D], F32)
            pDn = [lps3("q3d0", [128, 512]), lps3("q3d1", [128, 512])]

            p = Prog(ctx)
            p.mark("p3b0")
            for ig in range(IG):
                p.dma("sp", wdT[:, ig, :], io["wdq_s"][:, ig * 128:(ig + 1) * 128],
                      sem="wdt", transpose=True, label=f"wdT{ig}", xfer=6000)
            pu = {}

            def bw(key):
                return [pu[key]] if key in pu else []

            if ctx.debug:
                p.step("ve", lambda e: (
                    e.tensor_copy(y_f2[:, 0, 0:1], ctx.s1_col[:, :]),
                    e.tensor_copy(y_f2[:, 0, 1:2], ctx.is1_col[:, :]),
                    e.tensor_copy(y_f2[:, 0, 2:3], ctx.s2_col[:, :]),
                    e.tensor_copy(y_f2[:, 0, 3:4], ctx.is2_col[:, :])),
                    inc=[("sVX", 1)], label="cdbg")
                p.dma("sp", io["col_dbg"], y_f2[:, 0, 0:4], sem="cdb",
                      waits=[("sVX", "cdbg")], label="cdbgd")

            for i in range(NCHUNKS):
                b, c = divmod(i, NCH)
                j4, t4 = divmod(i, 4)
                n = f"{i}"
                sl = i % 4
                s2 = i % 2
                p.cur_group = ("B", i)
                u2 = u2_2[:, s2, :, :]
                z16b = z16b2[:, s2, :, :]
                hq = hq2[:, s2, :, :]
                y_f = y_f2[:, s2, :]
                lw = [("sEV", f"q3u{i-4}")] if i >= 4 else []
                if i == 0 and ctx.debug:
                    lw = lw + [("cdb", "cdbgd")]
                p.dma("sp", t_ld[:, sl, :, :],
                      io["t_s"][j4, :, :, t4 * C:(t4 + 1) * C], sem=f"qt{i % 4}",
                      waits=lw, label=f"q3tl{n}", xfer=6000)
                xw = [("sVX", f"q3y{i-4}")] if i >= 4 else []
                p.dma("sp", x1t[:, sl, :], io["x1_s"][b, c * C:(c + 1) * C, :],
                      sem=f"qx{i % 4}", waits=xw, label=f"q3xl{n}", xfer=4000)
                uw = [(f"qt{i % 4}", f"q3tl{n}")]
                if i >= 2:
                    uw += [("sVX", f"q3cl{i-2}"), ("sEV", f"q3z{i-2}")]
                p.step("se", lambda e, sl=sl, u2=u2: e.activation(
                    u2, t_ld[:, sl, :, :], AF.Copy, scale=ctx.is2_col[:, :]),
                    waits=uw, inc=[("sEV", 1)], label=f"q3u{i}", dur=2300)
                p.step("ve", lambda e, u2=u2: e.tensor_scalar(
                    u2, u2, 127.0, -128.0, ALU.min, ALU.max),
                    waits=[("sEV", f"q3u{i}")], inc=[("sVX", 1)],
                    label=f"q3cl{i}", dur=1500)
                zw = [("sVX", f"q3cl{i}")]
                if i >= 2:
                    zw.append(("sVX", f"q3hq{i-2}"))
                p.step("se", lambda e, z16b=z16b, u2=u2: e.activation(
                    z16b, u2, AF.Copy, bias=RBIAS),
                    waits=zw, inc=[("sEV", 1)], label=f"q3z{i}", dur=2400)
                hqw = [("sEV", f"q3z{i}")]
                if i >= 2:
                    hqw.append(("sMM", f"q3dn{i-2}_1"))
                p.step("ve", lambda e, hq=hq, z16b=z16b: e.tensor_scalar(
                    hq, z16b, -RBIAS, ctx.s2_col[:, :], ALU.add, ALU.mult),
                    waits=hqw, inc=[("sVX", 1)], label=f"q3hq{i}", dur=1300)
                if ctx.debug:
                    p.dma("sp", io["hq_dbg"][b, c, :, :], hq, sem="dbg",
                          waits=[("sVX", f"q3hq{i}")], label=f"q3hqd{i}")
                for hh in range(2):
                    def dmm(e, hh=hh, hq=hq):
                        r = None
                        for ig in range(IG):
                            r = e.matmul(pDn[hh][:, :], hq[:, ig, :],
                                         wdT[:, ig, hh * 512:(hh + 1) * 512],
                                         start=(ig == 0), stop=(ig == IG - 1))
                        return r
                    p.step("te", dmm,
                           waits=[("sVX", f"q3hq{i}"), ("wdt", f"wdT{IG-1}")]
                           + bw(f"d{hh}"),
                           inc=[("sMM", 1)], label=f"q3dn{n}_{hh}", dur=4700)
                yw = [(f"qx{i % 4}", f"q3xl{n}"), ("sMM", f"q3dn{n}_0"),
                      ("sMM", f"q3dn{n}_1")]
                if i >= 2:
                    yw.append((f"qy{s2}", f"q3ys{i-2}"))
                p.chain("ve", [
                    lambda e, sl=sl, y_f=y_f: e.tensor_tensor(y_f[:, 0:512],
                                                              x1t[:, sl, 0:512],
                                                              pDn[0][:, :],
                                                              ALU.add),
                    lambda e, sl=sl, y_f=y_f: e.tensor_tensor(y_f[:, 512:1024],
                                                              x1t[:, sl, 512:1024],
                                                              pDn[1][:, :],
                                                              ALU.add)],
                    waits=yw, inc=[("sVX", 1)], label=f"q3y{i}")
                pu["d0"] = ("sVX", f"q3y{i}")
                pu["d1"] = ("sVX", f"q3y{i}")
                p.dma("sp", io["y"][b, c * C:(c + 1) * C, :], y_f,
                      sem=f"qy{i % 2}", waits=[("sVX", f"q3y{i}")],
                      label=f"q3ys{i}", xfer=4000)
            p.cur_group = None
            p.quiesce("p3b")
            p.emit(reorder=True)



def emit_debug_dump(ctx):
    io = ctx.io
    p = Prog(ctx)
    p.mark("d0")
    for j, s in enumerate([ctx.r1_st, ctx.b1_st, ctx.beta_st, ctx.betan_st,
                           ctx.r2_st, ctx.mu2_st, ctx.var2_st, ctx.mx_st,
                           ctx.mn_st, ctx.qs_st]):
        p.dma("sp", io["st_dbg"][:, j, :], s[:, :], sem="dbg")
    p.quiesce("dbg")
    p.emit()


def emit_final(ctx):
    pass


_NC_CACHE = {}


def kernel(**inputs):
    if "full" not in _NC_CACHE:
        _NC_CACHE["full"] = build()
    nc = _NC_CACHE["full"]
    x = np.asarray(inputs["L0_attn_x"])
    in_maps = []
    for cid in range(N_CORES):
        m = {k: np.ascontiguousarray(np.asarray(v)) for k, v in inputs.items()
             if k != "L0_attn_x"}
        m["L0_attn_x"] = np.ascontiguousarray(x[cid * NB:(cid + 1) * NB])
        in_maps.append(m)
    res = run_bass_kernel_spmd(nc, in_maps, core_ids=list(range(N_CORES)))
    return np.concatenate([np.asarray(r["y"]) for r in res.results], axis=0)
